# version 02
# baseline (speedup 1.0000x reference)
import contextlib

import numpy as np

import concourse.bass as bass
import concourse.mybir as mybir
from concourse.bass_utils import run_bass_kernel_spmd

F32 = mybir.dt.float32
BF16 = mybir.dt.bfloat16
I32 = mybir.dt.int32
ALU = mybir.AluOpType
ACT = mybir.ActivationFunctionType

DM = 1024
T = 2048
DI = 2048
NST = 16
RANK = 64
NDB = DI // 128
NFT = DM // 128
HID = 4096
NH = 8
HD = 64
S_FULL = 4096
MOE_I = 2816
NEXP = 7
EPS = 1e-5

GROUPS = [[0, 1], [2, 3], [4, 5], [6, 7]]

ENGS = ("sync", "tensor", "vector", "scalar", "gpsimd")
SEM_OF = {"sync": "pQ", "tensor": "pT", "vector": "pV", "scalar": "pS",
          "gpsimd": "pG"}
UNORDERED = {"pG"}


class Prog:

    def __init__(self, nc):
        self.nc = nc
        self.steps = {e: [] for e in ENGS}
        self.val = {}
        self.know = {e: {} for e in ENGS}
        self.tokknow = {}
        self.regions = {}
        self.fence_pending = {e: None for e in ENGS}

    def _dep_tokens(self, reads, writes):
        deps = []
        for k in reads:
            ent = self.regions.get(k)
            if ent and ent[0] is not None:
                deps.append(ent[0])
        for k in writes:
            ent = self.regions.get(k)
            if ent:
                if ent[0] is not None:
                    deps.append(ent[0])
                deps.extend(ent[1])
        return deps

    def _plan(self, eng, fn, reads, writes, semkey, amt, extra_deps=()):
        deps = self._dep_tokens(reads, writes)
        deps.extend(extra_deps)
        if self.fence_pending[eng] is not None:
            deps.extend(self.fence_pending[eng])
            self.fence_pending[eng] = None
        need = {}
        for (sk, v) in deps:
            if sk in UNORDERED:
                v = self.val.get(sk, 0)
            need[sk] = max(need.get(sk, 0), v)
        kn = self.know[eng]
        waits = []
        for sk, v in sorted(need.items(), key=lambda x: str(x[0])):
            if kn.get(sk, 0) < v:
                waits.append((sk, v))
        for (sk, v) in waits:
            kn[sk] = max(kn.get(sk, 0), v)
            tk = self.tokknow.get((sk, v))
            if tk:
                for k2, v2 in tk.items():
                    if kn.get(k2, 0) < v2:
                        kn[k2] = v2
        self.val[semkey] = self.val.get(semkey, 0) + amt
        tok = (semkey, self.val[semkey])
        snap = dict(kn)
        snap[semkey] = tok[1]
        self.tokknow[tok] = snap
        for k in writes:
            self.regions[k] = [tok, []]
        for k in reads:
            ent = self.regions.setdefault(k, [None, []])
            ent[1].append(tok)
        self.steps[eng].append({"fn": fn, "waits": waits,
                                "inc": (semkey, amt)})
        return tok

    def op(self, eng, fn, r=(), w=()):
        return self._plan(eng, fn, r, w, SEM_OF[eng], 1)

    def _chan_deps(self, eng, semkey):
        q = self.val.get(semkey, 0)
        if q > 0 and self.know[eng].get(semkey, 0) < q:
            return [(semkey, q)]
        return []

    NPOOL = 80

    def dma(self, eng, out_ap, in_ap, ch="misc", r=(), w=(), **kw):
        def fn(e, out_ap=out_ap, in_ap=in_ap, kw=kw):
            return e.dma_start(out=out_ap, in_=in_ap, **kw)
        if not hasattr(self, "_chpool"):
            self._chpool = {}
        slot = self._chpool.setdefault(ch, len(self._chpool) % self.NPOOL)
        semkey = ("D", slot)
        extra = self._chan_deps(eng, semkey)
        return self._plan(eng, fn, r, w, semkey, 16, extra_deps=extra)

    def coll(self, kind, op, ins, outs, r=(), w=()):
        def fn(e, kind=kind, op=op, ins=ins, outs=outs):
            return e.collective_compute(kind, op, replica_groups=GROUPS,
                                        ins=ins, outs=outs)
        extra = self._chan_deps("gpsimd", "cc")
        return self._plan("gpsimd", fn, r, w, "cc", 1, extra_deps=extra)

    def fence(self):
        snap = [(sk, v) for sk, v in self.val.items()]
        for e in ENGS:
            prior = self.fence_pending[e]
            self.fence_pending[e] = (prior or []) + snap
        self.regions = {}

    def emit(self):
        nc = self.nc
        semkeys = sorted(set(list(self.val.keys())
                             + [sk for e in ENGS for st in self.steps[e]
                                for (sk, _v) in st["waits"]]),
                         key=str)
        with contextlib.ExitStack() as es:
            sems = {sk: es.enter_context(
                nc.semaphore(str(sk).replace(" ", "")[:62]))
                for sk in semkeys}
            with nc.Block() as block:
                for ename in ENGS:
                    steps = self.steps[ename]
                    if not steps:
                        continue

                    def body(eng, steps=steps, sems=sems):
                        for st in steps:
                            for (sk, v) in st["waits"]:
                                eng.wait_ge(sems[sk], v)
                            inst = st["fn"](eng)
                            sk, amt = st["inc"]
                            inst.then_inc(sems[sk], amt)

                    getattr(block, ename)(body)


class Arena:
    def __init__(self, nc, name, part_bytes):
        self.buf = nc.alloc_sbuf_tensor(name, [128, part_bytes // 4], F32)
        self.part_bytes = part_bytes
        self.off = 0
        self.marks = []
        self.peak = 0

    def alloc(self, cols, dtype, parts=128):
        esz = mybir.dt.size(dtype)
        self.off = (self.off + 7) // 8 * 8
        byte_cols = cols * esz
        assert self.off + byte_cols <= self.part_bytes, (
            f"arena overflow: {self.off} + {byte_cols} > {self.part_bytes}")
        f32_off = self.off // 4
        if dtype == F32:
            ap = self.buf[0:parts, f32_off:f32_off + cols]
        else:
            n_f32 = (byte_cols + 3) // 4
            ap = self.buf[0:parts, f32_off:f32_off + n_f32].bitcast(dtype)
            ap = ap[:, 0:cols]
        self.off += byte_cols
        self.peak = max(self.peak, self.off)
        return ap

    def mark(self):
        self.marks.append(self.off)

    def release(self):
        self.off = self.marks.pop()


class PsumPool:
    def __init__(self, nc):
        self.banks = [nc.alloc_psum_tensor(f"pb{i}", [128, 512], F32)
                      for i in range(8)]
        self.i = 0
        self.held = set()

    def get(self):
        for _ in range(8):
            self.i = (self.i + 1) % 8
            if self.i not in self.held:
                return self.i, self.banks[self.i], f"ps{self.i}"
        raise RuntimeError("no free psum banks")

    def hold(self, idx):
        self.held.add(idx)

    def free(self, idx):
        self.held.discard(idx)


class WStream:

    def __init__(self, p, ar, name, aps, depth=6, cast=True,
                 cast_eng="scalar", dtype=F32, buf_from=None, src_keys=None,
                 engs=("sync",)):
        self.p = p
        self.name = name
        self.aps = aps
        self.engs = engs
        self.src_keys = src_keys or [[] for _ in aps]
        self.cast = cast and dtype == F32
        self.cast_eng = cast_eng
        if buf_from is not None:
            assert buf_from.name == name
            self.depth = buf_from.depth
            self.src = buf_from.src
            self.b16 = buf_from.b16
        else:
            self.depth = depth
            maxcols = max(a.shape[-1] for a in aps)
            self.src = [ar.alloc(maxcols, dtype) for _ in range(depth)]
            self.b16 = ([ar.alloc(maxcols, BF16) for _ in range(depth)]
                        if self.cast else None)
        self.next = 0

    def _issue(self, i):
        ap = self.aps[i]
        s = i % self.depth
        pp, cc = ap.shape[0], ap.shape[-1]
        self.p.dma(self.engs[s % len(self.engs)],
                   self.src[s][0:pp, 0:cc], ap,
                   ch=("ws", self.name, s),
                   r=[(self.name, "f", s)] + list(self.src_keys[i]),
                   w=[(self.name, "f", s)])
        if self.cast:
            dst = self.b16[s][0:pp, 0:cc]
            src = self.src[s][0:pp, 0:cc]
            if self.cast_eng == "scalar":
                fn = lambda e, a=dst, b=src: e.copy(out=a, in_=b)
            else:
                fn = lambda e, a=dst, b=src: e.tensor_copy(out=a, in_=b)
            self.p.op(self.cast_eng, fn,
                      r=[(self.name, "f", s), (self.name, "b", s)],
                      w=[(self.name, "b", s)])

    def prefetch(self, upto):
        while self.next <= min(upto, len(self.aps) - 1):
            self._issue(self.next)
            self.next += 1

    def get(self, i):
        self.prefetch(i + self.depth - 1)
        assert i > self.next - self.depth - 1, (
            f"WStream {self.name}: get({i}) after slot clobbered "
            f"(next={self.next}, depth={self.depth})")
        s = i % self.depth
        ap = self.aps[i]
        pp, cc = ap.shape[0], ap.shape[-1]
        if self.cast:
            return self.b16[s][0:pp, 0:cc], (self.name, "b", s)
        return self.src[s][0:pp, 0:cc], (self.name, "f", s)


class MK:
    def __init__(self, nc, dump=()):
        self.nc = nc
        self.p = Prog(nc)
        self.ar = Arena(nc, "A", 212736)
        self.pp = PsumPool(nc)
        self.dump = set(dump)
        self.pc_items = []
        self.pc_i = 0
        self.inp = {}
        self.intern = {}
        self.out = {}

    def ein(self, name, shape, dtype=F32):
        t = self.nc.dram_tensor(name, list(shape), dtype, kind="ExternalInput")
        self.inp[name] = t
        return t

    def eout(self, name, shape, dtype=F32):
        t = self.nc.dram_tensor(name, list(shape), dtype,
                                kind="ExternalOutput")
        self.out[name] = t
        return t

    def idram(self, name, shape, dtype=F32):
        t = self.nc.dram_tensor(name, list(shape), dtype)
        self.intern[name] = t
        return t

    def get_stream(self, name, aps, **kw):
        if not hasattr(self, "_wscache"):
            self._wscache = {}
        prev = self._wscache.get(name)
        ws = WStream(self.p, self.ar, name, aps, buf_from=prev, **kw)
        self._wscache[name] = ws
        return ws

    def drop_streams(self, *names):
        for n in names:
            self._wscache.pop(n, None)

    def precast_setup(self):
        ar = self.ar
        self.pc_src = [ar.alloc(128, F32) for _ in range(2)]
        self.pc_dst = [ar.alloc(128, BF16) for _ in range(2)]
        self.pc_items = []
        self.pc_i = 0

    def precast(self, nm, src_ap, R, C):
        dst = self.idram(nm + "_bf", [R, C], BF16)
        for r in range(R // 128):
            for ci, c0 in enumerate(range(0, C, 128)):
                cw = min(128, C - c0)
                self.pc_items.append(
                    (src_ap[r * 128:(r + 1) * 128, c0:c0 + cw],
                     dst.ap()[r * 128:(r + 1) * 128, c0:c0 + cw],
                     (nm, "pcw", r, ci), cw))
        return dst

    def pc_pump(self, n):
        p = self.p
        for _ in range(n):
            if self.pc_i >= len(self.pc_items):
                return
            i = self.pc_i
            self.pc_i += 1
            sap, dap, key, cw = self.pc_items[i]
            s = i % 2
            p.dma("gpsimd", self.pc_src[s][:, 0:cw], sap, ch=("pci", s),
                  r=[("pcs", s)], w=[("pcs", s)])
            p.op("gpsimd", lambda e, a=self.pc_dst[s][:, 0:cw],
                 b=self.pc_src[s][:, 0:cw]: e.tensor_copy(out=a, in_=b),
                 r=[("pcs", s), ("pcb", s)], w=[("pcb", s)])
            p.dma("gpsimd", dap, self.pc_dst[s][:, 0:cw], ch=("pco", s),
                  r=[("pcb", s)], w=[("pcb", s), key])

    def maybe_dump(self, name, tiles, keys=None, cols=T):
        if name not in self.dump:
            return
        nt = len(tiles)
        d = self.eout("dbg_" + name, [nt * 128, cols], F32)
        cw = min(cols, 512)
        tmp = self.ar.alloc(cw, F32)
        for i in range(nt):
            ap = tiles[i]
            rk = list(keys[i]) if keys else []
            if ap.dtype != F32:
                for c0 in range(0, cols, cw):
                    self.p.op("vector",
                              lambda e, a=tmp, b=ap[:, c0:c0 + cw]:
                              e.tensor_copy(out=a, in_=b),
                              r=rk + [("dmpt", name)], w=[("dmpt", name)])
                    self.p.dma("gpsimd",
                               d.ap()[i * 128:(i + 1) * 128, c0:c0 + cw],
                               tmp, ch="dbg1", r=[("dmpt", name)],
                               w=[("dmpt", name)])
            else:
                self.p.dma("gpsimd", d.ap()[i * 128:(i + 1) * 128, 0:cols],
                           ap, ch="dbg1", r=rk + ["dbg1pace"],
                           w=["dbg1pace"])

    def dump_dram(self, name, dram, rows, cols, dtype=F32):
        if name not in self.dump:
            return
        self.p.fence()
        d = self.eout("dbg_" + name, [rows, cols], F32)
        self.ar.mark()
        stage = self.ar.alloc(cols, dtype)
        st2 = self.ar.alloc(cols, F32)
        for i in range(rows // 128):
            self.p.dma("gpsimd", stage,
                       dram.ap()[i * 128:(i + 1) * 128, 0:cols],
                       ch="dbg0", r=[("dds", name)], w=[("dds", name)])
            self.p.op("vector", lambda e, a=st2, b=stage:
                      e.tensor_copy(out=a, in_=b),
                      r=[("dds", name), ("dd2", name)], w=[("dd2", name)])
            self.p.dma("gpsimd", d.ap()[i * 128:(i + 1) * 128, :], st2,
                       ch="dbg1", r=[("dd2", name)], w=[("dd2", name)])
        self.ar.release()
        self.p.fence()

    def setup_consts(self):
        p, ar = self.p, self.ar
        self.ones_f32 = ar.alloc(128, F32)
        p.op("vector", lambda e, a=self.ones_f32: e.memset(a, 1.0), w=["ones"])
        self.c_eps = ar.alloc(1, F32)
        p.op("vector", lambda e, a=self.c_eps: e.memset(a, EPS), w=["c_eps"])
        self.c32 = ar.alloc(1, F32)
        p.op("vector", lambda e, a=self.c32: e.memset(a, 1.0 / 32.0),
             w=["c32"])
        idf = self.nc.inline_tensor(np.eye(128, dtype=np.float32), "id128")
        self.id_f32 = ar.alloc(128, F32)
        self.id_bf16 = ar.alloc(128, BF16)
        p.dma("sync", self.id_f32, idf.ap(), ch="c0", w=["idf32"])
        p.op("scalar", lambda e, a=self.id_bf16, b=self.id_f32:
             e.copy(out=a, in_=b), r=["idf32"], w=["id"])

    def selB(self, n):
        return self.sel[0:96, n * 128:(n + 1) * 128]

    def selC(self, n):
        return self.sel[0:96, (16 + n) * 128:(17 + n) * 128]

    def loadp(self, name, cols, rows=128):
        p, ar = self.p, self.ar
        t = self.ein(name, [rows, cols], F32)
        sb = ar.alloc(cols, F32, parts=rows)
        p.dma("sync", sb, t.ap(), ch="c1", r=["c1pace"], w=[("prm", name), "c1pace"])
        self.prm[name] = sb
        return sb

    def layernorm(self, src_getter, dst_writer):
        p, ar, pp = self.p, self.ar, self.pp
        x2 = ar.alloc(512, F32)
        msb = [ar.alloc(512, F32) for _ in range(4)]
        vsb = [ar.alloc(512, F32) for _ in range(4)]
        tmp = ar.alloc(512, F32)
        banks = []
        for c in range(8):
            bi, bank, bkey = pp.get()
            pp.hold(bi)
            banks.append((bi, bank, bkey))
        for f in range(NFT):
            xap, xkey = src_getter(f)
            for c in range(4):
                p.op("scalar", lambda e, o=x2,
                     i=xap[:, c * 512:(c + 1) * 512]:
                     e.square(out=o, in_=i), r=[xkey, "ln_x2"], w=["ln_x2"])
                _, bank, bkey = banks[c]
                p.op("tensor", lambda e, o=bank[:, :], w=self.ones_f32,
                     rr=xap[:, c * 512:(c + 1) * 512], s=(f == 0),
                     t=(f == NFT - 1): e.matmul(o, w, rr, start=s, stop=t),
                     r=["ones", xkey] + ([bkey] if f else []), w=[bkey])
                _, bank2, bkey2 = banks[4 + c]
                p.op("tensor", lambda e, o=bank2[:, :], w=self.ones_f32,
                     rr=x2, s=(f == 0),
                     t=(f == NFT - 1): e.matmul(o, w, rr, start=s, stop=t),
                     r=["ones", "ln_x2"] + ([bkey2] if f else []), w=[bkey2])
        for c in range(4):
            _, bankA, bkeyA = banks[c]
            _, bankB, bkeyB = banks[4 + c]
            p.op("scalar", lambda e, o=msb[c], i=bankA: e.mul(
                out=o, in_=i[:, :], mul=1.0 / DM), r=[bkeyA], w=[("ln_m", c)])
            p.op("scalar", lambda e, o=vsb[c], i=bankB: e.mul(
                out=o, in_=i[:, :], mul=1.0 / DM), r=[bkeyB], w=[("ln_v", c)])
            p.op("scalar", lambda e, o=tmp, i=msb[c]: e.square(out=o, in_=i),
                 r=[("ln_m", c), "ln_tmp"], w=["ln_tmp"])
            p.op("vector", lambda e, o=vsb[c], a=vsb[c], b=tmp:
                 e.tensor_sub(o, a, b), r=[("ln_v", c), "ln_tmp"],
                 w=[("ln_v", c)])
            p.op("scalar", lambda e, o=vsb[c], i=vsb[c]: e.activation(
                out=o, in_=i, func=ACT.Sqrt, bias=self.c_eps),
                r=[("ln_v", c), "c_eps"], w=[("ln_v", c)])
            p.op("vector", lambda e, o=vsb[c], i=vsb[c]:
                 e.reciprocal(out=o, in_=i), r=[("ln_v", c)],
                 w=[("ln_v", c)])
        for c in range(8):
            pp.free(banks[c][0])
        nrm = [ar.alloc(512, F32) for _ in range(2)]
        for f in range(NFT):
            xap, xkey = src_getter(f)
            for c in range(4):
                nb = nrm[c % 2]
                nk = ("ln_nrm", c % 2)
                p.op("vector", lambda e, o=nb,
                     a=xap[:, c * 512:(c + 1) * 512], b=msb[c]:
                     e.tensor_sub(o, a, b), r=[xkey, ("ln_m", c), nk], w=[nk])
                p.op("vector", lambda e, o=nb, a=nb, b=vsb[c]:
                     e.tensor_mul(o, a, b), r=[nk, ("ln_v", c)], w=[nk])
                dst_writer(f, c, nb, nk)

    def load_params_l0(self):
        self.prm = {}
        p, ar = self.p, self.ar
        selc = np.zeros((96, 32 * 128), dtype=np.float32)
        for n in range(NST):
            selc[64 + n, n * 128:(n + 1) * 128] = 1.0
            selc[80 + n, (16 + n) * 128:(17 + n) * 128] = 1.0
        selt = self.nc.inline_tensor(selc, "selconst")
        self.sel = ar.alloc(32 * 128, BF16, parts=96)
        selstage = ar.alloc(512, F32, parts=96)
        for j in range(8):
            p.dma("sync", selstage,
                  selt.ap()[:, j * 512:(j + 1) * 512],
                  ch="c0", r=["selst"], w=["selst"])
            p.op("scalar", lambda e, a=self.sel[0:96, j * 512:(j + 1) * 512],
                 b=selstage: e.copy(out=a, in_=b),
                 r=["selst", "sel"], w=["sel"])
        for d in ("s", "r"):
            self.loadp("convw_" + d, 64)
            self.loadp("convb_" + d, 16)
            self.loadp("dtb_" + d, 16)
            self.loadp("Dsk_" + d, 16)
            al = self.loadp("Alog_" + d, 256)
            A = ar.alloc(256, F32)
            p.op("scalar", lambda e, a=A, b=al: e.activation(
                out=a, in_=b, func=ACT.Exp), r=[("prm", "Alog_" + d)],
                w=[("prmA0", d)])
            p.op("vector", lambda e, a=A: e.tensor_scalar_mul(
                out=a, in0=a, scalar1=-1.0), r=[("prmA0", d)],
                w=[("prm", "A_" + d)])
            self.prm["A_" + d] = A
        self.loadp("ln_g", 8)
        self.loadp("ln_b", 8)
        self.loadp("projb", 8)
        self.loadp("nbr0", 1)
        self.loadp("nbr1", 1)
        self.dbl = {"s": ar.alloc(T, BF16), "r": ar.alloc(T, BF16)}
        self.xcn = ar.alloc(32, BF16)
        self.stcol = [ar.alloc(NST, F32) for _ in range(NDB)]

    def l0_ln_xn(self):
        p, ar = self.p, self.ar
        self.xT = self.ein("xT", [DM, T])
        self.xcol_in = self.idram("xcol_in", [DM, 3], BF16)
        self.xcol_out = self.idram("xcol_out", [2, DM, 3], BF16)
        xsb = [ar.alloc(T, F32) for _ in range(NFT)]
        for f in range(NFT):
            p.dma("gpsimd", xsb[f], self.xT.ap()[f * 128:(f + 1) * 128, :],
                  ch=("x", f % 4), w=[("x", f)])

        def dst(f, c, nrm, key):
            g = self.prm["ln_g"][:, f:f + 1]
            b = self.prm["ln_b"][:, f:f + 1]
            p.op("scalar", lambda e, o=self.xn[f][:, c * 512:(c + 1) * 512],
                 i=nrm, g=g, b=b: e.activation(out=o, in_=i, func=ACT.Identity,
                                               scale=g, bias=b),
                 r=[key, ("prm", "ln_g"), ("prm", "ln_b"), ("xn", f, c)],
                 w=[("xn", f, c)])

        self.layernorm(lambda f: (xsb[f], ("x", f)), dst)
        for f in range(NFT):
            p.dma("gpsimd", self.xcol_in.ap()[f * 128:(f + 1) * 128, 0:3],
                  self.xn[f][:, T - 3:T], ch=("xci", f % 2),
                  r=[("xn", f, 3)], w=[("xci", f)])
        p.coll("AllGather", ALU.bypass, [self.xcol_in.ap().opt()],
               [self.xcol_out.ap().opt()],
               r=[("xci", f) for f in range(NFT)], w=["xco"])
        xc0 = ar.alloc(32, BF16)
        xc1 = ar.alloc(32, BF16)
        p.op("vector", lambda e, a=xc0: e.memset(a, 0.0), w=["xc0"])
        p.op("vector", lambda e, a=xc1: e.memset(a, 0.0), w=["xc1"])
        for f in range(NFT):
            p.dma("gpsimd", xc0[:, f * 4:f * 4 + 3],
                  self.xcol_out.ap()[0, f * 128:(f + 1) * 128, 0:3],
                  ch="xc0", r=["xco", "xc0"], w=["xc0"])
            p.dma("gpsimd", xc1[:, f * 4:f * 4 + 3],
                  self.xcol_out.ap()[1, f * 128:(f + 1) * 128, 0:3],
                  ch="xc1", r=["xco", "xc1"], w=["xc1"])
        p.op("vector", lambda e, o=self.xcn, a=xc0,
             s=self.prm["nbr0"][:, 0:1]:
             e.tensor_scalar_mul(out=o, in0=a, scalar1=s),
             r=["xc0", ("prm", "nbr0"), "xcn"], w=["xcn"])
        p.op("vector", lambda e, o=self.xcn, a=xc1,
             s=self.prm["nbr1"][:, 0:1], b=self.xcn:
             e.scalar_tensor_tensor(out=o, in0=a, scalar=s, in1=b,
                                    op0=ALU.mult, op1=ALU.add),
             r=["xc1", ("prm", "nbr1"), "xcn"], w=["xcn"])
        self.maybe_dump(
            "xn", self.xn,
            keys=[[("xn", f, c) for c in range(4)] for f in range(NFT)])

    def l0_xz_conv(self, d):
        p, ar, pp = self.p, self.ar, self.pp
        winT = self.ein("winT_" + d, [DM, 2 * DI])
        u_sp = self.idram("u_sp_" + d, [DI, T], BF16)
        z_sp = self.idram("z_sp_" + d, [DI, T], BF16)
        u_sb = self.u_sb
        zstage = self.zstage
        order = [(m, k) for m in range(32) for k in range(8)]
        aps = [winT.ap()[k * 128:(k + 1) * 128, m * 128:(m + 1) * 128]
               for (m, k) in order]
        ws = self.get_stream("win", aps, depth=10)

        def rhs(k, c):
            if d == "s":
                return (self.xn[k][:, c * 512:(c + 1) * 512], ("xn", k, c))
            lo = T - (c + 1) * 512
            return (self.xn[k][:, lo:lo + 512][:, ::-1], ("xn", k, 3 - c))

        for m in range(32):
            nchunk = 5 if (m < NDB and d == "r") else 4
            banks = []
            for c in range(nchunk):
                bi, bank, bkey = pp.get()
                pp.hold(bi)
                banks.append((bi, bank, bkey))
            for k in range(8):
                wap, wkey = ws.get(m * 8 + k)
                for c in range(nchunk):
                    _, bank, bkey = banks[c]
                    cols = 3 if c == 4 else 512
                    if c == 4:
                        rap, rkey = self.xcn[:, k * 4:k * 4 + 3], "xcn"
                    else:
                        rap, rkey = rhs(k, c)
                    p.op("tensor", lambda e, o=bank[:, 0:cols], w=wap, rr=rap,
                         s=(k == 0), t=(k == 7):
                         e.matmul(o, w, rr, start=s, stop=t),
                         r=[wkey, rkey] + ([bkey] if k else []), w=[bkey])
            for c in range(nchunk):
                bi, bank, bkey = banks[c]
                cols = 3 if c == 4 else 512
                if m < NDB:
                    if c == 4:
                        dst = u_sb[m][:, 0:3]
                        wk = ("u", m, 4)
                    else:
                        dst = u_sb[m][:, 3 + c * 512:3 + (c + 1) * 512]
                        wk = ("u", m, c)
                    p.op("scalar", lambda e, o=dst, i=bank[:, 0:cols]:
                         e.copy(out=o, in_=i), r=[bkey, wk], w=[wk])
                else:
                    zs = zstage[c % 2]
                    p.op("scalar", lambda e, o=zs, i=bank[:, 0:512]:
                         e.copy(out=o, in_=i), r=[bkey, ("zst", c % 2)],
                         w=[("zst", c % 2)])
                    p.dma("gpsimd",
                          z_sp.ap()[(m - NDB) * 128:(m - NDB + 1) * 128,
                                    c * 512:(c + 1) * 512],
                          zs, ch=("zsp", c % 2), r=[("zst", c % 2)],
                          w=[("zst", c % 2), ("zspill", d, m - NDB, c)])
                pp.free(bi)
            if m < NDB and d == "s":
                p.op("vector", lambda e, a=u_sb[m][:, 0:3]: e.memset(a, 0.0),
                     r=[("u", m, 4)], w=[("u", m, 4)])
        acc = self.convacc
        cw = self.prm["convw_" + d]
        cb = self.prm["convb_" + d]
        for m in range(NDB):
            a = acc[m % 2]
            ukeys = [("u", m, i) for i in range(5)]
            p.op("vector", lambda e, o=a, i=u_sb[m][:, 0:T],
                 s=cw[:, 4 * m:4 * m + 1]:
                 e.tensor_scalar_mul(out=o, in0=i, scalar1=s),
                 r=ukeys + [("prm", "convw_" + d), ("cacc", m % 2)],
                 w=[("cacc", m % 2)])
            for t in range(1, 4):
                p.op("vector", lambda e, o=a, i=u_sb[m][:, t:t + T],
                     s=cw[:, 4 * m + t:4 * m + t + 1], b=a:
                     e.scalar_tensor_tensor(out=o, in0=i, scalar=s, in1=b,
                                            op0=ALU.mult, op1=ALU.add),
                     r=ukeys + [("prm", "convw_" + d), ("cacc", m % 2)],
                     w=[("cacc", m % 2)])
            p.op("scalar", lambda e, o=u_sb[m][:, 3:3 + T], i=a,
                 b=cb[:, m:m + 1]: e.activation(out=o, in_=i, func=ACT.Silu,
                                                bias=b),
                 r=[("cacc", m % 2), ("prm", "convb_" + d)] + ukeys,
                 w=ukeys + [("usil", m)])
            p.dma("gpsimd", u_sp.ap()[m * 128:(m + 1) * 128, :],
                  u_sb[m][:, 3:3 + T], ch=("usp", m % 2),
                  r=[("usil", m)] + ukeys, w=[("uspill", d, m)])

    def alloc_xproj_bufs(self):
        ar = self.ar
        self.xpb = dict(
            xp_stage=ar.alloc(96, F32),
            xp=[ar.alloc(96, BF16) for _ in range(NDB)],
            dtw_stage=ar.alloc(DI, F32, parts=64),
            dtw=ar.alloc(DI, BF16, parts=64),
            pre=ar.alloc(T, F32),
            t1=ar.alloc(T, F32),
            t2=ar.alloc(T, F32),
            dstg=[ar.alloc(T, BF16) for _ in range(2)],
        )

    def l0_xproj_dt(self, d):
        p, ar, pp = self.p, self.ar, self.pp
        xprojT = self.ein("xprojT_" + d, [DI, 96])
        dtwT = self.ein("dtwT_" + d, [RANK, DI])
        d_sp = self.idram("d_sp_" + d, [DI, T], BF16)
        u_sp = self.intern["u_sp_" + d]
        xb = self.xpb
        xp = xb["xp"]
        for k in range(NDB):
            stage = xb["xp_stage"]
            p.dma("sync", stage, xprojT.ap()[k * 128:(k + 1) * 128, :],
                  ch="c1", r=["xps"], w=["xps"])
            p.op("scalar", lambda e, a=xp[k], b=stage: e.copy(out=a, in_=b),
                 r=["xps", ("xp", k)], w=[("xp", k)])
        dtw_stage = xb["dtw_stage"]
        dtw = xb["dtw"]
        p.dma("sync", dtw_stage[0:RANK, :], dtwT.ap(), ch="c1", r=["dtws"], w=["dtws"])
        p.op("scalar", lambda e, a=dtw[0:RANK, :], b=dtw_stage[0:RANK, :]:
             e.copy(out=a, in_=b), r=["dtws", "dtw"], w=["dtw"])
        dbl = self.dbl[d]
        uaps = [u_sp.ap()[k * 128:(k + 1) * 128, c * 512:(c + 1) * 512]
                for c in range(4) for k in range(NDB)]
        ukeys = [[("uspill", d, k)] for c in range(4) for k in range(NDB)]
        us = self.get_stream("udb", uaps, depth=6, dtype=BF16, src_keys=ukeys)
        for c in range(4):
            bi, bank, bkey = pp.get()
            pp.hold(bi)
            for k in range(NDB):
                uap, ukey = us.get(c * NDB + k)
                p.op("tensor", lambda e, o=bank[0:96, :], w=xp[k], rr=uap,
                     s=(k == 0), t=(k == NDB - 1):
                     e.matmul(o, w, rr, start=s, stop=t),
                     r=[("xp", k), ukey, ("uspill", d, k)]
                     + ([bkey] if k else []), w=[bkey])
            p.op("scalar", lambda e, o=dbl[0:96, c * 512:(c + 1) * 512],
                 i=bank[0:96, :]: e.copy(out=o, in_=i),
                 r=[bkey, ("dbl", d, c)], w=[("dbl", d, c)])
            pp.free(bi)
        pre = xb["pre"]
        t1 = xb["t1"]
        t2 = xb["t2"]
        dstg = xb["dstg"]
        dtb = self.prm["dtb_" + d]
        for m in range(NDB):
            for c in range(4):
                bi, bank, bkey = pp.get()
                pp.hold(bi)
                p.op("tensor", lambda e, o=bank[:, :],
                     w=dtw[0:RANK, m * 128:(m + 1) * 128],
                     rr=dbl[0:RANK, c * 512:(c + 1) * 512]:
                     e.matmul(o, w, rr, start=True, stop=True),
                     r=[("dtw", d), ("dbl", d, c)], w=[bkey])
                p.op("scalar", lambda e, o=pre[:, c * 512:(c + 1) * 512],
                     i=bank[:, :], b=dtb[:, m:m + 1]:
                     e.activation(out=o, in_=i, func=ACT.Identity, bias=b),
                     r=[bkey, ("prm", "dtb_" + d), "dpre"], w=["dpre"])
                pp.free(bi)
            p.op("scalar", lambda e, o=t1, i=pre: e.activation(
                out=o, in_=i, func=ACT.Abs), r=["dpre", "dt1"], w=["dt1"])
            p.op("scalar", lambda e, o=t1, i=t1: e.activation(
                out=o, in_=i, func=ACT.Exp, scale=-1.0), r=["dt1"], w=["dt1"])
            p.op("scalar", lambda e, o=t1, i=t1: e.activation(
                out=o, in_=i, func=ACT.Ln, bias=1.0), r=["dt1"], w=["dt1"])
            p.op("scalar", lambda e, o=t2, i=pre: e.activation(
                out=o, in_=i, func=ACT.Relu), r=["dpre", "dt2"], w=["dt2"])
            ds_ = dstg[m % 2]
            p.op("vector", lambda e, o=ds_, a=t1, b=t2: e.tensor_add(o, a, b),
                 r=["dt1", "dt2", ("dstg", m % 2)], w=[("dstg", m % 2)])
            p.dma("gpsimd", d_sp.ap()[m * 128:(m + 1) * 128, :], ds_,
                  ch=("dsp", m % 2), r=[("dstg", m % 2)],
                  w=[("dstg", m % 2), ("dspill", d, m)])

    def alloc_scan_bufs(self):
        ar = self.ar
        self.scb = dict(
            Brep=[ar.alloc(T, BF16) for _ in range(NST)],
            Crep=[ar.alloc(T, BF16) for _ in range(NST)],
            dec=ar.alloc(T, F32),
            bb=ar.alloc(T, BF16),
            hh=ar.alloc(T, BF16),
            gg=ar.alloc(T, BF16),
            du=ar.alloc(T, BF16),
            zbuf=ar.alloc(512, BF16),
            zsil=ar.alloc(512, BF16),
            ytmp=ar.alloc(512, BF16),
            yst=ar.alloc(512, BF16),
            sin=[ar.alloc(NST, F32) for _ in range(NDB)],
            s0=ar.alloc(NST, F32),
            s1=ar.alloc(NST, F32),
            ost=ar.alloc(512, BF16),
        )

    def l0_scan(self, d):
        p, ar, pp = self.p, self.ar, self.pp
        u_sp = self.intern["u_sp_" + d]
        z_sp = self.intern["z_sp_" + d]
        d_sp = self.intern["d_sp_" + d]
        y_sp = self.idram("y_sp_" + d, [DI, T], BF16)
        dbl = self.dbl[d]
        A = self.prm["A_" + d]
        Dsk = self.prm["Dsk_" + d]
        sc = self.scb
        Brep, Crep = sc["Brep"], sc["Crep"]
        for n in range(NST):
            for (sel, rep, nm) in ((self.selB(n), Brep[n], "B"),
                                   (self.selC(n), Crep[n], "C")):
                for c in range(4):
                    bi, bank, bkey = pp.get()
                    pp.hold(bi)
                    p.op("tensor", lambda e, o=bank[:, :], w=sel,
                         rr=dbl[0:96, c * 512:(c + 1) * 512]:
                         e.matmul(o, w, rr, start=True, stop=True),
                         r=["sel", ("dbl", d, c)], w=[bkey])
                    eng = "scalar" if (n % 2 == 0) else "vector"
                    dst = rep[:, c * 512:(c + 1) * 512]
                    fn = ((lambda e, o=dst, i=bank: e.copy(out=o, in_=i[:, :]))
                          if eng == "scalar" else
                          (lambda e, o=dst, i=bank:
                           e.tensor_copy(out=o, in_=i[:, :])))
                    p.op(eng, fn, r=[bkey, (nm + "rep", n, c)],
                         w=[(nm + "rep", n, c)])
                    pp.free(bi)
        init = None
        if d == "r":
            init = sc["sin"]
            s0, s1 = sc["s0"], sc["s1"]
            for db in range(NDB):
                st = init[db]
                p.dma("gpsimd", s0,
                      self.st_out.ap()[0, db * 128:(db + 1) * 128, :],
                      ch="st0", r=["sto", "sin0"], w=["sin0"])
                p.dma("gpsimd", s1,
                      self.st_out.ap()[1, db * 128:(db + 1) * 128, :],
                      ch="st1", r=["sto", "sin1"], w=["sin1"])
                p.op("vector", lambda e, o=st, a=s0,
                     s=self.prm["nbr0"][:, 0:1]: e.tensor_scalar_mul(
                         out=o, in0=a, scalar1=s),
                     r=["sin0", ("prm", "nbr0"), ("sin", db)],
                     w=[("sin", db)])
                p.op("vector", lambda e, o=st, a=s1,
                     s=self.prm["nbr1"][:, 0:1], b=st:
                     e.scalar_tensor_tensor(out=o, in0=a, scalar=s, in1=b,
                                            op0=ALU.mult, op1=ALU.add),
                     r=["sin1", ("prm", "nbr1"), ("sin", db)],
                     w=[("sin", db)])
        else:
            self.st_in = self.idram("st_in", [DI, NST])
            self.st_out = self.idram("st_out", [2, DI, NST])
        uaps = [u_sp.ap()[db * 128:(db + 1) * 128, :] for db in range(NDB)]
        daps = [d_sp.ap()[db * 128:(db + 1) * 128, :] for db in range(NDB)]
        us = self.get_stream("usc", uaps, depth=2, dtype=BF16,
                             src_keys=[[("uspill", d, db)] for db in range(NDB)])
        ds = self.get_stream("dsc", daps, depth=2, dtype=BF16,
                             src_keys=[[("dspill", d, db)] for db in range(NDB)])
        dec, bb_, hh, gg, du = (sc["dec"], sc["bb"], sc["hh"], sc["gg"],
                                sc["du"])
        for db in range(NDB):
            self.pc_pump(60)
            uap, ukey = us.get(db)
            dap, dkey = ds.get(db)
            p.op("vector", lambda e, o=du, a=dap, b=uap: e.tensor_mul(o, a, b),
                 r=[ukey, dkey, ("uspill", d, db), ("dspill", d, db), "du"],
                 w=["du"])
            ybanks = []
            for c in range(4):
                bi, bank, bkey = pp.get()
                pp.hold(bi)
                ybanks.append((bi, bank, bkey))
            for n in range(NST):
                p.op("scalar", lambda e, o=dec, i=dap,
                     s=A[:, db * 16 + n:db * 16 + n + 1]:
                     e.activation(out=o, in_=i, func=ACT.Exp, scale=s),
                     r=[dkey, ("dspill", d, db), ("prm", "A_" + d), "dec"],
                     w=["dec"])
                p.op("vector", lambda e, o=bb_, a=du, b=Brep[n]:
                     e.tensor_mul(o, a, b),
                     r=["du", "bb"] + [("Brep", n, c) for c in range(4)],
                     w=["bb"])
                ini = 0.0 if d == "s" else init[db][:, n:n + 1]
                inik = [] if d == "s" else [("sin", db)]
                p.op("vector", lambda e, o=hh, d0=dec, d1=bb_, i=ini:
                     e.tensor_tensor_scan(out=o, data0=d0, data1=d1,
                                          initial=i, op0=ALU.mult,
                                          op1=ALU.add),
                     r=["dec", "bb", "hh"] + inik, w=["hh"])
                p.op("vector", lambda e, o=gg, a=hh, b=Crep[n]:
                     e.tensor_mul(o, a, b),
                     r=["hh", "gg"] + [("Crep", n, c) for c in range(4)],
                     w=["gg"])
                if d == "s":
                    p.op("vector", lambda e, o=self.stcol[db][:, n:n + 1],
                         i=hh[:, T - 1:T]: e.tensor_copy(out=o, in_=i),
                         r=["hh", ("stc", db)], w=[("stc", db)])
                for c in range(4):
                    _, bank, bkey = ybanks[c]
                    p.op("tensor", lambda e, o=bank[:, :], w=self.id_bf16,
                         rr=gg[:, c * 512:(c + 1) * 512], s=(n == 0),
                         t=(n == NST - 1): e.matmul(o, w, rr, start=s, stop=t),
                         r=["id", "gg"] + ([bkey] if n else []), w=[bkey])
            for c in range(4):
                bi, bank, bkey = ybanks[c]
                zb, zs = sc["zbuf"], sc["zsil"]
                yt, ys = sc["ytmp"], sc["yst"]
                p.dma("gpsimd", zb,
                      z_sp.ap()[db * 128:(db + 1) * 128,
                                c * 512:(c + 1) * 512],
                      ch="zld", r=["zb", ("zspill", d, db, c)], w=["zb"])
                p.op("scalar", lambda e, o=zs, i=zb: e.activation(
                    out=o, in_=i, func=ACT.Silu), r=["zb", "zs"], w=["zs"])
                p.op("vector", lambda e, o=yt,
                     i=uap[:, c * 512:(c + 1) * 512],
                     s=Dsk[:, db:db + 1], b=bank[:, :]:
                     e.scalar_tensor_tensor(out=o, in0=i, scalar=s, in1=b,
                                            op0=ALU.mult, op1=ALU.add),
                     r=[ukey, ("prm", "Dsk_" + d), bkey, "yt"], w=["yt"])
                p.op("vector", lambda e, o=ys, a=yt, b=zs:
                     e.tensor_mul(o, a, b), r=["yt", "zs", "yss"], w=["yss"])
                p.dma("gpsimd",
                      y_sp.ap()[db * 128:(db + 1) * 128,
                                c * 512:(c + 1) * 512],
                      ys, ch="ysp", r=["yss"],
                      w=["yss", ("yspill", d, db, c)])
                pp.free(bi)
        if d == "s":
            for db in range(NDB):
                p.dma("gpsimd", self.st_in.ap()[db * 128:(db + 1) * 128, :],
                      self.stcol[db], ch=("sti", db % 2),
                      r=[("stc", db)], w=[("sti", db)])
            p.coll("AllGather", ALU.bypass, [self.st_in.ap().opt()],
                   [self.st_out.ap().opt()],
                   r=[("sti", db) for db in range(NDB)], w=["sto"])

    def l0_wout(self, d):
        p, ar, pp = self.p, self.ar, self.pp
        woutT = self.ein("woutT_" + d, [DI, DM])
        y_sp = self.intern["y_sp_" + d]
        o_sp = self.idram("o_sp_" + d, [DM, T], BF16)
        waps = []
        for c in range(4):
            for k in range(NDB):
                for m in range(8):
                    waps.append(woutT.ap()[k * 128:(k + 1) * 128,
                                           m * 128:(m + 1) * 128])
        ws = self.get_stream("wot", waps, depth=5)
        yaps = []
        for c in range(4):
            for k in range(NDB):
                if d == "s":
                    yaps.append(y_sp.ap()[k * 128:(k + 1) * 128,
                                          c * 512:(c + 1) * 512])
                else:
                    lo = T - (c + 1) * 512
                    yaps.append(y_sp.ap()[k * 128:(k + 1) * 128, lo:lo + 512])
        ykeys = [[("yspill", d, k, (3 - c if d == "r" else c))]
                 for c in range(4) for k in range(NDB)]
        ys = self.get_stream("yot", yaps, depth=3, dtype=BF16, src_keys=ykeys)
        ost = self.scb["ost"]
        for c in range(4):
            self.pc_pump(100)
            banks = []
            for m in range(8):
                bi, bank, bkey = pp.get()
                pp.hold(bi)
                banks.append((bi, bank, bkey))
            for k in range(NDB):
                yap, ykey = ys.get(c * NDB + k)
                cc = 3 - c if d == "r" else c
                ydep = ("yspill", d, k, cc)
                rap = yap if d == "s" else yap[:, ::-1]
                for m in range(8):
                    _, bank, bkey = banks[m]
                    wap, wkey = ws.get((c * NDB + k) * 8 + m)
                    p.op("tensor", lambda e, o=bank[:, :], w=wap, rr=rap,
                         s=(k == 0), t=(k == NDB - 1):
                         e.matmul(o, w, rr, start=s, stop=t),
                         r=[wkey, ykey, ydep] + ([bkey] if k else []),
                         w=[bkey])
            for m in range(8):
                bi, bank, bkey = banks[m]
                p.op("scalar", lambda e, o=ost, i=bank: e.copy(
                    out=o, in_=i[:, :]), r=[bkey, "ost"], w=["ost"])
                p.dma("gpsimd",
                      o_sp.ap()[m * 128:(m + 1) * 128,
                                c * 512:(c + 1) * 512],
                      ost, ch="osp", r=["ost"],
                      w=["ost", ("ospill", d, m, c)])
                pp.free(bi)

    def l0_proj_ln(self):
        p, ar, pp = self.p, self.ar, self.pp
        projs = self.ein("projwT_s", [DM, DM])
        projr = self.ein("projwT_r", [DM, DM])
        self.stream1 = self.idram("stream1", [DM, T])
        pw = {}
        pstage = ar.alloc(DM, F32)
        for d, src in (("s", projs), ("r", projr)):
            sb = ar.alloc(DM * 8, BF16)
            for k in range(8):
                p.dma("sync", pstage, src.ap()[k * 128:(k + 1) * 128, :],
                      ch="c1", r=["pjs"], w=["pjs"])
                eng = "scalar" if k % 2 else "vector"
                dst = sb[:, k * DM:(k + 1) * DM]
                fn = ((lambda e, a=dst, b=pstage: e.copy(out=a, in_=b))
                      if eng == "scalar" else
                      (lambda e, a=dst, b=pstage: e.tensor_copy(out=a, in_=b)))
                p.op(eng, fn, r=["pjs", ("pj", d, k)], w=[("pj", d, k)])
            pw[d] = sb
        oaps = []
        for c in range(4):
            for d in ("s", "r"):
                o_sp = self.intern["o_sp_" + d]
                for k in range(8):
                    oaps.append(o_sp.ap()[k * 128:(k + 1) * 128,
                                          c * 512:(c + 1) * 512])
        okeys = []
        for c in range(4):
            for d in ("s", "r"):
                for k in range(8):
                    okeys.append([("ospill", d, k, c)])
        os = self.get_stream("ostr", oaps, depth=4, dtype=BF16, src_keys=okeys)
        res = [ar.alloc(T, F32) for _ in range(NFT)]
        for f in range(NFT):
            p.dma("gpsimd", res[f], self.xT.ap()[f * 128:(f + 1) * 128, :],
                  ch=("x", f % 4), w=[("res", f)])
        mtmp = ar.alloc(512, F32)
        projb = self.prm["projb"]
        oi = 0
        for c in range(4):
            banks = []
            for m in range(8):
                bi, bank, bkey = pp.get()
                pp.hold(bi)
                banks.append((bi, bank, bkey))
            first = True
            for d in ("s", "r"):
                for k in range(8):
                    oap, okey = os.get(oi)
                    odep = ("ospill", d, k, c)
                    oi += 1
                    for m in range(8):
                        _, bank, bkey = banks[m]
                        p.op("tensor", lambda e, o=bank[:, :],
                             w=pw[d][:, (k * 8 + m) * 128:
                                     (k * 8 + m + 1) * 128],
                             rr=oap, s=first, t=(d == "r" and k == 7):
                             e.matmul(o, w, rr, start=s, stop=t),
                             r=[("pj", d, k), okey, odep]
                             + ([] if first else [bkey]), w=[bkey])
                    first = False
            for m in range(8):
                bi, bank, bkey = banks[m]
                p.op("scalar", lambda e, o=mtmp, i=bank,
                     b=projb[:, m:m + 1]: e.activation(
                         out=o, in_=i[:, :], func=ACT.Identity, bias=b),
                     r=[bkey, ("prm", "projb"), "mtmp"], w=["mtmp"])
                p.op("vector", lambda e, o=res[m][:, c * 512:(c + 1) * 512],
                     i=res[m][:, c * 512:(c + 1) * 512], b=mtmp:
                     e.scalar_tensor_tensor(out=o, in0=i, scalar=2.0, in1=b,
                                            op0=ALU.mult, op1=ALU.add),
                     r=[("res", m), "mtmp"], w=[("res", m)])
                pp.free(bi)
        self.maybe_dump("mix2x", res, keys=[[("res", m)] for m in range(NFT)])
        s1st = [ar.alloc(512, F32) for _ in range(2)]

        def dst(f, c, nrm, key):
            st = s1st[(f * 4 + c) % 2]
            sk = ("s1st", (f * 4 + c) % 2)
            p.op("scalar", lambda e, o=st, i=nrm: e.copy(out=o, in_=i),
                 r=[key, sk], w=[sk])
            p.dma("gpsimd",
                  self.stream1.ap()[f * 128:(f + 1) * 128,
                                    c * 512:(c + 1) * 512],
                  st, ch=("s1", (f * 4 + c) % 2), r=[sk],
                  w=[sk, ("str1", f, c)])

        self.layernorm(lambda f: (res[f], ("res", f)), dst)

    def build_l0(self):
        ar = self.ar
        ar.mark()
        self.load_params_l0()
        ar.mark()
        self.xn = [ar.alloc(T, BF16) for _ in range(NFT)]
        ar.mark()
        self.l0_ln_xn()
        self.p.fence()
        ar.release()
        ar.mark()
        self.u_sb = [ar.alloc(3 + T, BF16) for _ in range(NDB)]
        self.zstage = [ar.alloc(512, BF16) for _ in range(2)]
        self.convacc = [ar.alloc(T, BF16) for _ in range(2)]
        self.alloc_xproj_bufs()
        self.l0_xz_conv("s")
        self.l0_xz_conv("r")
        self.l0_xproj_dt("s")
        self.l0_xproj_dt("r")
        self.p.fence()
        ar.release()
        ar.release()
        ar.mark()
        self.drop_streams("win", "udb", "usc", "dsc", "wot", "yot", "ostr")
        self.alloc_scan_bufs()
        self.l0_scan("s")
        self.l0_wout("s")
        self.l0_scan("r")
        self.l0_wout("r")
        self.p.fence()
        ar.release()
        ar.mark()
        self.l0_proj_ln()
        self.p.fence()
        ar.release()
        ar.release()
        for nm, rows, dt in (("u_sp_s", DI, BF16), ("z_sp_s", DI, BF16),
                             ("d_sp_s", DI, BF16), ("y_sp_s", DI, BF16),
                             ("o_sp_s", DM, BF16), ("u_sp_r", DI, BF16),
                             ("d_sp_r", DI, BF16), ("y_sp_r", DI, BF16),
                             ("o_sp_r", DM, BF16), ("stream1", DM, F32)):
            if nm in self.dump:
                self.ar.mark()
                self.dump_dram(nm, self.intern[nm], rows, T, dt)
                self.ar.release()


    def ternarize(self, tag, wT, K, M, wq_dram):
        p, ar, pp = self.p, self.ar, self.pp
        ar.mark()
        acc = ar.alloc(1, F32)
        red = ar.alloc(1, F32)
        p.op("vector", lambda e, a=acc: e.memset(a, 0.0), w=[(tag, "acc")])
        aps1 = [wT.ap()[ki * 128:(ki + 1) * 128, ci * 512:(ci + 1) * 512]
                for ki in range(K // 128) for ci in range(M // 512)]
        ws1 = WStream(p, ar, "tz" + tag, aps1, depth=4, cast=False)
        for i in range(len(aps1)):
            wap, wkey = ws1.get(i)
            p.op("vector", lambda e, o=red, a=wap: e.tensor_reduce(
                out=o, in_=a, axis=mybir.AxisListType.X, op=ALU.add,
                apply_absolute_value=True),
                r=[wkey, (tag, "red")], w=[(tag, "red")])
            p.op("vector", lambda e, o=acc, a=acc, b=red:
                 e.tensor_add(o, a, b),
                 r=[(tag, "acc"), (tag, "red")], w=[(tag, "acc")])
        bi, bank, bkey = pp.get()
        pp.hold(bi)
        p.op("tensor", lambda e, o=bank[:, 0:1], w=self.ones_f32, rr=acc:
             e.matmul(o, w, rr, start=True, stop=True),
             r=["ones", (tag, "acc")], w=[bkey])
        srep = ar.alloc(1, F32)
        rs = ar.alloc(1, F32)
        p.op("scalar", lambda e, o=srep, i=bank[:, 0:1]: e.mul(
            out=o, in_=i, mul=1.0 / (K * M)), r=[bkey], w=[(tag, "s")])
        pp.free(bi)
        p.op("vector", lambda e, o=srep, a=srep: e.tensor_scalar_max(
            out=o, in0=a, scalar1=EPS), r=[(tag, "s")], w=[(tag, "s")])
        p.op("vector", lambda e, o=rs, i=srep: e.reciprocal(out=o, in_=i),
             r=[(tag, "s"), (tag, "rs")], w=[(tag, "rs")])
        g1 = ar.alloc(512, F32)
        g2 = ar.alloc(512, F32)
        wq = ar.alloc(512, BF16)
        ws2 = WStream(p, ar, "tz2" + tag, aps1, depth=4, cast=False)
        for i, (ki, ci) in enumerate([(ki, ci) for ki in range(K // 128)
                                      for ci in range(M // 512)]):
            wap, wkey = ws2.get(i)
            p.op("vector", lambda e, o=g1, a=wap, s=rs:
                 e.tensor_scalar(out=o, in0=a, scalar1=s, scalar2=0.5,
                                 op0=ALU.mult, op1=ALU.is_ge),
                 r=[wkey, (tag, "rs"), (tag, "g1")], w=[(tag, "g1")])
            p.op("vector", lambda e, o=g2, a=wap, s=rs:
                 e.tensor_scalar(out=o, in0=a, scalar1=s, scalar2=-0.5,
                                 op0=ALU.mult, op1=ALU.is_le),
                 r=[wkey, (tag, "rs"), (tag, "g2")], w=[(tag, "g2")])
            p.op("vector", lambda e, o=wq, a=g1, b=g2: e.tensor_sub(o, a, b),
                 r=[(tag, "g1"), (tag, "g2"), (tag, "wq")], w=[(tag, "wq")])
            p.dma("gpsimd",
                  wq_dram.ap()[ki * 128:(ki + 1) * 128,
                               ci * 512:(ci + 1) * 512],
                  wq, ch=(tag, "sp"), r=[(tag, "wq")],
                  w=[(tag, "wq"), (tag, "q", ki, ci)])
        sret = self.srep_keep.pop(0)
        p.op("vector", lambda e, o=sret, a=srep: e.tensor_copy(out=o, in_=a),
             r=[(tag, "s"), (tag, "skeep")], w=[(tag, "skeep")])
        self.ar.release()
        return sret

    def build_ffn(self):
        p, ar, pp = self.p, self.ar, self.pp
        self.pc_pump(10 ** 6)
        w1T = self.ein("w1T", [DM, HID])
        w2T = self.ein("w2T", [HID, DM])
        w1q_sp = self.idram("w1q_sp", [DM, HID], BF16)
        w2q_sp = self.idram("w2q_sp", [HID, DM], BF16)
        self.stream2 = self.idram("stream2", [DM, T])
        self.xg_in = self.idram("xg_in", [DM, T], BF16)
        ar.mark()
        self.srep_keep = [ar.alloc(1, F32), ar.alloc(1, F32)]
        s_w1 = self.ternarize("t1", w1T, DM, HID, w1q_sp)
        s_w2 = self.ternarize("t2", w2T, HID, DM, w2q_sp)
        s1 = [ar.alloc(T, F32) for _ in range(NFT)]
        for f in range(NFT):
            p.dma("gpsimd", s1[f],
                  self.stream1.ap()[f * 128:(f + 1) * 128, :],
                  ch=("x", f % 4), r=[("str1", f, c) for c in range(4)],
                  w=[("s1", f)])
        xqT = [ar.alloc(T, BF16) for _ in range(NFT)]
        xtok = ar.alloc(DM, F32)
        q1 = ar.alloc(512, F32)
        am = ar.alloc(1, F32)
        sc = ar.alloc(1, F32)
        corr = ar.alloc(NST, F32)
        cmagic = ar.alloc(1, F32)
        cm1536 = ar.alloc(1, F32)
        p.op("vector", lambda e, a=cmagic: e.memset(a, 1536.5), w=["cmagic"])
        p.op("vector", lambda e, a=cm1536: e.memset(a, -1536.0), w=["cm1536"])
        for j in range(NST):
            bi, bank, bkey = pp.get()
            pp.hold(bi)
            bi2, bank2, bkey2 = pp.get()
            pp.hold(bi2)
            for f in range(NFT):
                bk, bb, kk = (bi, bank, bkey) if f < 4 else (bi2, bank2, bkey2)
                p.op("tensor", lambda e, o=bb[:, (f % 4) * 128:(f % 4 + 1) * 128],
                     i=s1[f][:, j * 128:(j + 1) * 128], d=self.id_f32:
                     e.transpose(o, i, d),
                     r=[("s1", f), "idf32", kk], w=[kk])
            for half, (bk, bb, kk) in enumerate(((bi, bank, bkey),
                                                 (bi2, bank2, bkey2))):
                p.op("scalar", lambda e, o=xtok[:, half * 512:(half + 1) * 512],
                     i=bb[:, :]: e.copy(out=o, in_=i),
                     r=[kk, "xtok"], w=["xtok"])
            pp.free(bi)
            pp.free(bi2)
            p.op("vector", lambda e, o=am, a=xtok: e.tensor_reduce(
                out=o, in_=a, axis=mybir.AxisListType.X, op=ALU.max,
                apply_absolute_value=True), r=["xtok", "am"], w=["am"])
            p.op("vector", lambda e, o=am, a=am: e.tensor_scalar_max(
                out=o, in0=a, scalar1=EPS), r=["am"], w=["am"])
            p.op("vector", lambda e, o=corr[:, j:j + 1], a=am,
                 s=s_w1: e.scalar_tensor_tensor(
                     out=o, in0=a, scalar=1.0 / 127.0, in1=s,
                     op0=ALU.mult, op1=ALU.mult),
                 r=["am", ("t1", "skeep"), "corr"], w=["corr"])
            p.op("vector", lambda e, o=sc, i=am: e.reciprocal(out=o, in_=i),
                 r=["am", "sc"], w=["sc"])
            p.op("vector", lambda e, o=sc, a=sc: e.tensor_scalar_mul(
                out=o, in0=a, scalar1=127.0), r=["sc"], w=["sc"])
            for half in range(2):
                p.op("scalar", lambda e, o=q1,
                     i=xtok[:, half * 512:(half + 1) * 512], s=sc, b=cmagic:
                     e.activation(out=o, in_=i, func=ACT.Identity, scale=s,
                                  bias=b),
                     r=["xtok", "sc", "cmagic", "q1"], w=["q1"])
                p.op("vector", lambda e, o=q1.bitcast(I32), a=q1.bitcast(I32):
                     e.tensor_single_scalar(out=o, in_=a, scalar=-8192,
                                            op=ALU.bitwise_and),
                     r=["q1"], w=["q1"])
                bi, bank, bkey = pp.get()
                pp.hold(bi)
                for f4 in range(4):
                    f = half * 4 + f4
                    p.op("tensor", lambda e, o=bank[:, f4 * 128:(f4 + 1) * 128],
                         i=q1[:, f4 * 128:(f4 + 1) * 128], d=self.id_f32:
                         e.transpose(o, i, d), r=["q1", "idf32", bkey],
                         w=[bkey])
                for f4 in range(4):
                    f = half * 4 + f4
                    p.op("scalar", lambda e, o=xqT[f][:, j * 128:(j + 1) * 128],
                         i=bank[:, f4 * 128:(f4 + 1) * 128], b=cm1536:
                         e.activation(out=o, in_=i, func=ACT.Identity,
                                      bias=b),
                         r=[bkey, "cm1536", ("xqT", f)], w=[("xqT", f)])
                pp.free(bi)
        self.maybe_dump("xqT", [xqT[f] for f in range(NFT)],
                        keys=[[("xqT", f)] for f in range(NFT)], cols=T)
        self.maybe_dump("corr", [corr], keys=[["corr"]], cols=NST)
        self.maybe_dump("sw1", [s_w1], keys=[[("t1", "skeep")]], cols=1)
        self.maybe_dump("sw2", [s_w2], keys=[[("t2", "skeep")]], cols=1)
        NG = 8
        w1aps = []
        for g in range(NG):
            for hc in range(8):
                for k in range(8):
                    w1aps.append(w1q_sp.ap()[k * 128:(k + 1) * 128,
                                             hc * 512:(hc + 1) * 512])
        w1s = WStream(p, ar, "w1q", w1aps, depth=3, dtype=BF16,
                      engs=("sync", "gpsimd"),
                      src_keys=[[("t1", "q", k, hc)]
                                for g in range(NG) for hc in range(8)
                                for k in range(8)])
        w2aps = []
        for g in range(NG):
            for dc in range(8):
                for hk in range(32):
                    w2aps.append(w2q_sp.ap()[hk * 128:(hk + 1) * 128,
                                             dc * 128:(dc + 1) * 128])
        w2s = WStream(p, ar, "w2q", w2aps, depth=6, dtype=BF16,
                      engs=("gpsimd", "sync"),
                      src_keys=[[("t2", "q", hk, dc // 4)]
                                for g in range(NG) for dc in range(8)
                                for hk in range(32)])
        hb = [ar.alloc(HID, F32) for _ in range(2)]
        hq1 = ar.alloc(1024, F32)
        hqT = [ar.alloc(256, BF16) for _ in range(32)]
        am2 = ar.alloc(1, F32)
        sc2 = ar.alloc(1, F32)
        corr2 = ar.alloc(1, F32)
        nbias = ar.alloc(1, F32)
        w1i = 0
        w2i = 0
        for g in range(NG):
            for hc in range(8):
                banks = []
                for mt in range(2):
                    bi, bank, bkey = pp.get()
                    pp.hold(bi)
                    banks.append((bi, bank, bkey))
                for k in range(8):
                    wap, wkey = w1s.get(w1i)
                    w1i += 1
                    for mt in range(2):
                        _, bank, bkey = banks[mt]
                        j = g * 2 + mt
                        p.op("tensor", lambda e, o=bank[:, :],
                             w=xqT[k][:, j * 128:(j + 1) * 128], rr=wap,
                             s=(k == 0), t=(k == 7):
                             e.matmul(o, w, rr, start=s, stop=t),
                             r=[("xqT", k), wkey] + ([bkey] if k else []),
                             w=[bkey])
                for mt in range(2):
                    bi, bank, bkey = banks[mt]
                    j = g * 2 + mt
                    p.op("scalar", lambda e,
                         o=hb[mt][:, hc * 512:(hc + 1) * 512], i=bank[:, :],
                         s=corr[:, j:j + 1]:
                         e.activation(out=o, in_=i, func=ACT.Gelu, scale=s),
                         r=[bkey, "corr", ("hb", mt, hc)],
                         w=[("hb", mt, hc)])
                    pp.free(bi)
            if g == 0:
                self.maybe_dump("hb0", [hb[0], hb[1]],
                                keys=[[("hb", m, hc) for hc in range(8)]
                                      for m in range(2)], cols=HID)
            for mt in range(2):
                hbkeys = [("hb", mt, hc) for hc in range(8)]
                p.op("vector", lambda e, o=am2, a=hb[mt]: e.tensor_reduce(
                    out=o, in_=a, axis=mybir.AxisListType.X, op=ALU.max,
                    apply_absolute_value=True),
                    r=hbkeys + ["am2"], w=["am2"])
                p.op("vector", lambda e, o=am2, a=am2: e.tensor_scalar_max(
                    out=o, in0=a, scalar1=EPS), r=["am2"], w=["am2"])
                p.op("vector", lambda e, o=corr2, a=am2, s=s_w2:
                     e.scalar_tensor_tensor(out=o, in0=a, scalar=1.0 / 127.0,
                                            in1=s, op0=ALU.mult, op1=ALU.mult),
                     r=["am2", ("t2", "skeep"), "corr2"], w=["corr2"])
                p.op("vector", lambda e, o=nbias, a=corr2:
                     e.tensor_scalar_mul(out=o, in0=a, scalar1=-1536.0),
                     r=["corr2", "nbias"], w=["nbias"])
                p.op("vector", lambda e, o=sc2, i=am2: e.reciprocal(
                    out=o, in_=i), r=["am2", "sc2"], w=["sc2"])
                p.op("vector", lambda e, o=sc2, a=sc2: e.tensor_scalar_mul(
                    out=o, in0=a, scalar1=127.0), r=["sc2"], w=["sc2"])
                for qtr in range(4):
                    p.op("scalar", lambda e, o=hq1,
                         i=hb[mt][:, qtr * 1024:(qtr + 1) * 1024], s=sc2,
                         b=cmagic: e.activation(out=o, in_=i,
                                                func=ACT.Identity,
                                                scale=s, bias=b),
                         r=hbkeys + ["sc2", "cmagic", "hq1"], w=["hq1"])
                    p.op("vector", lambda e, o=hq1.bitcast(I32),
                         a=hq1.bitcast(I32):
                         e.tensor_single_scalar(out=o, in_=a, scalar=-8192,
                                                op=ALU.bitwise_and),
                         r=["hq1"], w=["hq1"])
                    p.op("scalar", lambda e, o=hq1, i=hq1, s=corr2, b=nbias:
                         e.activation(out=o, in_=i, func=ACT.Identity,
                                      scale=s, bias=b),
                         r=["hq1", "corr2", "nbias"], w=["hq1"])
                    for hk in range(8):
                        if hk % 4 == 0:
                            bi, bank, bkey = pp.get()
                            pp.hold(bi)
                        p.op("tensor", lambda e,
                             o=bank[:, (hk % 4) * 128:(hk % 4 + 1) * 128],
                             i=hq1[:, hk * 128:(hk + 1) * 128], d=self.id_f32:
                             e.transpose(o, i, d),
                             r=["hq1", "idf32", bkey], w=[bkey])
                        if hk % 4 == 3:
                            hkg = qtr * 8 + hk - 3
                            for q in range(4):
                                p.op("scalar", lambda e,
                                     o=hqT[hkg + q][:, mt * 128:(mt + 1) * 128],
                                     i=bank[:, q * 128:(q + 1) * 128]:
                                     e.copy(out=o, in_=i),
                                     r=[bkey, ("hqT", hkg + q)],
                                     w=[("hqT", hkg + q)])
                            pp.free(bi)
            if g == 0:
                self.maybe_dump("hqT0", hqT,
                                keys=[[("hqT", i)] for i in range(32)],
                                cols=256)
            for dc in range(8):
                bi, bank, bkey = pp.get()
                pp.hold(bi)
                for hk in range(32):
                    wap, wkey = w2s.get(w2i)
                    w2i += 1
                    p.op("tensor", lambda e, o=bank[:, 0:256], w=wap,
                         rr=hqT[hk][:, :], s=(hk == 0), t=(hk == 31):
                         e.matmul(o, w, rr, start=s, stop=t),
                         r=[wkey, ("hqT", hk)] + ([bkey] if hk else []),
                         w=[bkey])
                f = dc
                cs = g * 256
                p.op("vector", lambda e, o=s1[f][:, cs:cs + 256],
                     a=s1[f][:, cs:cs + 256], b=bank[:, 0:256]:
                     e.tensor_add(o, a, b), r=[("s1", f), bkey],
                     w=[("s1", f)])
                pp.free(bi)
        self.maybe_dump("s1post", s1, keys=[[("s1", f)] for f in range(NFT)],
                        cols=T)
        s2st = [ar.alloc(512, F32) for _ in range(2)]
        s2bt = [ar.alloc(512, BF16) for _ in range(2)]

        def dst2(f, c, nrm, key):
            i2 = (f * 4 + c) % 2
            st = s2st[i2]
            bt = s2bt[i2]
            p.op("scalar", lambda e, o=st, i=nrm: e.copy(out=o, in_=i),
                 r=[key, ("s2st", i2)], w=[("s2st", i2)])
            p.op("vector", lambda e, o=bt, i=st: e.tensor_copy(out=o, in_=i),
                 r=[("s2st", i2), ("s2bt", i2)], w=[("s2bt", i2)])
            p.dma("gpsimd",
                  self.stream2.ap()[f * 128:(f + 1) * 128,
                                    c * 512:(c + 1) * 512],
                  st, ch=("s1", i2), r=[("s2st", i2)],
                  w=[("s2st", i2), ("str2", f, c)])
            p.dma("gpsimd",
                  self.xg_in.ap()[f * 128:(f + 1) * 128,
                                  c * 512:(c + 1) * 512],
                  bt, ch=("s2b", i2), r=[("s2bt", i2)],
                  w=[("s2bt", i2), ("xgi", f, c)])

        self.layernorm(lambda f: (s1[f], ("s1", f)), dst2)
        self.p.fence()
        ar.release()

    def build_attn(self):
        p, ar, pp = self.p, self.ar, self.pp
        wkT = self.ein("wkT", [DM, 512])
        wqT = self.ein("wqT", [DM, 512])
        wvT = self.ein("wvT", [DM, 512])
        slopes = self.ein("slopes", [128, 8])
        maskA = self.ein("maskA", [128, 1])
        maskB = self.ein("maskB", [128, 1])
        self.xg = self.idram("xg", [2, DM, T], BF16)
        kqv_sp = self.idram("kqv_sp", [1536, S_FULL], BF16)
        attn_in = self.idram("attn_in", [512, S_FULL], BF16)
        attn_out = self.idram("attn_out", [2, 512, S_FULL], BF16)
        self.stream3 = self.idram("stream3", [DM, T])
        self.x3g = self.idram("x3g", [DM, T], BF16)
        tkcol_np = np.concatenate(
            [np.arange(2048, dtype=np.float32),
             np.arange(4095, 2047, -1, dtype=np.float32)]
        ).reshape(32, 128).T.copy()
        tkcolt = self.nc.inline_tensor(tkcol_np, "truekcol")
        tkc = self.nc.inline_tensor(
            np.concatenate([np.arange(2048, dtype=np.float32),
                            np.arange(4095, 2047, -1, dtype=np.float32)]
                           ).reshape(1, S_FULL), "truek")
        tri = np.tril(np.full((128, 128), -1e30, dtype=np.float32), -1)
        trj = np.triu(np.full((128, 128), -1e30, dtype=np.float32), 1)
        mlo = self.nc.inline_tensor(tri, "masklo")
        mup = self.nc.inline_tensor(trj, "maskup")
        p.coll("AllGather", ALU.bypass, [self.xg_in.ap().opt()],
               [self.xg.ap().opt()],
               r=[("xgi", f, c) for f in range(NFT) for c in range(4)],
               w=["xg"])
        ar.mark()
        xt = [ar.alloc(T, BF16) for _ in range(NFT * 2)]
        for sl in range(2):
            for f in range(NFT):
                p.dma("gpsimd", xt[sl * NFT + f],
                      self.xg.ap()[sl, f * 128:(f + 1) * 128, :],
                      ch=("x", f % 4), r=["xg"], w=[("xt", sl, f)])
        kst = ar.alloc(512, BF16)
        for wi, (wt, base) in enumerate(((wkT, 0), (wqT, 512), (wvT, 1024))):
            aps = [wt.ap()[k * 128:(k + 1) * 128, m * 128:(m + 1) * 128]
                   for m in range(4) for k in range(8)]
            ws = self.get_stream("wkqv", aps)
            for m in range(4):
                banks = []
                for tc8 in range(8):
                    bi, bank, bkey = pp.get()
                    pp.hold(bi)
                    banks.append((bi, bank, bkey))
                for k in range(8):
                    wap, wkey = ws.get(m * 8 + k)
                    for tc8 in range(8):
                        _, bank, bkey = banks[tc8]
                        sl, tc = tc8 // 4, tc8 % 4
                        p.op("tensor", lambda e, o=bank[:, :], w=wap,
                             rr=xt[sl * NFT + k][:, tc * 512:(tc + 1) * 512],
                             s=(k == 0), t=(k == 7):
                             e.matmul(o, w, rr, start=s, stop=t),
                             r=[wkey, ("xt", sl, k)] + ([bkey] if k else []),
                             w=[bkey])
                for tc8 in range(8):
                    bi, bank, bkey = banks[tc8]
                    if wi == 1:
                        p.op("scalar", lambda e, o=kst, i=bank, s=self.c32:
                             e.activation(out=o, in_=i[:, :],
                                          func=ACT.Identity, scale=s),
                             r=[bkey, "kst", "c32"], w=["kst"])
                    else:
                        p.op("scalar", lambda e, o=kst, i=bank:
                             e.copy(out=o, in_=i[:, :]),
                             r=[bkey, "kst"], w=["kst"])
                    p.dma("gpsimd",
                          kqv_sp.ap()[base + m * 128:base + (m + 1) * 128,
                                      tc8 * 512:(tc8 + 1) * 512],
                          kst, ch="kqv", r=["kst"],
                          w=["kst", ("kqv", wi, m, tc8)])
                    pp.free(bi)
        self.drop_streams("wkqv")
        kqvkeys = [("kqv", wi, m, tc8) for wi in range(3) for m in range(4)
                   for tc8 in range(8)]
        p.fence()
        ar.release()
        ar.mark()
        slp = ar.alloc(8, F32)
        p.dma("sync", slp, slopes.ap(), ch="c1", w=["slp"])
        mlo_sb = ar.alloc(128, F32)
        mup_sb = ar.alloc(128, F32)
        tk_sb = ar.alloc(S_FULL, F32, parts=1)
        p.dma("sync", mlo_sb, mlo.ap(), ch="c0", w=["mlo"])
        p.dma("sync", mup_sb, mup.ap(), ch="c0", r=["mlo"], w=["mup"])
        p.dma("sync", tk_sb[0:1, :], tkc.ap(), ch="c0", r=["mup"], w=["tk"])
        tkcol = ar.alloc(32, F32)
        p.dma("sync", tkcol, tkcolt.ap(), ch="c0", r=["tk"], w=["tkcol"])
        kh = ar.alloc(S_FULL, BF16)
        qh = ar.alloc(S_FULL, BF16)
        vv = [ar.alloc(72, BF16) for _ in range(32)]
        vt = ar.alloc(S_FULL, BF16, parts=64)
        r0 = ar.alloc(S_FULL, F32, parts=1)
        r1 = ar.alloc(S_FULL, F32, parts=1)
        r2 = ar.alloc(S_FULL, F32, parts=1)
        rb1 = ar.alloc(S_FULL, BF16, parts=1)
        sq = ar.alloc(S_FULL, F32, parts=64)
        pt = ar.alloc(512, BF16)
        osr = ar.alloc(512, F32)
        ost = ar.alloc(512, BF16)

        onesrow = ar.alloc(S_FULL, BF16, parts=1)
        p.op("vector", lambda e, a=onesrow[0:1, :]: e.memset(a, 1.0),
             w=["onesrow"])

        def qlo_of(kj):
            if kj < 16:
                return kj
            return 16

        acol = ar.alloc(32, F32)
        for h in range(NH):
            p.dma("gpsimd", kh[0:64, :],
                  kqv_sp.ap()[h * 64:(h + 1) * 64, :], ch="kh",
                  r=kqvkeys + ["khb"], w=["khb"])
            p.dma("gpsimd", kh[64:65, :], onesrow[0:1, :], ch="khr1",
                  r=["onesrow", "khb"], w=["khb"])
            p.op("vector", lambda e, o=acol, a=tkcol, s=slp[:, h:h + 1]:
                 e.tensor_scalar_mul(out=o, in0=a, scalar1=s),
                 r=["tkcol", "slp", "acol"], w=["acol"])
            p.dma("gpsimd", qh[0:64, :],
                  kqv_sp.ap()[512 + h * 64:512 + (h + 1) * 64, :], ch="qh",
                  r=kqvkeys + ["qhb"], w=["qhb"])
            p.op("scalar", lambda e, o=sq[0:64, :], i=qh[0:64, :]:
                 e.square(out=o, in_=i), r=["qhb", "sq"], w=["sq"])
            for c in range(8):
                bi, bank, bkey = pp.get()
                pp.hold(bi)
                p.op("tensor", lambda e, o=bank[0:1, :],
                     w=self.ones_f32[0:64, 0:1],
                     rr=sq[0:64, c * 512:(c + 1) * 512]:
                     e.matmul(o, w, rr, start=True, stop=True),
                     r=["ones", "sq"], w=[bkey])
                p.op("scalar", lambda e, o=r0[0:1, c * 512:(c + 1) * 512],
                     i=bank[0:1, :]: e.activation(out=o, in_=i,
                                                  func=ACT.Sqrt),
                     r=[bkey, "r0"], w=["r0"])
                pp.free(bi)
            p.op("scalar", lambda e, o=sq[0:64, :], i=kh[0:64, :]:
                 e.square(out=o, in_=i), r=["khb", "sq"], w=["sq"])
            for c in range(8):
                bi, bank, bkey = pp.get()
                pp.hold(bi)
                p.op("tensor", lambda e, o=bank[0:1, :],
                     w=self.ones_f32[0:64, 0:1],
                     rr=sq[0:64, c * 512:(c + 1) * 512]:
                     e.matmul(o, w, rr, start=True, stop=True),
                     r=["ones", "sq"], w=[bkey])
                p.op("scalar", lambda e, o=r1[0:1, c * 512:(c + 1) * 512],
                     i=bank[0:1, :]: e.copy(out=o, in_=i),
                     r=[bkey, "r1"], w=["r1"])
                pp.free(bi)
            p.op("vector", lambda e, o=r2[0:1, 0:1], a=r1[0:1, :]:
                 e.tensor_reduce(out=o, in_=a, axis=mybir.AxisListType.X,
                                 op=ALU.max), r=["r1", "r2"], w=["r2"])
            p.op("scalar", lambda e, o=r2[0:1, 0:1], i=r2[0:1, 0:1]:
                 e.activation(out=o, in_=i, func=ACT.Sqrt),
                 r=["r2"], w=["r2"])
            p.op("vector", lambda e, o=r0[0:1, :], a=r0[0:1, :],
                 s=r2[0:1, 0:1]: e.tensor_scalar_mul(out=o, in0=a, scalar1=s),
                 r=["r0", "r2"], w=["r0"])
            p.op("vector", lambda e, o=r1[0:1, :], a=tk_sb[0:1, :],
                 s=slp[0:1, h:h + 1], b=r0[0:1, :]:
                 e.scalar_tensor_tensor(out=o, in0=a, scalar=s, in1=b,
                                        op0=ALU.mult, op1=ALU.add),
                 r=["tk", "slp", "r0", "r1"], w=["r1"])
            p.op("vector", lambda e, o=rb1[0:1, :], a=r1[0:1, :]:
                 e.tensor_scalar_mul(out=o, in0=a, scalar1=-1.0),
                 r=["r1", "rb1"], w=["rb1"])
            p.dma("gpsimd", qh[64:65, :], rb1[0:1, :], ch="qhr",
                  r=["rb1", "qhb"], w=["qhb"])
            p.dma("gpsimd", vt[0:64, :],
                  kqv_sp.ap()[1024 + h * 64:1024 + (h + 1) * 64, :], ch="vh",
                  r=kqvkeys + ["vtb"], w=["vtb"])
            for kj in range(32):
                bi, bank, bkey = pp.get()
                pp.hold(bi)
                p.op("vector", lambda e, a=vv[kj][0:128, 64:65]:
                     e.memset(a, 1.0), r=[("vv", kj)], w=[("vv", kj)])
                p.op("tensor", lambda e, o=bank[:, 0:32].bitcast(BF16),
                     i=vt[0:64, kj * 128:(kj + 1) * 128],
                     d=self.id_bf16[0:64, 0:64]:
                     e.transpose(o, i, d), r=["vtb", "id", bkey], w=[bkey])
                p.op("scalar", lambda e, o=vv[kj][0:128, 0:64],
                     i=bank[:, 0:32].bitcast(BF16):
                     e.copy(out=o, in_=i), r=[bkey, ("vv", kj)],
                     w=[("vv", kj)])
                pp.free(bi)
                if h == 1 and kj == 0 and "vvdump" in self.dump:
                    self.maybe_dump("vvdump", [vv[0]], keys=[[("vv", 0)]],
                                    cols=72)
            for qc in range(8):
                q0t = qc * 4
                bo, bank_o, bkey_o = pp.get()
                pp.hold(bo)
                kjs = []
                for kj in range(32):
                    if kj < 16:
                        if q0t + 3 >= kj:
                            kjs.append(kj)
                    else:
                        if q0t + 3 >= 16 and q0t <= kj:
                            kjs.append(kj)
                first = True
                for kj in kjs:
                    bs, bank_s, bkey_s = pp.get()
                    pp.hold(bs)
                    p.op("tensor", lambda e, o=bank_s[:, :],
                         w=kh[0:65, kj * 128:(kj + 1) * 128],
                         rr=qh[0:65, qc * 512:(qc + 1) * 512]:
                         e.matmul(o, w, rr, start=True, stop=True),
                         r=["khb", "qhb"], w=[bkey_s])
                    if h == 1 and qc == 0 and kj == 0 and                             "sdump" in self.dump:
                        sdbuf = self.ar.alloc(512, F32)
                        p.op("scalar", lambda e, o=sdbuf, i=bank_s:
                             e.copy(out=o, in_=i[:, :]),
                             r=[bkey_s, "sdb"], w=["sdb"])
                        self.maybe_dump("sdump", [sdbuf], keys=[["sdb"]],
                                        cols=512)
                        self.maybe_dump("mdump", [mlo_sb], keys=[["mlo"]],
                                        cols=128)
                    for st in range(4):
                        qt = q0t + st
                        sub = bank_s[:, st * 128:(st + 1) * 128]
                        if kj < 16:
                            if qt < 16:
                                if qt < kj:
                                    p.op("vector", lambda e, o=sub, a=sub:
                                         e.tensor_scalar_add(
                                             out=o, in0=a, scalar1=-1e30),
                                         r=[bkey_s], w=[bkey_s])
                                elif qt == kj:
                                    p.op("vector", lambda e, o=sub, a=sub,
                                         b=mlo_sb: e.tensor_add(o, a, b),
                                         r=[bkey_s, "mlo"], w=[bkey_s])
                        else:
                            if qt < 16 or qt > kj:
                                p.op("vector", lambda e, o=sub, a=sub:
                                     e.tensor_scalar_add(
                                         out=o, in0=a, scalar1=-1e30),
                                     r=[bkey_s], w=[bkey_s])
                            elif qt == kj:
                                p.op("vector", lambda e, o=sub, a=sub,
                                     b=mup_sb: e.tensor_add(o, a, b),
                                     r=[bkey_s, "mup"], w=[bkey_s])
                    p.op("scalar", lambda e, o=pt, i=bank_s,
                         b=acol[:, kj:kj + 1]: e.activation(
                             out=o, in_=i[:, :], func=ACT.Exp, bias=b),
                         r=[bkey_s, "pt", "acol"], w=["pt"])
                    if h == 1 and qc == 0 and kj == 0 and                             "ptdump" in self.dump:
                        self.maybe_dump("ptdump", [pt], keys=[["pt"]],
                                        cols=512)
                    pp.free(bs)
                    p.op("tensor", lambda e, o=bank_o[0:65, :],
                         w=vv[kj][0:128, 0:65], rr=pt[:, :],
                         s=first, t=(kj == kjs[-1]):
                         e.matmul(o, w, rr, start=s, stop=t),
                         r=[("vv", kj), "pt"] + ([] if first else [bkey_o]),
                         w=[bkey_o])
                    first = False
                p.op("scalar", lambda e, o=osr[64:65, :], i=bank_o[64:65, :]:
                     e.copy(out=o, in_=i), r=[bkey_o, "osr"], w=["osr"])
                p.dma("gpsimd", r2[0:1, 0:512], osr[64:65, :], ch="snrm",
                      r=["osr", "r2"], w=["r2"])
                p.op("vector", lambda e, o=r2[0:1, 0:512], i=r2[0:1, 0:512]:
                     e.reciprocal(out=o, in_=i), r=["r2"], w=["r2"])
                bb, bank_b, bkey_b = pp.get()
                pp.hold(bb)
                p.op("tensor", lambda e, o=bank_b[0:64, :],
                     w=self.ones_f32[0:1, 0:64], rr=r2[0:1, 0:512]:
                     e.matmul(o, w, rr, start=True, stop=True),
                     r=["ones", "r2"], w=[bkey_b])
                p.op("scalar", lambda e, o=osr[0:64, :], i=bank_b[0:64, :]:
                     e.copy(out=o, in_=i), r=[bkey_b, "osr"],
                     w=["osr"])
                pp.free(bb)
                p.op("vector", lambda e, o=ost[0:64, :], a=bank_o[0:64, :],
                     b=osr[0:64, :]: e.tensor_mul(o, a, b),
                     r=[bkey_o, "osr", "ostA"], w=["ostA"])
                pp.free(bo)
                p.dma("gpsimd",
                      attn_in.ap()[h * 64:(h + 1) * 64,
                                   qc * 512:(qc + 1) * 512],
                      ost[0:64, :], ch="aout", r=["ostA"],
                      w=["ostA", ("ain", h, qc)])
        p.coll("AllGather", ALU.bypass, [attn_in.ap().opt()],
               [attn_out.ap().opt()],
               r=[("ain", h, qc) for h in range(NH) for qc in range(8)],
               w=["aog"])
        p.fence()
        ar.release()
        ar.mark()
        mA = ar.alloc(1, F32)
        mB = ar.alloc(1, F32)
        p.dma("sync", mA, maskA.ap(), ch="c1", w=["mA"])
        p.dma("sync", mB, maskB.ap(), ch="c1", r=["mA"], w=["mB"])
        res = [ar.alloc(T, F32) for _ in range(NFT)]
        ao = ar.alloc(T, BF16)
        at = ar.alloc(T, BF16)
        for f in range(NFT):
            p.dma("gpsimd", res[f],
                  self.stream2.ap()[f * 128:(f + 1) * 128, :],
                  ch=("x", f % 4), r=[("str2", f, c) for c in range(4)],
                  w=[("res", f)])
        for f in range(NFT):
            sl = f // 4
            fr = f % 4
            p.dma("gpsimd", ao,
                  attn_out.ap()[sl, fr * 128:(fr + 1) * 128, 0:T],
                  ch="ao0", r=["aog", "aobuf"], w=["aobuf"])
            p.dma("gpsimd", at,
                  attn_out.ap()[sl, fr * 128:(fr + 1) * 128, T:2 * T],
                  ch="ao1", r=["aog", "atbuf"], w=["atbuf"])
            p.op("vector", lambda e, o=ao, a=ao, s=mA:
                 e.tensor_scalar_mul(out=o, in0=a, scalar1=s),
                 r=["aobuf", "mA"], w=["aobuf"])
            p.op("vector", lambda e, o=ao, a=at, s=mB, b=ao:
                 e.scalar_tensor_tensor(out=o, in0=a, scalar=s, in1=b,
                                        op0=ALU.mult, op1=ALU.add),
                 r=["atbuf", "mB", "aobuf"], w=["aobuf"])
            p.op("vector", lambda e, o=res[f], a=res[f], b=ao:
                 e.tensor_add(o, a, b), r=[("res", f), "aobuf"],
                 w=[("res", f)])
        s3st = [ar.alloc(512, F32) for _ in range(2)]
        s3bt = [ar.alloc(512, BF16) for _ in range(2)]

        def dst3(f, c, nrm, key):
            i2 = (f * 4 + c) % 2
            st = s3st[i2]
            bt = s3bt[i2]
            p.op("scalar", lambda e, o=st, i=nrm: e.copy(out=o, in_=i),
                 r=[key, ("s3st", i2)], w=[("s3st", i2)])
            p.op("vector", lambda e, o=bt, i=st: e.tensor_copy(out=o, in_=i),
                 r=[("s3st", i2), ("s3bt", i2)], w=[("s3bt", i2)])
            p.dma("gpsimd",
                  self.stream3.ap()[f * 128:(f + 1) * 128,
                                    c * 512:(c + 1) * 512],
                  st, ch=("s1", i2), r=[("s3st", i2)],
                  w=[("s3st", i2), ("str3", f, c)])
            p.dma("gpsimd",
                  self.x3g.ap()[f * 128:(f + 1) * 128,
                                c * 512:(c + 1) * 512],
                  bt, ch=("s2b", i2), r=[("s3bt", i2)],
                  w=[("s3bt", i2), ("x3g", f, c)])

        self.layernorm(lambda f: (res[f], ("res", f)), dst3)
        p.fence()
        ar.release()

    def build_moe(self):
        p, ar, pp = self.p, self.ar, self.pp
        wrT = self.ein("wrT", [DM, 8])
        rbp = self.ein("rb", [8, 1])
        outT = self.eout("outT", [DM, T])
        NI = MOE_I // 128
        sel7c = np.zeros((7, 7 * 128), dtype=np.float32)
        for e2 in range(7):
            sel7c[e2, e2 * 128:(e2 + 1) * 128] = 1.0
        sel7t = self.nc.inline_tensor(sel7c, "sel7")
        ar.mark()
        x3 = [ar.alloc(T, BF16) for _ in range(NFT)]
        for f in range(NFT):
            p.dma("gpsimd", x3[f],
                  self.x3g.ap()[f * 128:(f + 1) * 128, :],
                  ch=("x", f % 4), r=[("x3g", f, c) for c in range(4)],
                  w=[("x3", f)])
        wT = ar.alloc(T, F32, parts=7)
        sel7 = ar.alloc(7 * 128, F32, parts=7)
        p.dma("sync", sel7, sel7t.ap(), ch="c0", w=["sel7"])
        rb_sb = ar.alloc(1, F32, parts=8)
        p.dma("sync", rb_sb, rbp.ap(), ch="c1", w=["rb"])
        ar.mark()
        wrs = ar.alloc(8, F32)
        wrb = ar.alloc(64, BF16)
        for k in range(NFT):
            p.dma("sync", wrs, wrT.ap()[k * 128:(k + 1) * 128, :], ch="c1",
                  r=["wrs"], w=["wrs"])
            p.op("scalar", lambda e, a=wrb[:, k * 8:(k + 1) * 8], b=wrs:
                 e.copy(out=a, in_=b), r=["wrs", "wrb"], w=["wrb"])
        lg = ar.alloc(T, F32, parts=7)
        for tc in range(4):
            bi, bank, bkey = pp.get()
            pp.hold(bi)
            for k in range(NFT):
                p.op("tensor", lambda e, o=bank[0:7, :],
                     w=wrb[:, k * 8:k * 8 + 7],
                     rr=x3[k][:, tc * 512:(tc + 1) * 512],
                     s=(k == 0), t=(k == 7):
                     e.matmul(o, w, rr, start=s, stop=t),
                     r=["wrb", ("x3", k)] + ([bkey] if k else []), w=[bkey])
            p.op("scalar", lambda e, o=lg[0:7, tc * 512:(tc + 1) * 512],
                 i=bank[0:7, :], b=rb_sb[0:7, 0:1]:
                 e.activation(out=o, in_=i, func=ACT.Identity, bias=b),
                 r=[bkey, "rb", "lg"], w=["lg"])
            pp.free(bi)
        gt = ar.alloc(NST * 8, F32)
        wtok = ar.alloc(NST * 8, F32)
        tv = [ar.alloc(8, F32) for _ in range(6)]
        for j in range(NST):
            bi, bank, bkey = pp.get()
            pp.hold(bi)
            p.op("tensor", lambda e, o=bank[:, 0:7],
                 i=lg[0:7, j * 128:(j + 1) * 128], d=self.id_f32[0:7, 0:7]:
                 e.transpose(o, i, d), r=["lg", "idf32", bkey], w=[bkey])
            g = gt[:, j * 8:j * 8 + 7]
            p.op("scalar", lambda e, o=g, i=bank[:, 0:7]: e.copy(out=o, in_=i),
                 r=[bkey, "gt"], w=["gt"])
            pp.free(bi)
            mx, nm, sm, m1, m2, den = tv
            p.op("vector", lambda e, o=mx[:, 0:1], a=g: e.tensor_reduce(
                out=o, in_=a, axis=mybir.AxisListType.X, op=ALU.max),
                r=["gt", "tv"], w=["tv"])
            p.op("vector", lambda e, o=nm[:, 0:1], a=mx[:, 0:1]:
                 e.tensor_scalar_mul(out=o, in0=a, scalar1=-1.0),
                 r=["tv"], w=["tv"])
            p.op("scalar", lambda e, o=g, i=g, b=nm[:, 0:1]: e.activation(
                out=o, in_=i, func=ACT.Exp, bias=b),
                r=["gt", "tv"], w=["gt"])
            p.op("vector", lambda e, o=sm[:, 0:1], a=g: e.tensor_reduce(
                out=o, in_=a, axis=mybir.AxisListType.X, op=ALU.add),
                r=["gt", "tv"], w=["tv"])
            p.op("vector", lambda e, o=sm[:, 0:1], i=sm[:, 0:1]:
                 e.reciprocal(out=o, in_=i), r=["tv"], w=["tv"])
            p.op("vector", lambda e, o=g, a=g, s=sm[:, 0:1]:
                 e.tensor_scalar_mul(out=o, in0=a, scalar1=s),
                 r=["gt", "tv"], w=["gt"])
            p.op("vector", lambda e, o=m1[:, 0:1], a=g: e.tensor_reduce(
                out=o, in_=a, axis=mybir.AxisListType.X, op=ALU.max),
                r=["gt", "tv"], w=["tv"])
            wv = wtok[:, j * 8:j * 8 + 7]
            p.op("vector", lambda e, o=wv, a=g, s=m1[:, 0:1]:
                 e.tensor_single_scalar(out=o, in_=a, scalar=s, op=ALU.is_ge),
                 r=["gt", "tv", "wt"], w=["wt"])
            p2 = gt[:, j * 8 + 7:j * 8 + 8]
            pp2 = wtok
            p.op("vector", lambda e, o=g, a=g, b=wv, m1v=m1:
                 e.scalar_tensor_tensor(out=o, in0=b, scalar=m1v[:, 0:1],
                                        in1=a, op0=ALU.mult, op1=ALU.subtract),
                 r=["gt", "wt", "tv"], w=["gt"])
            p.op("vector", lambda e, o=g, a=g: e.tensor_scalar_mul(
                out=o, in0=a, scalar1=-1.0), r=["gt"], w=["gt"])
            p.op("vector", lambda e, o=m2[:, 0:1], a=g: e.tensor_reduce(
                out=o, in_=a, axis=mybir.AxisListType.X, op=ALU.max),
                r=["gt", "tv"], w=["tv"])
            p.op("vector", lambda e, o=den[:, 0:1], a=m1[:, 0:1],
                 b=m2[:, 0:1]: e.tensor_add(o, a, b), r=["tv"], w=["tv"])
            p.op("vector", lambda e, o=den[:, 0:1], i=den[:, 0:1]:
                 e.reciprocal(out=o, in_=i), r=["tv"], w=["tv"])
            p.op("vector", lambda e, o=g, a=g, s=m2[:, 0:1]:
                 e.tensor_single_scalar(out=o, in_=a, scalar=s, op=ALU.is_ge),
                 r=["gt", "tv"], w=["gt"])
            p.op("vector", lambda e, o=wv, a=wv, s=m1[:, 0:1]:
                 e.tensor_scalar_mul(out=o, in0=a, scalar1=s),
                 r=["wt", "tv"], w=["wt"])
            p.op("vector", lambda e, o=wv, a=g, s=m2[:, 0:1], b=wv:
                 e.scalar_tensor_tensor(out=o, in0=a, scalar=s, in1=b,
                                        op0=ALU.mult, op1=ALU.add),
                 r=["gt", "tv", "wt"], w=["wt"])
            p.op("vector", lambda e, o=wv, a=wv, s=den[:, 0:1]:
                 e.tensor_scalar_mul(out=o, in0=a, scalar1=s),
                 r=["wt", "tv"], w=["wt"])
            bi, bank, bkey = pp.get()
            pp.hold(bi)
            p.op("tensor", lambda e, o=bank[0:7, 0:128], i=wv, d=self.id_f32:
                 e.transpose(o, i, d), r=["wt", "idf32", bkey], w=[bkey])
            p.op("scalar", lambda e, o=wT[0:7, j * 128:(j + 1) * 128],
                 i=bank[0:7, 0:128]: e.copy(out=o, in_=i),
                 r=[bkey, "wTf"], w=["wTf"])
            pp.free(bi)
        p.fence()
        ar.release()
        acc = [ar.alloc(T, BF16) for _ in range(NFT)]
        ar.mark()
        h1 = [ar.alloc(T, BF16) for _ in range(NI)]
        gs = ar.alloc(T, BF16)
        usb = ar.alloc(T, BF16)
        wrep = ar.alloc(T, BF16)
        for f in range(NFT):
            p.op("vector", lambda e, a=acc[f]: e.memset(a, 0.0),
                 w=[("acc", f)])
        for ei in range(8):
            routed = ei > 0
            e_ = ei - 1
            if routed:
                gT = self.moe_bf[f"rg{e_}"].ap()
                uT = self.moe_bf[f"ru{e_}"].ap()
                dT = self.moe_bf[f"rd{e_}"].ap()
                gnm, unm, dnm = f"rg{e_}", f"ru{e_}", f"rd{e_}"
            else:
                gT = self.moe_bf["sg"].ap()
                uT = self.moe_bf["su"].ap()
                dT = self.moe_bf["sd"].ap()
                gnm, unm, dnm = "sg", "su", "sd"
            if routed:
                for tc in range(4):
                    bi, bank, bkey = pp.get()
                    pp.hold(bi)
                    p.op("tensor", lambda e, o=bank[:, :],
                         w=sel7[0:7, e_ * 128:(e_ + 1) * 128],
                         rr=wT[0:7, tc * 512:(tc + 1) * 512]:
                         e.matmul(o, w, rr, start=True, stop=True),
                         r=["sel7", "wTf"], w=[bkey])
                    p.op("scalar", lambda e,
                         o=wrep[:, tc * 512:(tc + 1) * 512], i=bank[:, :]:
                         e.copy(out=o, in_=i), r=[bkey, "wrep"], w=["wrep"])
                    pp.free(bi)
            gaps = [gT[k * 128:(k + 1) * 128, m * 128:(m + 1) * 128]
                    for m in range(NI) for k in range(NFT)]
            uaps = [uT[k * 128:(k + 1) * 128, m * 128:(m + 1) * 128]
                    for m in range(NI) for k in range(NFT)]
            gkeys = [[(gnm, "pcw", k, m)]
                     for m in range(NI) for k in range(NFT)]
            ukeys = [[(unm, "pcw", k, m)]
                     for m in range(NI) for k in range(NFT)]
            gws = self.get_stream("moeg", gaps, dtype=BF16, src_keys=gkeys,
                                  engs=("sync", "gpsimd"))
            uws = self.get_stream("moeu", uaps, dtype=BF16, src_keys=ukeys,
                                  engs=("gpsimd", "sync"))
            for m in range(NI):
                banks = []
                for tc in range(4):
                    bi, bank, bkey = pp.get()
                    pp.hold(bi)
                    banks.append((bi, bank, bkey))
                for k in range(NFT):
                    wap, wkey = gws.get(m * NFT + k)
                    for tc in range(4):
                        _, bank, bkey = banks[tc]
                        p.op("tensor", lambda e, o=bank[:, :], w=wap,
                             rr=x3[k][:, tc * 512:(tc + 1) * 512],
                             s=(k == 0), t=(k == 7):
                             e.matmul(o, w, rr, start=s, stop=t),
                             r=[wkey, ("x3", k)] + ([bkey] if k else []),
                             w=[bkey])
                for tc in range(4):
                    _, bank, bkey = banks[tc]
                    p.op("scalar", lambda e, o=gs[:, tc * 512:(tc + 1) * 512],
                         i=bank[:, :]: e.activation(out=o, in_=i,
                                                    func=ACT.Silu),
                         r=[bkey, "gs"], w=["gs"])
                for k in range(NFT):
                    wap, wkey = uws.get(m * NFT + k)
                    for tc in range(4):
                        _, bank, bkey = banks[tc]
                        p.op("tensor", lambda e, o=bank[:, :], w=wap,
                             rr=x3[k][:, tc * 512:(tc + 1) * 512],
                             s=(k == 0), t=(k == 7):
                             e.matmul(o, w, rr, start=s, stop=t),
                             r=[wkey, ("x3", k), "gs"] + ([bkey] if k else []),
                             w=[bkey])
                for tc in range(4):
                    _, bank, bkey = banks[tc]
                    p.op("scalar", lambda e, o=usb[:, tc * 512:(tc + 1) * 512],
                         i=bank[:, :]: e.copy(out=o, in_=i),
                         r=[bkey, "usb"], w=["usb"])
                p.op("vector", lambda e, o=h1[m], a=gs, b=usb:
                     e.tensor_mul(o, a, b), r=["gs", "usb", ("h1", m)],
                     w=[("h1", m)])
                if routed:
                    p.op("vector", lambda e, o=h1[m], a=h1[m], b=wrep:
                         e.tensor_mul(o, a, b),
                         r=[("h1", m), "wrep"], w=[("h1", m)])
                for tc in range(4):
                    pp.free(banks[tc][0])
            daps = [dT[hk * 128:(hk + 1) * 128, m2 * 128:(m2 + 1) * 128]
                    for m2 in range(NFT) for hk in range(NI)]
            dkeys = [[(dnm, "pcw", hk, m2)]
                     for m2 in range(NFT) for hk in range(NI)]
            dws = self.get_stream("moed", daps, dtype=BF16, src_keys=dkeys,
                                  engs=("sync", "gpsimd"))
            for m2 in range(NFT):
                banks = []
                for tc in range(4):
                    bi, bank, bkey = pp.get()
                    pp.hold(bi)
                    banks.append((bi, bank, bkey))
                for hk in range(NI):
                    wap, wkey = dws.get(m2 * NI + hk)
                    for tc in range(4):
                        _, bank, bkey = banks[tc]
                        p.op("tensor", lambda e, o=bank[:, :], w=wap,
                             rr=h1[hk][:, tc * 512:(tc + 1) * 512],
                             s=(hk == 0), t=(hk == NI - 1):
                             e.matmul(o, w, rr, start=s, stop=t),
                             r=[wkey, ("h1", hk)] + ([bkey] if hk else []),
                             w=[bkey])
                for tc in range(4):
                    bi, bank, bkey = banks[tc]
                    p.op("vector", lambda e,
                         o=acc[m2][:, tc * 512:(tc + 1) * 512],
                         a=acc[m2][:, tc * 512:(tc + 1) * 512], b=bank[:, :]:
                         e.tensor_add(o, a, b),
                         r=[("acc", m2), bkey], w=[("acc", m2)])
                    pp.free(bi)
        p.fence()
        ar.release()
        self.drop_streams("moeg", "moeu", "moed")
        res = [ar.alloc(T, F32) for _ in range(NFT)]
        for f in range(NFT):
            p.dma("gpsimd", res[f],
                  self.stream3.ap()[f * 128:(f + 1) * 128, :],
                  ch=("x", f % 4), r=[("str3", f, c) for c in range(4)],
                  w=[("res", f)])
            p.op("vector", lambda e, o=res[f], a=res[f], b=acc[f]:
                 e.tensor_add(o, a, b), r=[("res", f), ("acc", f)],
                 w=[("res", f)])
        o4st = [ar.alloc(512, F32) for _ in range(2)]
        self.final_regions = []

        def dst4(f, c, nrm, key):
            i2 = (f * 4 + c) % 2
            st = o4st[i2]
            p.op("scalar", lambda e, o=st, i=nrm: e.copy(out=o, in_=i),
                 r=[key, ("o4", i2)], w=[("o4", i2)])
            p.dma("gpsimd",
                  outT.ap()[f * 128:(f + 1) * 128, c * 512:(c + 1) * 512],
                  st, ch=("s1", i2), r=[("o4", i2)],
                  w=[("o4", i2), ("out", f, c)])
            self.final_regions.append(("out", f, c))

        self.layernorm(lambda f: (res[f], ("res", f)), dst4)
        p.fence()
        ar.release()

    def build_all(self):
        self.setup_consts()
        self.precast_setup()
        sgT = self.ein("sgT", [DM, MOE_I])
        suT = self.ein("suT", [DM, MOE_I])
        sdT = self.ein("sdT", [MOE_I, DM])
        rgT = self.ein("rgT", [NEXP, DM, MOE_I])
        ruT = self.ein("ruT", [NEXP, DM, MOE_I])
        rdT = self.ein("rdT", [NEXP, MOE_I, DM])
        self.moe_bf = {}
        self.moe_bf["sg"] = self.precast("sg", sgT.ap(), DM, MOE_I)
        self.moe_bf["su"] = self.precast("su", suT.ap(), DM, MOE_I)
        self.moe_bf["sd"] = self.precast("sd", sdT.ap(), MOE_I, DM)
        for e in range(NEXP):
            self.moe_bf[f"rg{e}"] = self.precast(
                f"rg{e}", rgT.ap()[e], DM, MOE_I)
            self.moe_bf[f"ru{e}"] = self.precast(
                f"ru{e}", ruT.ap()[e], DM, MOE_I)
            self.moe_bf[f"rd{e}"] = self.precast(
                f"rd{e}", rdT.ap()[e], MOE_I, DM)
        self.build_l0()
        self.build_ffn()
        for nm, rows, cols, dt in (("stream2", DM, T, F32),
                                   ("w1q_sp", DM, HID, BF16),
                                   ("w2q_sp", HID, DM, BF16)):
            if nm in self.dump:
                self.ar.mark()
                self.dump_dram(nm, self.intern[nm], rows, cols, dt)
                self.ar.release()
        self.build_attn()
        for nm, rows, cols, dt in (("stream3", DM, T, F32),
                                   ("kqv_sp", 1536, S_FULL, BF16),
                                   ("attn_in", 512, S_FULL, BF16)):
            if nm in self.dump:
                self.ar.mark()
                self.dump_dram(nm, self.intern[nm], rows, cols, dt)
                self.ar.release()
        self.build_moe()
        self.p.op("gpsimd", lambda e: e.engine_nop(),
                  r=list(self.final_regions))


def _t(a):
    return np.ascontiguousarray(np.asarray(a).T).astype(np.float32)


def stage_l0_inputs(inputs):
    x = np.asarray(inputs["L0_attn_x"])
    per_core = []
    for c in range(8):
        b, h = c // 2, c % 2
        xs = x[b, :2048] if h == 0 else x[b, 2048:][::-1]
        m = {"xT": _t(xs)}
        dmap = {"s": "f" if h == 0 else "b", "r": "b" if h == 0 else "f"}
        for d, t in dmap.items():
            m["winT_" + d] = _t(inputs[f"L0_attn_w_in_{t}"])
            m["convw_" + d] = np.ascontiguousarray(
                np.asarray(inputs[f"L0_attn_conv_w_{t}"]).reshape(16, 128, 4)
                .transpose(1, 0, 2).reshape(128, 64)).astype(np.float32)
            m["convb_" + d] = np.ascontiguousarray(
                np.asarray(inputs[f"L0_attn_conv_b_{t}"]).reshape(16, 128).T
            ).astype(np.float32)
            m["xprojT_" + d] = _t(inputs[f"L0_attn_x_proj_{t}"])
            m["dtwT_" + d] = _t(inputs[f"L0_attn_dt_w_{t}"])
            m["dtb_" + d] = np.ascontiguousarray(
                np.asarray(inputs[f"L0_attn_dt_b_{t}"]).reshape(16, 128).T
            ).astype(np.float32)
            m["Alog_" + d] = np.ascontiguousarray(
                np.asarray(inputs[f"L0_attn_A_log_{t}"]).reshape(16, 128, 16)
                .transpose(1, 0, 2).reshape(128, 256)).astype(np.float32)
            m["Dsk_" + d] = np.ascontiguousarray(
                np.asarray(inputs[f"L0_attn_D_{t}"]).reshape(16, 128).T
            ).astype(np.float32)
            m["woutT_" + d] = _t(inputs[f"L0_attn_w_out_{t}"])
        pT = _t(inputs["L0_attn_proj_w"])
        m["projwT_s"] = np.ascontiguousarray(
            pT[:1024] if h == 0 else pT[1024:])
        m["projwT_r"] = np.ascontiguousarray(
            pT[1024:] if h == 0 else pT[:1024])
        m["projb"] = np.ascontiguousarray(
            np.asarray(inputs["L0_attn_proj_b"]).reshape(8, 128).T
        ).astype(np.float32)
        m["ln_g"] = np.ascontiguousarray(
            np.asarray(inputs["L0_attn_ln_g"]).reshape(8, 128).T
        ).astype(np.float32)
        m["ln_b"] = np.ascontiguousarray(
            np.asarray(inputs["L0_attn_ln_b"]).reshape(8, 128).T
        ).astype(np.float32)
        m["nbr0"] = np.full((128, 1), 1.0 if h == 1 else 0.0, np.float32)
        m["nbr1"] = np.full((128, 1), 0.0 if h == 1 else 1.0, np.float32)
        per_core.append(m)
    return per_core

def stage_all_inputs(inputs):
    per_core = stage_l0_inputs(inputs)
    wk = _t(inputs["L1_attn_W_kqv"])
    sgT = _t(inputs["L1_ffn_sg"])
    suT = _t(inputs["L1_ffn_su"])
    sdT = _t(inputs["L1_ffn_sd"])
    rgT = np.ascontiguousarray(
        np.asarray(inputs["L1_ffn_rg"]).transpose(0, 2, 1)).astype(np.float32)
    ruT = np.ascontiguousarray(
        np.asarray(inputs["L1_ffn_ru"]).transpose(0, 2, 1)).astype(np.float32)
    rdT = np.ascontiguousarray(
        np.asarray(inputs["L1_ffn_rd"]).transpose(0, 2, 1)).astype(np.float32)
    wrT = np.zeros((1024, 8), np.float32)
    wrT[:, 0:7] = _t(inputs["L1_ffn_wr"])
    rb = np.zeros((8, 1), np.float32)
    rb[0:7, 0] = np.asarray(inputs["L1_ffn_rb"]).astype(np.float32)
    slopes_all = (1.0 / ((2 ** 8) ** (1.0 / 16)) **
                  (np.arange(1, 17))).astype(np.float32)
    for c in range(8):
        b, h = c // 2, c % 2
        m = per_core[c]
        m["w1T"] = _t(inputs["L0_ffn_w1"])
        m["w2T"] = _t(inputs["L0_ffn_w2"])
        hs = slice(h * 512, (h + 1) * 512)
        m["wkT"] = np.ascontiguousarray(wk[:, 0:1024][:, hs])
        m["wqT"] = np.ascontiguousarray(wk[:, 1024:2048][:, hs])
        m["wvT"] = np.ascontiguousarray(wk[:, 2048:3072][:, hs])
        m["slopes"] = np.tile(slopes_all[h * 8:(h + 1) * 8].reshape(1, 8),
                              (128, 1)).copy()
        m["maskA"] = np.full((128, 1), 1.0 if h == 0 else 0.0, np.float32)
        m["maskB"] = np.full((128, 1), 0.0 if h == 0 else 1.0, np.float32)
        m["wrT"] = wrT
        m["rb"] = rb
        m["sgT"] = sgT
        m["suT"] = suT
        m["sdT"] = sdT
        m["rgT"] = rgT
        m["ruT"] = ruT
        m["rdT"] = rdT
    return per_core


def kernel(**inputs):
    nc = bass.Bass("TRN2", target_bir_lowering=False)
    m = MK(nc)
    m.build_all()
    m.p.emit()
    in_maps = stage_all_inputs(inputs)
    res = run_bass_kernel_spmd(nc, in_maps, core_ids=list(range(8)))
    out = np.empty((4, 4096, 1024), np.float32)
    for c in range(8):
        b, h = c // 2, c % 2
        sl = np.asarray(res.results[c]["outT"]).T
        if h == 1:
            sl = sl[::-1]
        out[b, h * 2048:(h + 1) * 2048] = sl
    return out


# version 03 (final)
# speedup vs baseline: 1.8669x; 1.8669x over previous
import contextlib

import numpy as np

import concourse.bass as bass
import concourse.mybir as mybir
from concourse.bass_utils import run_bass_kernel_spmd

F32 = mybir.dt.float32
BF16 = mybir.dt.bfloat16
I32 = mybir.dt.int32
ALU = mybir.AluOpType
ACT = mybir.ActivationFunctionType

DM = 1024
T = 2048
DI = 2048
NST = 16
RANK = 64
NDB = DI // 128
NFT = DM // 128
HID = 4096
NH = 8
HD = 64
S_FULL = 4096
MOE_I = 2816
NEXP = 7
EPS = 1e-5

GROUPS = [[0, 1], [2, 3], [4, 5], [6, 7]]

ENGS = ("sync", "tensor", "vector", "scalar", "gpsimd")
SEM_OF = {"sync": "pQ", "tensor": "pT", "vector": "pV", "scalar": "pS",
          "gpsimd": "pG"}
UNORDERED = {"pG"}


class Prog:

    def __init__(self, nc):
        self.nc = nc
        self.steps = {e: [] for e in ENGS}
        self.val = {}
        self.know = {e: {} for e in ENGS}
        self.tokknow = {}
        self.regions = {}
        self.fence_pending = {e: None for e in ENGS}

    def _dep_tokens(self, reads, writes):
        deps = []
        for k in reads:
            ent = self.regions.get(k)
            if ent and ent[0] is not None:
                deps.append(ent[0])
        for k in writes:
            ent = self.regions.get(k)
            if ent:
                if ent[0] is not None:
                    deps.append(ent[0])
                deps.extend(ent[1])
        return deps

    def _plan(self, eng, fn, reads, writes, semkey, amt, extra_deps=()):
        deps = self._dep_tokens(reads, writes)
        deps.extend(extra_deps)
        if self.fence_pending[eng] is not None:
            deps.extend(self.fence_pending[eng])
            self.fence_pending[eng] = None
        need = {}
        for (sk, v) in deps:
            if sk in UNORDERED:
                v = self.val.get(sk, 0)
            need[sk] = max(need.get(sk, 0), v)
        kn = self.know[eng]
        waits = []
        for sk, v in sorted(need.items(), key=lambda x: str(x[0])):
            if kn.get(sk, 0) < v:
                waits.append((sk, v))
        for (sk, v) in waits:
            kn[sk] = max(kn.get(sk, 0), v)
            tk = self.tokknow.get((sk, v))
            if tk:
                for k2, v2 in tk.items():
                    if kn.get(k2, 0) < v2:
                        kn[k2] = v2
        self.val[semkey] = self.val.get(semkey, 0) + amt
        tok = (semkey, self.val[semkey])
        snap = dict(kn)
        snap[semkey] = tok[1]
        self.tokknow[tok] = snap
        for k in writes:
            self.regions[k] = [tok, []]
        for k in reads:
            ent = self.regions.setdefault(k, [None, []])
            ent[1].append(tok)
        self.steps[eng].append({"fn": fn, "waits": waits,
                                "inc": (semkey, amt)})
        return tok

    def op(self, eng, fn, r=(), w=()):
        return self._plan(eng, fn, r, w, SEM_OF[eng], 1)

    def _chan_deps(self, eng, semkey):
        q = self.val.get(semkey, 0)
        if q > 0 and self.know[eng].get(semkey, 0) < q:
            return [(semkey, q)]
        return []

    NPOOL = 80

    def dma(self, eng, out_ap, in_ap, ch="misc", r=(), w=(), **kw):
        def fn(e, out_ap=out_ap, in_ap=in_ap, kw=kw):
            return e.dma_start(out=out_ap, in_=in_ap, **kw)
        if not hasattr(self, "_chpool"):
            self._chpool = {}
        slot = self._chpool.setdefault(ch, len(self._chpool) % self.NPOOL)
        semkey = ("D", slot)
        extra = self._chan_deps(eng, semkey)
        return self._plan(eng, fn, r, w, semkey, 16, extra_deps=extra)

    def coll(self, kind, op, ins, outs, r=(), w=()):
        def fn(e, kind=kind, op=op, ins=ins, outs=outs):
            return e.collective_compute(kind, op, replica_groups=GROUPS,
                                        ins=ins, outs=outs)
        extra = self._chan_deps("gpsimd", "cc")
        return self._plan("gpsimd", fn, r, w, "cc", 1, extra_deps=extra)

    def fence(self):
        snap = [(sk, v) for sk, v in self.val.items()]
        for e in ENGS:
            prior = self.fence_pending[e]
            self.fence_pending[e] = (prior or []) + snap
        self.regions = {}

    def emit(self):
        nc = self.nc
        semkeys = sorted(set(list(self.val.keys())
                             + [sk for e in ENGS for st in self.steps[e]
                                for (sk, _v) in st["waits"]]),
                         key=str)
        with contextlib.ExitStack() as es:
            sems = {sk: es.enter_context(
                nc.semaphore(str(sk).replace(" ", "")[:62]))
                for sk in semkeys}
            with nc.Block() as block:
                for ename in ENGS:
                    steps = self.steps[ename]
                    if not steps:
                        continue

                    def body(eng, steps=steps, sems=sems):
                        for st in steps:
                            for (sk, v) in st["waits"]:
                                eng.wait_ge(sems[sk], v)
                            inst = st["fn"](eng)
                            sk, amt = st["inc"]
                            inst.then_inc(sems[sk], amt)

                    getattr(block, ename)(body)


class Arena:
    def __init__(self, nc, name, part_bytes):
        self.buf = nc.alloc_sbuf_tensor(name, [128, part_bytes // 4], F32)
        self.part_bytes = part_bytes
        self.off = 0
        self.marks = []
        self.peak = 0

    def alloc(self, cols, dtype, parts=128):
        esz = mybir.dt.size(dtype)
        self.off = (self.off + 7) // 8 * 8
        byte_cols = cols * esz
        assert self.off + byte_cols <= self.part_bytes, (
            f"arena overflow: {self.off} + {byte_cols} > {self.part_bytes}")
        f32_off = self.off // 4
        if dtype == F32:
            ap = self.buf[0:parts, f32_off:f32_off + cols]
        else:
            n_f32 = (byte_cols + 3) // 4
            ap = self.buf[0:parts, f32_off:f32_off + n_f32].bitcast(dtype)
            ap = ap[:, 0:cols]
        self.off += byte_cols
        self.peak = max(self.peak, self.off)
        return ap

    def mark(self):
        self.marks.append(self.off)

    def release(self):
        self.off = self.marks.pop()


class PsumPool:
    def __init__(self, nc):
        self.banks = [nc.alloc_psum_tensor(f"pb{i}", [128, 512], F32)
                      for i in range(8)]
        self.i = 0
        self.held = set()

    def get(self):
        for _ in range(8):
            self.i = (self.i + 1) % 8
            if self.i not in self.held:
                return self.i, self.banks[self.i], f"ps{self.i}"
        raise RuntimeError("no free psum banks")

    def hold(self, idx):
        self.held.add(idx)

    def free(self, idx):
        self.held.discard(idx)


class WStream:

    def __init__(self, p, ar, name, aps, depth=6, cast=True,
                 cast_eng="scalar", dtype=F32, buf_from=None, src_keys=None):
        self.p = p
        self.name = name
        self.aps = aps
        self.src_keys = src_keys or [[] for _ in aps]
        self.cast = cast and dtype == F32
        self.cast_eng = cast_eng
        if buf_from is not None:
            assert buf_from.name == name
            self.depth = buf_from.depth
            self.src = buf_from.src
            self.b16 = buf_from.b16
        else:
            self.depth = depth
            maxcols = max(a.shape[-1] for a in aps)
            self.src = [ar.alloc(maxcols, dtype) for _ in range(depth)]
            self.b16 = ([ar.alloc(maxcols, BF16) for _ in range(depth)]
                        if self.cast else None)
        self.next = 0

    def _issue(self, i):
        ap = self.aps[i]
        s = i % self.depth
        pp, cc = ap.shape[0], ap.shape[-1]
        self.p.dma("sync", self.src[s][0:pp, 0:cc], ap,
                   ch=("ws", self.name, s),
                   r=[(self.name, "f", s)] + list(self.src_keys[i]),
                   w=[(self.name, "f", s)])
        if self.cast:
            dst = self.b16[s][0:pp, 0:cc]
            src = self.src[s][0:pp, 0:cc]
            if self.cast_eng == "scalar":
                fn = lambda e, a=dst, b=src: e.copy(out=a, in_=b)
            else:
                fn = lambda e, a=dst, b=src: e.tensor_copy(out=a, in_=b)
            self.p.op(self.cast_eng, fn,
                      r=[(self.name, "f", s), (self.name, "b", s)],
                      w=[(self.name, "b", s)])

    def prefetch(self, upto):
        while self.next <= min(upto, len(self.aps) - 1):
            self._issue(self.next)
            self.next += 1

    def get(self, i):
        self.prefetch(i + self.depth - 1)
        assert i > self.next - self.depth - 1, (
            f"WStream {self.name}: get({i}) after slot clobbered "
            f"(next={self.next}, depth={self.depth})")
        s = i % self.depth
        ap = self.aps[i]
        pp, cc = ap.shape[0], ap.shape[-1]
        if self.cast:
            return self.b16[s][0:pp, 0:cc], (self.name, "b", s)
        return self.src[s][0:pp, 0:cc], (self.name, "f", s)


class MK:
    def __init__(self, nc, dump=()):
        self.nc = nc
        self.p = Prog(nc)
        self.ar = Arena(nc, "A", 207 * 1024)
        self.pp = PsumPool(nc)
        self.dump = set(dump)
        self.inp = {}
        self.intern = {}
        self.out = {}

    def ein(self, name, shape, dtype=F32):
        t = self.nc.dram_tensor(name, list(shape), dtype, kind="ExternalInput")
        self.inp[name] = t
        return t

    def eout(self, name, shape, dtype=F32):
        t = self.nc.dram_tensor(name, list(shape), dtype,
                                kind="ExternalOutput")
        self.out[name] = t
        return t

    def idram(self, name, shape, dtype=F32):
        t = self.nc.dram_tensor(name, list(shape), dtype)
        self.intern[name] = t
        return t

    def get_stream(self, name, aps, **kw):
        if not hasattr(self, "_wscache"):
            self._wscache = {}
        prev = self._wscache.get(name)
        ws = WStream(self.p, self.ar, name, aps, buf_from=prev, **kw)
        self._wscache[name] = ws
        return ws

    def drop_streams(self, *names):
        for n in names:
            self._wscache.pop(n, None)

    def maybe_dump(self, name, tiles, keys=None, cols=T):
        if name not in self.dump:
            return
        nt = len(tiles)
        d = self.eout("dbg_" + name, [nt * 128, cols], F32)
        cw = min(cols, 512)
        tmp = self.ar.alloc(cw, F32)
        for i in range(nt):
            ap = tiles[i]
            rk = list(keys[i]) if keys else []
            if ap.dtype != F32:
                for c0 in range(0, cols, cw):
                    self.p.op("vector",
                              lambda e, a=tmp, b=ap[:, c0:c0 + cw]:
                              e.tensor_copy(out=a, in_=b),
                              r=rk + [("dmpt", name)], w=[("dmpt", name)])
                    self.p.dma("gpsimd",
                               d.ap()[i * 128:(i + 1) * 128, c0:c0 + cw],
                               tmp, ch="dbg1", r=[("dmpt", name)],
                               w=[("dmpt", name)])
            else:
                self.p.dma("gpsimd", d.ap()[i * 128:(i + 1) * 128, 0:cols],
                           ap, ch="dbg1", r=rk + ["dbg1pace"],
                           w=["dbg1pace"])

    def dump_dram(self, name, dram, rows, cols, dtype=F32):
        if name not in self.dump:
            return
        self.p.fence()
        d = self.eout("dbg_" + name, [rows, cols], F32)
        self.ar.mark()
        stage = self.ar.alloc(cols, dtype)
        st2 = self.ar.alloc(cols, F32)
        for i in range(rows // 128):
            self.p.dma("gpsimd", stage,
                       dram.ap()[i * 128:(i + 1) * 128, 0:cols],
                       ch="dbg0", r=[("dds", name)], w=[("dds", name)])
            self.p.op("vector", lambda e, a=st2, b=stage:
                      e.tensor_copy(out=a, in_=b),
                      r=[("dds", name), ("dd2", name)], w=[("dd2", name)])
            self.p.dma("gpsimd", d.ap()[i * 128:(i + 1) * 128, :], st2,
                       ch="dbg1", r=[("dd2", name)], w=[("dd2", name)])
        self.ar.release()
        self.p.fence()

    def setup_consts(self):
        p, ar = self.p, self.ar
        self.ones_f32 = ar.alloc(128, F32)
        p.op("vector", lambda e, a=self.ones_f32: e.memset(a, 1.0), w=["ones"])
        self.c_eps = ar.alloc(1, F32)
        p.op("vector", lambda e, a=self.c_eps: e.memset(a, EPS), w=["c_eps"])
        self.c32 = ar.alloc(1, F32)
        p.op("vector", lambda e, a=self.c32: e.memset(a, 1.0 / 32.0),
             w=["c32"])
        idf = self.nc.inline_tensor(np.eye(128, dtype=np.float32), "id128")
        self.id_f32 = ar.alloc(128, F32)
        self.id_bf16 = ar.alloc(128, BF16)
        p.dma("sync", self.id_f32, idf.ap(), ch="c0", w=["idf32"])
        p.op("scalar", lambda e, a=self.id_bf16, b=self.id_f32:
             e.copy(out=a, in_=b), r=["idf32"], w=["id"])

    def selB(self, n):
        return self.sel[0:96, n * 128:(n + 1) * 128]

    def selC(self, n):
        return self.sel[0:96, (16 + n) * 128:(17 + n) * 128]

    def loadp(self, name, cols, rows=128):
        p, ar = self.p, self.ar
        t = self.ein(name, [rows, cols], F32)
        sb = ar.alloc(cols, F32, parts=rows)
        p.dma("sync", sb, t.ap(), ch="c1", r=["c1pace"], w=[("prm", name), "c1pace"])
        self.prm[name] = sb
        return sb

    def layernorm(self, src_getter, dst_writer):
        p, ar, pp = self.p, self.ar, self.pp
        x2 = ar.alloc(512, F32)
        msb = [ar.alloc(512, F32) for _ in range(4)]
        vsb = [ar.alloc(512, F32) for _ in range(4)]
        tmp = ar.alloc(512, F32)
        banks = []
        for c in range(8):
            bi, bank, bkey = pp.get()
            pp.hold(bi)
            banks.append((bi, bank, bkey))
        for f in range(NFT):
            xap, xkey = src_getter(f)
            for c in range(4):
                p.op("scalar", lambda e, o=x2,
                     i=xap[:, c * 512:(c + 1) * 512]:
                     e.square(out=o, in_=i), r=[xkey, "ln_x2"], w=["ln_x2"])
                _, bank, bkey = banks[c]
                p.op("tensor", lambda e, o=bank[:, :], w=self.ones_f32,
                     rr=xap[:, c * 512:(c + 1) * 512], s=(f == 0),
                     t=(f == NFT - 1): e.matmul(o, w, rr, start=s, stop=t),
                     r=["ones", xkey] + ([bkey] if f else []), w=[bkey])
                _, bank2, bkey2 = banks[4 + c]
                p.op("tensor", lambda e, o=bank2[:, :], w=self.ones_f32,
                     rr=x2, s=(f == 0),
                     t=(f == NFT - 1): e.matmul(o, w, rr, start=s, stop=t),
                     r=["ones", "ln_x2"] + ([bkey2] if f else []), w=[bkey2])
        for c in range(4):
            _, bankA, bkeyA = banks[c]
            _, bankB, bkeyB = banks[4 + c]
            p.op("scalar", lambda e, o=msb[c], i=bankA: e.mul(
                out=o, in_=i[:, :], mul=1.0 / DM), r=[bkeyA], w=[("ln_m", c)])
            p.op("scalar", lambda e, o=vsb[c], i=bankB: e.mul(
                out=o, in_=i[:, :], mul=1.0 / DM), r=[bkeyB], w=[("ln_v", c)])
            p.op("scalar", lambda e, o=tmp, i=msb[c]: e.square(out=o, in_=i),
                 r=[("ln_m", c), "ln_tmp"], w=["ln_tmp"])
            p.op("vector", lambda e, o=vsb[c], a=vsb[c], b=tmp:
                 e.tensor_sub(o, a, b), r=[("ln_v", c), "ln_tmp"],
                 w=[("ln_v", c)])
            p.op("scalar", lambda e, o=vsb[c], i=vsb[c]: e.activation(
                out=o, in_=i, func=ACT.Sqrt, bias=self.c_eps),
                r=[("ln_v", c), "c_eps"], w=[("ln_v", c)])
            p.op("vector", lambda e, o=vsb[c], i=vsb[c]:
                 e.reciprocal(out=o, in_=i), r=[("ln_v", c)],
                 w=[("ln_v", c)])
        for c in range(8):
            pp.free(banks[c][0])
        nrm = [ar.alloc(512, F32) for _ in range(2)]
        for f in range(NFT):
            xap, xkey = src_getter(f)
            for c in range(4):
                nb = nrm[c % 2]
                nk = ("ln_nrm", c % 2)
                p.op("vector", lambda e, o=nb,
                     a=xap[:, c * 512:(c + 1) * 512], b=msb[c]:
                     e.tensor_sub(o, a, b), r=[xkey, ("ln_m", c), nk], w=[nk])
                p.op("vector", lambda e, o=nb, a=nb, b=vsb[c]:
                     e.tensor_mul(o, a, b), r=[nk, ("ln_v", c)], w=[nk])
                dst_writer(f, c, nb, nk)

    def load_params_l0(self):
        self.prm = {}
        p, ar = self.p, self.ar
        selc = np.zeros((96, 32 * 128), dtype=np.float32)
        for n in range(NST):
            selc[64 + n, n * 128:(n + 1) * 128] = 1.0
            selc[80 + n, (16 + n) * 128:(17 + n) * 128] = 1.0
        selt = self.nc.inline_tensor(selc, "selconst")
        self.sel = ar.alloc(32 * 128, BF16, parts=96)
        selstage = ar.alloc(512, F32, parts=96)
        for j in range(8):
            p.dma("sync", selstage,
                  selt.ap()[:, j * 512:(j + 1) * 512],
                  ch="c0", r=["selst"], w=["selst"])
            p.op("scalar", lambda e, a=self.sel[0:96, j * 512:(j + 1) * 512],
                 b=selstage: e.copy(out=a, in_=b),
                 r=["selst", "sel"], w=["sel"])
        for d in ("s", "r"):
            self.loadp("convw_" + d, 64)
            self.loadp("convb_" + d, 16)
            self.loadp("dtb_" + d, 16)
            self.loadp("Dsk_" + d, 16)
            al = self.loadp("Alog_" + d, 256)
            A = ar.alloc(256, F32)
            p.op("scalar", lambda e, a=A, b=al: e.activation(
                out=a, in_=b, func=ACT.Exp), r=[("prm", "Alog_" + d)],
                w=[("prmA0", d)])
            p.op("vector", lambda e, a=A: e.tensor_scalar_mul(
                out=a, in0=a, scalar1=-1.0), r=[("prmA0", d)],
                w=[("prm", "A_" + d)])
            self.prm["A_" + d] = A
        self.loadp("ln_g", 8)
        self.loadp("ln_b", 8)
        self.loadp("projb", 8)
        self.loadp("nbr0", 1)
        self.loadp("nbr1", 1)
        self.dbl = {"s": ar.alloc(T, BF16), "r": ar.alloc(T, BF16)}
        self.xcn = ar.alloc(32, BF16)
        self.stcol = [ar.alloc(NST, F32) for _ in range(NDB)]

    def l0_ln_xn(self):
        p, ar = self.p, self.ar
        self.xT = self.ein("xT", [DM, T])
        self.xcol_in = self.idram("xcol_in", [DM, 3], BF16)
        self.xcol_out = self.idram("xcol_out", [2, DM, 3], BF16)
        xsb = [ar.alloc(T, F32) for _ in range(NFT)]
        for f in range(NFT):
            p.dma("gpsimd", xsb[f], self.xT.ap()[f * 128:(f + 1) * 128, :],
                  ch=("x", f % 4), w=[("x", f)])

        def dst(f, c, nrm, key):
            g = self.prm["ln_g"][:, f:f + 1]
            b = self.prm["ln_b"][:, f:f + 1]
            p.op("scalar", lambda e, o=self.xn[f][:, c * 512:(c + 1) * 512],
                 i=nrm, g=g, b=b: e.activation(out=o, in_=i, func=ACT.Identity,
                                               scale=g, bias=b),
                 r=[key, ("prm", "ln_g"), ("prm", "ln_b"), ("xn", f, c)],
                 w=[("xn", f, c)])

        self.layernorm(lambda f: (xsb[f], ("x", f)), dst)
        for f in range(NFT):
            p.dma("gpsimd", self.xcol_in.ap()[f * 128:(f + 1) * 128, 0:3],
                  self.xn[f][:, T - 3:T], ch=("xci", f % 2),
                  r=[("xn", f, 3)], w=[("xci", f)])
        p.coll("AllGather", ALU.bypass, [self.xcol_in.ap().opt()],
               [self.xcol_out.ap().opt()],
               r=[("xci", f) for f in range(NFT)], w=["xco"])
        xc0 = ar.alloc(32, BF16)
        xc1 = ar.alloc(32, BF16)
        p.op("vector", lambda e, a=xc0: e.memset(a, 0.0), w=["xc0"])
        p.op("vector", lambda e, a=xc1: e.memset(a, 0.0), w=["xc1"])
        for f in range(NFT):
            p.dma("gpsimd", xc0[:, f * 4:f * 4 + 3],
                  self.xcol_out.ap()[0, f * 128:(f + 1) * 128, 0:3],
                  ch="xc0", r=["xco", "xc0"], w=["xc0"])
            p.dma("gpsimd", xc1[:, f * 4:f * 4 + 3],
                  self.xcol_out.ap()[1, f * 128:(f + 1) * 128, 0:3],
                  ch="xc1", r=["xco", "xc1"], w=["xc1"])
        p.op("vector", lambda e, o=self.xcn, a=xc0,
             s=self.prm["nbr0"][:, 0:1]:
             e.tensor_scalar_mul(out=o, in0=a, scalar1=s),
             r=["xc0", ("prm", "nbr0"), "xcn"], w=["xcn"])
        p.op("vector", lambda e, o=self.xcn, a=xc1,
             s=self.prm["nbr1"][:, 0:1], b=self.xcn:
             e.scalar_tensor_tensor(out=o, in0=a, scalar=s, in1=b,
                                    op0=ALU.mult, op1=ALU.add),
             r=["xc1", ("prm", "nbr1"), "xcn"], w=["xcn"])
        self.maybe_dump(
            "xn", self.xn,
            keys=[[("xn", f, c) for c in range(4)] for f in range(NFT)])

    def l0_xz_conv(self, d):
        p, ar, pp = self.p, self.ar, self.pp
        winT = self.ein("winT_" + d, [DM, 2 * DI])
        u_sp = self.idram("u_sp_" + d, [DI, T], BF16)
        z_sp = self.idram("z_sp_" + d, [DI, T], BF16)
        u_sb = self.u_sb
        zstage = self.zstage
        order = [(m, k) for m in range(32) for k in range(8)]
        aps = [winT.ap()[k * 128:(k + 1) * 128, m * 128:(m + 1) * 128]
               for (m, k) in order]
        ws = self.get_stream("win", aps, depth=10)

        def rhs(k, c):
            if d == "s":
                return (self.xn[k][:, c * 512:(c + 1) * 512], ("xn", k, c))
            lo = T - (c + 1) * 512
            return (self.xn[k][:, lo:lo + 512][:, ::-1], ("xn", k, 3 - c))

        for m in range(32):
            nchunk = 5 if (m < NDB and d == "r") else 4
            banks = []
            for c in range(nchunk):
                bi, bank, bkey = pp.get()
                pp.hold(bi)
                banks.append((bi, bank, bkey))
            for k in range(8):
                wap, wkey = ws.get(m * 8 + k)
                for c in range(nchunk):
                    _, bank, bkey = banks[c]
                    cols = 3 if c == 4 else 512
                    if c == 4:
                        rap, rkey = self.xcn[:, k * 4:k * 4 + 3], "xcn"
                    else:
                        rap, rkey = rhs(k, c)
                    p.op("tensor", lambda e, o=bank[:, 0:cols], w=wap, rr=rap,
                         s=(k == 0), t=(k == 7):
                         e.matmul(o, w, rr, start=s, stop=t),
                         r=[wkey, rkey] + ([bkey] if k else []), w=[bkey])
            for c in range(nchunk):
                bi, bank, bkey = banks[c]
                cols = 3 if c == 4 else 512
                if m < NDB:
                    if c == 4:
                        dst = u_sb[m][:, 0:3]
                        wk = ("u", m, 4)
                    else:
                        dst = u_sb[m][:, 3 + c * 512:3 + (c + 1) * 512]
                        wk = ("u", m, c)
                    p.op("scalar", lambda e, o=dst, i=bank[:, 0:cols]:
                         e.copy(out=o, in_=i), r=[bkey, wk], w=[wk])
                else:
                    zs = zstage[c % 2]
                    p.op("scalar", lambda e, o=zs, i=bank[:, 0:512]:
                         e.copy(out=o, in_=i), r=[bkey, ("zst", c % 2)],
                         w=[("zst", c % 2)])
                    p.dma("gpsimd",
                          z_sp.ap()[(m - NDB) * 128:(m - NDB + 1) * 128,
                                    c * 512:(c + 1) * 512],
                          zs, ch=("zsp", c % 2), r=[("zst", c % 2)],
                          w=[("zst", c % 2), ("zspill", d, m - NDB, c)])
                pp.free(bi)
            if m < NDB and d == "s":
                p.op("vector", lambda e, a=u_sb[m][:, 0:3]: e.memset(a, 0.0),
                     r=[("u", m, 4)], w=[("u", m, 4)])
        acc = self.convacc
        cw = self.prm["convw_" + d]
        cb = self.prm["convb_" + d]
        for m in range(NDB):
            a = acc[m % 2]
            ukeys = [("u", m, i) for i in range(5)]
            p.op("vector", lambda e, o=a, i=u_sb[m][:, 0:T],
                 s=cw[:, 4 * m:4 * m + 1]:
                 e.tensor_scalar_mul(out=o, in0=i, scalar1=s),
                 r=ukeys + [("prm", "convw_" + d), ("cacc", m % 2)],
                 w=[("cacc", m % 2)])
            for t in range(1, 4):
                p.op("vector", lambda e, o=a, i=u_sb[m][:, t:t + T],
                     s=cw[:, 4 * m + t:4 * m + t + 1], b=a:
                     e.scalar_tensor_tensor(out=o, in0=i, scalar=s, in1=b,
                                            op0=ALU.mult, op1=ALU.add),
                     r=ukeys + [("prm", "convw_" + d), ("cacc", m % 2)],
                     w=[("cacc", m % 2)])
            p.op("scalar", lambda e, o=u_sb[m][:, 3:3 + T], i=a,
                 b=cb[:, m:m + 1]: e.activation(out=o, in_=i, func=ACT.Silu,
                                                bias=b),
                 r=[("cacc", m % 2), ("prm", "convb_" + d)] + ukeys,
                 w=ukeys + [("usil", m)])
            p.dma("gpsimd", u_sp.ap()[m * 128:(m + 1) * 128, :],
                  u_sb[m][:, 3:3 + T], ch=("usp", m % 2),
                  r=[("usil", m)] + ukeys, w=[("uspill", d, m)])

    def alloc_xproj_bufs(self):
        ar = self.ar
        self.xpb = dict(
            xp_stage=ar.alloc(96, F32),
            xp=[ar.alloc(96, BF16) for _ in range(NDB)],
            dtw_stage=ar.alloc(DI, F32, parts=64),
            dtw=ar.alloc(DI, BF16, parts=64),
            pre=ar.alloc(T, F32),
            t1=ar.alloc(T, F32),
            t2=ar.alloc(T, F32),
            dstg=[ar.alloc(T, BF16) for _ in range(2)],
        )

    def l0_xproj_dt(self, d):
        p, ar, pp = self.p, self.ar, self.pp
        xprojT = self.ein("xprojT_" + d, [DI, 96])
        dtwT = self.ein("dtwT_" + d, [RANK, DI])
        d_sp = self.idram("d_sp_" + d, [DI, T], BF16)
        u_sp = self.intern["u_sp_" + d]
        xb = self.xpb
        xp = xb["xp"]
        for k in range(NDB):
            stage = xb["xp_stage"]
            p.dma("sync", stage, xprojT.ap()[k * 128:(k + 1) * 128, :],
                  ch="c1", r=["xps"], w=["xps"])
            p.op("scalar", lambda e, a=xp[k], b=stage: e.copy(out=a, in_=b),
                 r=["xps", ("xp", k)], w=[("xp", k)])
        dtw_stage = xb["dtw_stage"]
        dtw = xb["dtw"]
        p.dma("sync", dtw_stage[0:RANK, :], dtwT.ap(), ch="c1", r=["dtws"], w=["dtws"])
        p.op("scalar", lambda e, a=dtw[0:RANK, :], b=dtw_stage[0:RANK, :]:
             e.copy(out=a, in_=b), r=["dtws", "dtw"], w=["dtw"])
        dbl = self.dbl[d]
        uaps = [u_sp.ap()[k * 128:(k + 1) * 128, c * 512:(c + 1) * 512]
                for c in range(4) for k in range(NDB)]
        ukeys = [[("uspill", d, k)] for c in range(4) for k in range(NDB)]
        us = self.get_stream("udb", uaps, depth=6, dtype=BF16, src_keys=ukeys)
        for c in range(4):
            bi, bank, bkey = pp.get()
            pp.hold(bi)
            for k in range(NDB):
                uap, ukey = us.get(c * NDB + k)
                p.op("tensor", lambda e, o=bank[0:96, :], w=xp[k], rr=uap,
                     s=(k == 0), t=(k == NDB - 1):
                     e.matmul(o, w, rr, start=s, stop=t),
                     r=[("xp", k), ukey, ("uspill", d, k)]
                     + ([bkey] if k else []), w=[bkey])
            p.op("scalar", lambda e, o=dbl[0:96, c * 512:(c + 1) * 512],
                 i=bank[0:96, :]: e.copy(out=o, in_=i),
                 r=[bkey, ("dbl", d, c)], w=[("dbl", d, c)])
            pp.free(bi)
        pre = xb["pre"]
        t1 = xb["t1"]
        t2 = xb["t2"]
        dstg = xb["dstg"]
        dtb = self.prm["dtb_" + d]
        for m in range(NDB):
            for c in range(4):
                bi, bank, bkey = pp.get()
                pp.hold(bi)
                p.op("tensor", lambda e, o=bank[:, :],
                     w=dtw[0:RANK, m * 128:(m + 1) * 128],
                     rr=dbl[0:RANK, c * 512:(c + 1) * 512]:
                     e.matmul(o, w, rr, start=True, stop=True),
                     r=[("dtw", d), ("dbl", d, c)], w=[bkey])
                p.op("scalar", lambda e, o=pre[:, c * 512:(c + 1) * 512],
                     i=bank[:, :], b=dtb[:, m:m + 1]:
                     e.activation(out=o, in_=i, func=ACT.Identity, bias=b),
                     r=[bkey, ("prm", "dtb_" + d), "dpre"], w=["dpre"])
                pp.free(bi)
            p.op("scalar", lambda e, o=t1, i=pre: e.activation(
                out=o, in_=i, func=ACT.Abs), r=["dpre", "dt1"], w=["dt1"])
            p.op("scalar", lambda e, o=t1, i=t1: e.activation(
                out=o, in_=i, func=ACT.Exp, scale=-1.0), r=["dt1"], w=["dt1"])
            p.op("scalar", lambda e, o=t1, i=t1: e.activation(
                out=o, in_=i, func=ACT.Ln, bias=1.0), r=["dt1"], w=["dt1"])
            p.op("scalar", lambda e, o=t2, i=pre: e.activation(
                out=o, in_=i, func=ACT.Relu), r=["dpre", "dt2"], w=["dt2"])
            ds_ = dstg[m % 2]
            p.op("vector", lambda e, o=ds_, a=t1, b=t2: e.tensor_add(o, a, b),
                 r=["dt1", "dt2", ("dstg", m % 2)], w=[("dstg", m % 2)])
            p.dma("gpsimd", d_sp.ap()[m * 128:(m + 1) * 128, :], ds_,
                  ch=("dsp", m % 2), r=[("dstg", m % 2)],
                  w=[("dstg", m % 2), ("dspill", d, m)])

    def alloc_scan_bufs(self):
        ar = self.ar
        self.scb = dict(
            Brep=[ar.alloc(T, BF16) for _ in range(NST)],
            Crep=[ar.alloc(T, BF16) for _ in range(NST)],
            dec=ar.alloc(T, F32),
            bb=ar.alloc(T, BF16),
            hh=ar.alloc(T, BF16),
            gg=ar.alloc(T, BF16),
            du=ar.alloc(T, BF16),
            zbuf=ar.alloc(512, BF16),
            zsil=ar.alloc(512, BF16),
            ytmp=ar.alloc(512, BF16),
            yst=ar.alloc(512, BF16),
            sin=[ar.alloc(NST, F32) for _ in range(NDB)],
            s0=ar.alloc(NST, F32),
            s1=ar.alloc(NST, F32),
            ost=ar.alloc(512, BF16),
        )

    def l0_scan(self, d):
        p, ar, pp = self.p, self.ar, self.pp
        u_sp = self.intern["u_sp_" + d]
        z_sp = self.intern["z_sp_" + d]
        d_sp = self.intern["d_sp_" + d]
        y_sp = self.idram("y_sp_" + d, [DI, T], BF16)
        dbl = self.dbl[d]
        A = self.prm["A_" + d]
        Dsk = self.prm["Dsk_" + d]
        sc = self.scb
        Brep, Crep = sc["Brep"], sc["Crep"]
        for n in range(NST):
            for (sel, rep, nm) in ((self.selB(n), Brep[n], "B"),
                                   (self.selC(n), Crep[n], "C")):
                for c in range(4):
                    bi, bank, bkey = pp.get()
                    pp.hold(bi)
                    p.op("tensor", lambda e, o=bank[:, :], w=sel,
                         rr=dbl[0:96, c * 512:(c + 1) * 512]:
                         e.matmul(o, w, rr, start=True, stop=True),
                         r=["sel", ("dbl", d, c)], w=[bkey])
                    eng = "scalar" if (n % 2 == 0) else "vector"
                    dst = rep[:, c * 512:(c + 1) * 512]
                    fn = ((lambda e, o=dst, i=bank: e.copy(out=o, in_=i[:, :]))
                          if eng == "scalar" else
                          (lambda e, o=dst, i=bank:
                           e.tensor_copy(out=o, in_=i[:, :])))
                    p.op(eng, fn, r=[bkey, (nm + "rep", n, c)],
                         w=[(nm + "rep", n, c)])
                    pp.free(bi)
        init = None
        if d == "r":
            init = sc["sin"]
            s0, s1 = sc["s0"], sc["s1"]
            for db in range(NDB):
                st = init[db]
                p.dma("gpsimd", s0,
                      self.st_out.ap()[0, db * 128:(db + 1) * 128, :],
                      ch="st0", r=["sto", "sin0"], w=["sin0"])
                p.dma("gpsimd", s1,
                      self.st_out.ap()[1, db * 128:(db + 1) * 128, :],
                      ch="st1", r=["sto", "sin1"], w=["sin1"])
                p.op("vector", lambda e, o=st, a=s0,
                     s=self.prm["nbr0"][:, 0:1]: e.tensor_scalar_mul(
                         out=o, in0=a, scalar1=s),
                     r=["sin0", ("prm", "nbr0"), ("sin", db)],
                     w=[("sin", db)])
                p.op("vector", lambda e, o=st, a=s1,
                     s=self.prm["nbr1"][:, 0:1], b=st:
                     e.scalar_tensor_tensor(out=o, in0=a, scalar=s, in1=b,
                                            op0=ALU.mult, op1=ALU.add),
                     r=["sin1", ("prm", "nbr1"), ("sin", db)],
                     w=[("sin", db)])
        else:
            self.st_in = self.idram("st_in", [DI, NST])
            self.st_out = self.idram("st_out", [2, DI, NST])
        uaps = [u_sp.ap()[db * 128:(db + 1) * 128, :] for db in range(NDB)]
        daps = [d_sp.ap()[db * 128:(db + 1) * 128, :] for db in range(NDB)]
        us = self.get_stream("usc", uaps, depth=2, dtype=BF16,
                             src_keys=[[("uspill", d, db)] for db in range(NDB)])
        ds = self.get_stream("dsc", daps, depth=2, dtype=BF16,
                             src_keys=[[("dspill", d, db)] for db in range(NDB)])
        dec, bb_, hh, gg, du = (sc["dec"], sc["bb"], sc["hh"], sc["gg"],
                                sc["du"])
        for db in range(NDB):
            uap, ukey = us.get(db)
            dap, dkey = ds.get(db)
            p.op("vector", lambda e, o=du, a=dap, b=uap: e.tensor_mul(o, a, b),
                 r=[ukey, dkey, ("uspill", d, db), ("dspill", d, db), "du"],
                 w=["du"])
            ybanks = []
            for c in range(4):
                bi, bank, bkey = pp.get()
                pp.hold(bi)
                ybanks.append((bi, bank, bkey))
            for n in range(NST):
                p.op("scalar", lambda e, o=dec, i=dap,
                     s=A[:, db * 16 + n:db * 16 + n + 1]:
                     e.activation(out=o, in_=i, func=ACT.Exp, scale=s),
                     r=[dkey, ("dspill", d, db), ("prm", "A_" + d), "dec"],
                     w=["dec"])
                p.op("vector", lambda e, o=bb_, a=du, b=Brep[n]:
                     e.tensor_mul(o, a, b),
                     r=["du", "bb"] + [("Brep", n, c) for c in range(4)],
                     w=["bb"])
                ini = 0.0 if d == "s" else init[db][:, n:n + 1]
                inik = [] if d == "s" else [("sin", db)]
                p.op("vector", lambda e, o=hh, d0=dec, d1=bb_, i=ini:
                     e.tensor_tensor_scan(out=o, data0=d0, data1=d1,
                                          initial=i, op0=ALU.mult,
                                          op1=ALU.add),
                     r=["dec", "bb", "hh"] + inik, w=["hh"])
                p.op("vector", lambda e, o=gg, a=hh, b=Crep[n]:
                     e.tensor_mul(o, a, b),
                     r=["hh", "gg"] + [("Crep", n, c) for c in range(4)],
                     w=["gg"])
                if d == "s":
                    p.op("vector", lambda e, o=self.stcol[db][:, n:n + 1],
                         i=hh[:, T - 1:T]: e.tensor_copy(out=o, in_=i),
                         r=["hh", ("stc", db)], w=[("stc", db)])
                for c in range(4):
                    _, bank, bkey = ybanks[c]
                    p.op("tensor", lambda e, o=bank[:, :], w=self.id_bf16,
                         rr=gg[:, c * 512:(c + 1) * 512], s=(n == 0),
                         t=(n == NST - 1): e.matmul(o, w, rr, start=s, stop=t),
                         r=["id", "gg"] + ([bkey] if n else []), w=[bkey])
            for c in range(4):
                bi, bank, bkey = ybanks[c]
                zb, zs = sc["zbuf"], sc["zsil"]
                yt, ys = sc["ytmp"], sc["yst"]
                p.dma("gpsimd", zb,
                      z_sp.ap()[db * 128:(db + 1) * 128,
                                c * 512:(c + 1) * 512],
                      ch="zld", r=["zb", ("zspill", d, db, c)], w=["zb"])
                p.op("scalar", lambda e, o=zs, i=zb: e.activation(
                    out=o, in_=i, func=ACT.Silu), r=["zb", "zs"], w=["zs"])
                p.op("vector", lambda e, o=yt,
                     i=uap[:, c * 512:(c + 1) * 512],
                     s=Dsk[:, db:db + 1], b=bank[:, :]:
                     e.scalar_tensor_tensor(out=o, in0=i, scalar=s, in1=b,
                                            op0=ALU.mult, op1=ALU.add),
                     r=[ukey, ("prm", "Dsk_" + d), bkey, "yt"], w=["yt"])
                p.op("vector", lambda e, o=ys, a=yt, b=zs:
                     e.tensor_mul(o, a, b), r=["yt", "zs", "yss"], w=["yss"])
                p.dma("gpsimd",
                      y_sp.ap()[db * 128:(db + 1) * 128,
                                c * 512:(c + 1) * 512],
                      ys, ch="ysp", r=["yss"],
                      w=["yss", ("yspill", d, db, c)])
                pp.free(bi)
        if d == "s":
            for db in range(NDB):
                p.dma("gpsimd", self.st_in.ap()[db * 128:(db + 1) * 128, :],
                      self.stcol[db], ch=("sti", db % 2),
                      r=[("stc", db)], w=[("sti", db)])
            p.coll("AllGather", ALU.bypass, [self.st_in.ap().opt()],
                   [self.st_out.ap().opt()],
                   r=[("sti", db) for db in range(NDB)], w=["sto"])

    def l0_wout(self, d):
        p, ar, pp = self.p, self.ar, self.pp
        woutT = self.ein("woutT_" + d, [DI, DM])
        y_sp = self.intern["y_sp_" + d]
        o_sp = self.idram("o_sp_" + d, [DM, T], BF16)
        waps = []
        for c in range(4):
            for k in range(NDB):
                for m in range(8):
                    waps.append(woutT.ap()[k * 128:(k + 1) * 128,
                                           m * 128:(m + 1) * 128])
        ws = self.get_stream("wot", waps, depth=6)
        yaps = []
        for c in range(4):
            for k in range(NDB):
                if d == "s":
                    yaps.append(y_sp.ap()[k * 128:(k + 1) * 128,
                                          c * 512:(c + 1) * 512])
                else:
                    lo = T - (c + 1) * 512
                    yaps.append(y_sp.ap()[k * 128:(k + 1) * 128, lo:lo + 512])
        ykeys = [[("yspill", d, k, (3 - c if d == "r" else c))]
                 for c in range(4) for k in range(NDB)]
        ys = self.get_stream("yot", yaps, depth=3, dtype=BF16, src_keys=ykeys)
        ost = self.scb["ost"]
        for c in range(4):
            banks = []
            for m in range(8):
                bi, bank, bkey = pp.get()
                pp.hold(bi)
                banks.append((bi, bank, bkey))
            for k in range(NDB):
                yap, ykey = ys.get(c * NDB + k)
                cc = 3 - c if d == "r" else c
                ydep = ("yspill", d, k, cc)
                rap = yap if d == "s" else yap[:, ::-1]
                for m in range(8):
                    _, bank, bkey = banks[m]
                    wap, wkey = ws.get((c * NDB + k) * 8 + m)
                    p.op("tensor", lambda e, o=bank[:, :], w=wap, rr=rap,
                         s=(k == 0), t=(k == NDB - 1):
                         e.matmul(o, w, rr, start=s, stop=t),
                         r=[wkey, ykey, ydep] + ([bkey] if k else []),
                         w=[bkey])
            for m in range(8):
                bi, bank, bkey = banks[m]
                p.op("scalar", lambda e, o=ost, i=bank: e.copy(
                    out=o, in_=i[:, :]), r=[bkey, "ost"], w=["ost"])
                p.dma("gpsimd",
                      o_sp.ap()[m * 128:(m + 1) * 128,
                                c * 512:(c + 1) * 512],
                      ost, ch="osp", r=["ost"],
                      w=["ost", ("ospill", d, m, c)])
                pp.free(bi)

    def l0_proj_ln(self):
        p, ar, pp = self.p, self.ar, self.pp
        projs = self.ein("projwT_s", [DM, DM])
        projr = self.ein("projwT_r", [DM, DM])
        self.stream1 = self.idram("stream1", [DM, T])
        pw = {}
        pstage = ar.alloc(DM, F32)
        for d, src in (("s", projs), ("r", projr)):
            sb = ar.alloc(DM * 8, BF16)
            for k in range(8):
                p.dma("sync", pstage, src.ap()[k * 128:(k + 1) * 128, :],
                      ch="c1", r=["pjs"], w=["pjs"])
                eng = "scalar" if k % 2 else "vector"
                dst = sb[:, k * DM:(k + 1) * DM]
                fn = ((lambda e, a=dst, b=pstage: e.copy(out=a, in_=b))
                      if eng == "scalar" else
                      (lambda e, a=dst, b=pstage: e.tensor_copy(out=a, in_=b)))
                p.op(eng, fn, r=["pjs", ("pj", d, k)], w=[("pj", d, k)])
            pw[d] = sb
        oaps = []
        for c in range(4):
            for d in ("s", "r"):
                o_sp = self.intern["o_sp_" + d]
                for k in range(8):
                    oaps.append(o_sp.ap()[k * 128:(k + 1) * 128,
                                          c * 512:(c + 1) * 512])
        okeys = []
        for c in range(4):
            for d in ("s", "r"):
                for k in range(8):
                    okeys.append([("ospill", d, k, c)])
        os = self.get_stream("ostr", oaps, depth=4, dtype=BF16, src_keys=okeys)
        res = [ar.alloc(T, F32) for _ in range(NFT)]
        for f in range(NFT):
            p.dma("gpsimd", res[f], self.xT.ap()[f * 128:(f + 1) * 128, :],
                  ch=("x", f % 4), w=[("res", f)])
        mtmp = ar.alloc(512, F32)
        projb = self.prm["projb"]
        oi = 0
        for c in range(4):
            banks = []
            for m in range(8):
                bi, bank, bkey = pp.get()
                pp.hold(bi)
                banks.append((bi, bank, bkey))
            first = True
            for d in ("s", "r"):
                for k in range(8):
                    oap, okey = os.get(oi)
                    odep = ("ospill", d, k, c)
                    oi += 1
                    for m in range(8):
                        _, bank, bkey = banks[m]
                        p.op("tensor", lambda e, o=bank[:, :],
                             w=pw[d][:, (k * 8 + m) * 128:
                                     (k * 8 + m + 1) * 128],
                             rr=oap, s=first, t=(d == "r" and k == 7):
                             e.matmul(o, w, rr, start=s, stop=t),
                             r=[("pj", d, k), okey, odep]
                             + ([] if first else [bkey]), w=[bkey])
                    first = False
            for m in range(8):
                bi, bank, bkey = banks[m]
                p.op("scalar", lambda e, o=mtmp, i=bank,
                     b=projb[:, m:m + 1]: e.activation(
                         out=o, in_=i[:, :], func=ACT.Identity, bias=b),
                     r=[bkey, ("prm", "projb"), "mtmp"], w=["mtmp"])
                p.op("vector", lambda e, o=res[m][:, c * 512:(c + 1) * 512],
                     i=res[m][:, c * 512:(c + 1) * 512], b=mtmp:
                     e.scalar_tensor_tensor(out=o, in0=i, scalar=2.0, in1=b,
                                            op0=ALU.mult, op1=ALU.add),
                     r=[("res", m), "mtmp"], w=[("res", m)])
                pp.free(bi)
        self.maybe_dump("mix2x", res, keys=[[("res", m)] for m in range(NFT)])
        s1st = [ar.alloc(512, F32) for _ in range(2)]

        def dst(f, c, nrm, key):
            st = s1st[(f * 4 + c) % 2]
            sk = ("s1st", (f * 4 + c) % 2)
            p.op("scalar", lambda e, o=st, i=nrm: e.copy(out=o, in_=i),
                 r=[key, sk], w=[sk])
            p.dma("gpsimd",
                  self.stream1.ap()[f * 128:(f + 1) * 128,
                                    c * 512:(c + 1) * 512],
                  st, ch=("s1", (f * 4 + c) % 2), r=[sk],
                  w=[sk, ("str1", f, c)])

        self.layernorm(lambda f: (res[f], ("res", f)), dst)

    def build_l0(self):
        ar = self.ar
        ar.mark()
        self.load_params_l0()
        ar.mark()
        self.xn = [ar.alloc(T, BF16) for _ in range(NFT)]
        ar.mark()
        self.l0_ln_xn()
        self.p.fence()
        ar.release()
        ar.mark()
        self.u_sb = [ar.alloc(3 + T, BF16) for _ in range(NDB)]
        self.zstage = [ar.alloc(512, BF16) for _ in range(2)]
        self.convacc = [ar.alloc(T, BF16) for _ in range(2)]
        self.alloc_xproj_bufs()
        self.l0_xz_conv("s")
        self.l0_xz_conv("r")
        self.l0_xproj_dt("s")
        self.l0_xproj_dt("r")
        self.p.fence()
        ar.release()
        ar.release()
        ar.mark()
        self.drop_streams("win", "udb", "usc", "dsc", "wot", "yot", "ostr")
        self.alloc_scan_bufs()
        self.l0_scan("s")
        self.l0_wout("s")
        self.l0_scan("r")
        self.l0_wout("r")
        self.p.fence()
        ar.release()
        ar.mark()
        self.l0_proj_ln()
        self.p.fence()
        ar.release()
        ar.release()
        for nm, rows, dt in (("u_sp_s", DI, BF16), ("z_sp_s", DI, BF16),
                             ("d_sp_s", DI, BF16), ("y_sp_s", DI, BF16),
                             ("o_sp_s", DM, BF16), ("u_sp_r", DI, BF16),
                             ("d_sp_r", DI, BF16), ("y_sp_r", DI, BF16),
                             ("o_sp_r", DM, BF16), ("stream1", DM, F32)):
            if nm in self.dump:
                self.ar.mark()
                self.dump_dram(nm, self.intern[nm], rows, T, dt)
                self.ar.release()


    def ternarize(self, tag, wT, K, M, wq_dram):
        p, ar, pp = self.p, self.ar, self.pp
        ar.mark()
        acc = ar.alloc(1, F32)
        red = ar.alloc(1, F32)
        p.op("vector", lambda e, a=acc: e.memset(a, 0.0), w=[(tag, "acc")])
        aps1 = [wT.ap()[ki * 128:(ki + 1) * 128, ci * 512:(ci + 1) * 512]
                for ki in range(K // 128) for ci in range(M // 512)]
        ws1 = WStream(p, ar, "tz" + tag, aps1, depth=4, cast=False)
        for i in range(len(aps1)):
            wap, wkey = ws1.get(i)
            p.op("vector", lambda e, o=red, a=wap: e.tensor_reduce(
                out=o, in_=a, axis=mybir.AxisListType.X, op=ALU.add,
                apply_absolute_value=True),
                r=[wkey, (tag, "red")], w=[(tag, "red")])
            p.op("vector", lambda e, o=acc, a=acc, b=red:
                 e.tensor_add(o, a, b),
                 r=[(tag, "acc"), (tag, "red")], w=[(tag, "acc")])
        bi, bank, bkey = pp.get()
        pp.hold(bi)
        p.op("tensor", lambda e, o=bank[:, 0:1], w=self.ones_f32, rr=acc:
             e.matmul(o, w, rr, start=True, stop=True),
             r=["ones", (tag, "acc")], w=[bkey])
        srep = ar.alloc(1, F32)
        rs = ar.alloc(1, F32)
        p.op("scalar", lambda e, o=srep, i=bank[:, 0:1]: e.mul(
            out=o, in_=i, mul=1.0 / (K * M)), r=[bkey], w=[(tag, "s")])
        pp.free(bi)
        p.op("vector", lambda e, o=srep, a=srep: e.tensor_scalar_max(
            out=o, in0=a, scalar1=EPS), r=[(tag, "s")], w=[(tag, "s")])
        p.op("vector", lambda e, o=rs, i=srep: e.reciprocal(out=o, in_=i),
             r=[(tag, "s"), (tag, "rs")], w=[(tag, "rs")])
        g1 = ar.alloc(512, F32)
        g2 = ar.alloc(512, F32)
        wq = ar.alloc(512, BF16)
        ws2 = WStream(p, ar, "tz2" + tag, aps1, depth=4, cast=False)
        for i, (ki, ci) in enumerate([(ki, ci) for ki in range(K // 128)
                                      for ci in range(M // 512)]):
            wap, wkey = ws2.get(i)
            p.op("vector", lambda e, o=g1, a=wap, s=rs:
                 e.tensor_scalar(out=o, in0=a, scalar1=s, scalar2=0.5,
                                 op0=ALU.mult, op1=ALU.is_ge),
                 r=[wkey, (tag, "rs"), (tag, "g1")], w=[(tag, "g1")])
            p.op("vector", lambda e, o=g2, a=wap, s=rs:
                 e.tensor_scalar(out=o, in0=a, scalar1=s, scalar2=-0.5,
                                 op0=ALU.mult, op1=ALU.is_le),
                 r=[wkey, (tag, "rs"), (tag, "g2")], w=[(tag, "g2")])
            p.op("vector", lambda e, o=wq, a=g1, b=g2: e.tensor_sub(o, a, b),
                 r=[(tag, "g1"), (tag, "g2"), (tag, "wq")], w=[(tag, "wq")])
            p.dma("gpsimd",
                  wq_dram.ap()[ki * 128:(ki + 1) * 128,
                               ci * 512:(ci + 1) * 512],
                  wq, ch=(tag, "sp"), r=[(tag, "wq")],
                  w=[(tag, "wq"), (tag, "q", ki, ci)])
        sret = self.srep_keep.pop(0)
        p.op("vector", lambda e, o=sret, a=srep: e.tensor_copy(out=o, in_=a),
             r=[(tag, "s"), (tag, "skeep")], w=[(tag, "skeep")])
        self.ar.release()
        return sret

    def build_ffn(self):
        p, ar, pp = self.p, self.ar, self.pp
        w1T = self.ein("w1T", [DM, HID])
        w2T = self.ein("w2T", [HID, DM])
        w1q_sp = self.idram("w1q_sp", [DM, HID], BF16)
        w2q_sp = self.idram("w2q_sp", [HID, DM], BF16)
        self.stream2 = self.idram("stream2", [DM, T])
        self.xg_in = self.idram("xg_in", [DM, T], BF16)
        ar.mark()
        self.srep_keep = [ar.alloc(1, F32), ar.alloc(1, F32)]
        s_w1 = self.ternarize("t1", w1T, DM, HID, w1q_sp)
        s_w2 = self.ternarize("t2", w2T, HID, DM, w2q_sp)
        s1 = [ar.alloc(T, F32) for _ in range(NFT)]
        for f in range(NFT):
            p.dma("gpsimd", s1[f],
                  self.stream1.ap()[f * 128:(f + 1) * 128, :],
                  ch=("x", f % 4), r=[("str1", f, c) for c in range(4)],
                  w=[("s1", f)])
        xqT = [ar.alloc(T, BF16) for _ in range(NFT)]
        xtok = ar.alloc(DM, F32)
        q1 = ar.alloc(512, F32)
        am = ar.alloc(1, F32)
        sc = ar.alloc(1, F32)
        corr = ar.alloc(NST, F32)
        cmagic = ar.alloc(1, F32)
        cm1536 = ar.alloc(1, F32)
        p.op("vector", lambda e, a=cmagic: e.memset(a, 1536.5), w=["cmagic"])
        p.op("vector", lambda e, a=cm1536: e.memset(a, -1536.0), w=["cm1536"])
        for j in range(NST):
            bi, bank, bkey = pp.get()
            pp.hold(bi)
            bi2, bank2, bkey2 = pp.get()
            pp.hold(bi2)
            for f in range(NFT):
                bk, bb, kk = (bi, bank, bkey) if f < 4 else (bi2, bank2, bkey2)
                p.op("tensor", lambda e, o=bb[:, (f % 4) * 128:(f % 4 + 1) * 128],
                     i=s1[f][:, j * 128:(j + 1) * 128], d=self.id_f32:
                     e.transpose(o, i, d),
                     r=[("s1", f), "idf32", kk], w=[kk])
            for half, (bk, bb, kk) in enumerate(((bi, bank, bkey),
                                                 (bi2, bank2, bkey2))):
                p.op("scalar", lambda e, o=xtok[:, half * 512:(half + 1) * 512],
                     i=bb[:, :]: e.copy(out=o, in_=i),
                     r=[kk, "xtok"], w=["xtok"])
            pp.free(bi)
            pp.free(bi2)
            p.op("vector", lambda e, o=am, a=xtok: e.tensor_reduce(
                out=o, in_=a, axis=mybir.AxisListType.X, op=ALU.max,
                apply_absolute_value=True), r=["xtok", "am"], w=["am"])
            p.op("vector", lambda e, o=am, a=am: e.tensor_scalar_max(
                out=o, in0=a, scalar1=EPS), r=["am"], w=["am"])
            p.op("vector", lambda e, o=corr[:, j:j + 1], a=am,
                 s=s_w1: e.scalar_tensor_tensor(
                     out=o, in0=a, scalar=1.0 / 127.0, in1=s,
                     op0=ALU.mult, op1=ALU.mult),
                 r=["am", ("t1", "skeep"), "corr"], w=["corr"])
            p.op("vector", lambda e, o=sc, i=am: e.reciprocal(out=o, in_=i),
                 r=["am", "sc"], w=["sc"])
            p.op("vector", lambda e, o=sc, a=sc: e.tensor_scalar_mul(
                out=o, in0=a, scalar1=127.0), r=["sc"], w=["sc"])
            for half in range(2):
                p.op("scalar", lambda e, o=q1,
                     i=xtok[:, half * 512:(half + 1) * 512], s=sc, b=cmagic:
                     e.activation(out=o, in_=i, func=ACT.Identity, scale=s,
                                  bias=b),
                     r=["xtok", "sc", "cmagic", "q1"], w=["q1"])
                p.op("vector", lambda e, o=q1.bitcast(I32), a=q1.bitcast(I32):
                     e.tensor_single_scalar(out=o, in_=a, scalar=-8192,
                                            op=ALU.bitwise_and),
                     r=["q1"], w=["q1"])
                bi, bank, bkey = pp.get()
                pp.hold(bi)
                for f4 in range(4):
                    f = half * 4 + f4
                    p.op("tensor", lambda e, o=bank[:, f4 * 128:(f4 + 1) * 128],
                         i=q1[:, f4 * 128:(f4 + 1) * 128], d=self.id_f32:
                         e.transpose(o, i, d), r=["q1", "idf32", bkey],
                         w=[bkey])
                for f4 in range(4):
                    f = half * 4 + f4
                    p.op("scalar", lambda e, o=xqT[f][:, j * 128:(j + 1) * 128],
                         i=bank[:, f4 * 128:(f4 + 1) * 128], b=cm1536:
                         e.activation(out=o, in_=i, func=ACT.Identity,
                                      bias=b),
                         r=[bkey, "cm1536", ("xqT", f)], w=[("xqT", f)])
                pp.free(bi)
        self.maybe_dump("xqT", [xqT[f] for f in range(NFT)],
                        keys=[[("xqT", f)] for f in range(NFT)], cols=T)
        self.maybe_dump("corr", [corr], keys=[["corr"]], cols=NST)
        self.maybe_dump("sw1", [s_w1], keys=[[("t1", "skeep")]], cols=1)
        self.maybe_dump("sw2", [s_w2], keys=[[("t2", "skeep")]], cols=1)
        NG = 4
        w1aps = []
        for g in range(NG):
            for hc in range(8):
                for k in range(8):
                    w1aps.append(w1q_sp.ap()[k * 128:(k + 1) * 128,
                                             hc * 512:(hc + 1) * 512])
        w1s = WStream(p, ar, "w1q", w1aps, depth=3, dtype=BF16,
                      src_keys=[[("t1", "q", k, hc)]
                                for g in range(NG) for hc in range(8)
                                for k in range(8)])
        w2aps = []
        for g in range(NG):
            for dc in range(8):
                for hk in range(32):
                    w2aps.append(w2q_sp.ap()[hk * 128:(hk + 1) * 128,
                                             dc * 128:(dc + 1) * 128])
        w2s = WStream(p, ar, "w2q", w2aps, depth=6, dtype=BF16,
                      src_keys=[[("t2", "q", hk, dc // 4)]
                                for g in range(NG) for dc in range(8)
                                for hk in range(32)])
        hb = [ar.alloc(HID, BF16) for _ in range(4)]
        hq1 = ar.alloc(1024, F32)
        hqT = [ar.alloc(512, BF16) for _ in range(32)]
        am2 = ar.alloc(1, F32)
        sc2 = ar.alloc(1, F32)
        corr2 = ar.alloc(1, F32)
        nbias = ar.alloc(1, F32)
        w1i = 0
        w2i = 0
        for g in range(NG):
            for hc in range(8):
                banks = []
                for mt in range(4):
                    bi, bank, bkey = pp.get()
                    pp.hold(bi)
                    banks.append((bi, bank, bkey))
                for k in range(8):
                    wap, wkey = w1s.get(w1i)
                    w1i += 1
                    for mt in range(4):
                        _, bank, bkey = banks[mt]
                        j = g * 4 + mt
                        p.op("tensor", lambda e, o=bank[:, :],
                             w=xqT[k][:, j * 128:(j + 1) * 128], rr=wap,
                             s=(k == 0), t=(k == 7):
                             e.matmul(o, w, rr, start=s, stop=t),
                             r=[("xqT", k), wkey] + ([bkey] if k else []),
                             w=[bkey])
                for mt in range(4):
                    bi, bank, bkey = banks[mt]
                    j = g * 4 + mt
                    p.op("scalar", lambda e,
                         o=hb[mt][:, hc * 512:(hc + 1) * 512], i=bank[:, :],
                         s=corr[:, j:j + 1]:
                         e.activation(out=o, in_=i, func=ACT.Gelu, scale=s),
                         r=[bkey, "corr", ("hb", mt, hc)],
                         w=[("hb", mt, hc)])
                    pp.free(bi)
            if g == 0:
                self.maybe_dump("hb0", [hb[0], hb[1]],
                                keys=[[("hb", m, hc) for hc in range(8)]
                                      for m in range(2)], cols=HID)
            for mt in range(4):
                hbkeys = [("hb", mt, hc) for hc in range(8)]
                p.op("vector", lambda e, o=am2, a=hb[mt]: e.tensor_reduce(
                    out=o, in_=a, axis=mybir.AxisListType.X, op=ALU.max,
                    apply_absolute_value=True),
                    r=hbkeys + ["am2"], w=["am2"])
                p.op("vector", lambda e, o=am2, a=am2: e.tensor_scalar_max(
                    out=o, in0=a, scalar1=EPS), r=["am2"], w=["am2"])
                p.op("vector", lambda e, o=corr2, a=am2, s=s_w2:
                     e.scalar_tensor_tensor(out=o, in0=a, scalar=1.0 / 127.0,
                                            in1=s, op0=ALU.mult, op1=ALU.mult),
                     r=["am2", ("t2", "skeep"), "corr2"], w=["corr2"])
                p.op("vector", lambda e, o=nbias, a=corr2:
                     e.tensor_scalar_mul(out=o, in0=a, scalar1=-1536.0),
                     r=["corr2", "nbias"], w=["nbias"])
                p.op("vector", lambda e, o=sc2, i=am2: e.reciprocal(
                    out=o, in_=i), r=["am2", "sc2"], w=["sc2"])
                p.op("vector", lambda e, o=sc2, a=sc2: e.tensor_scalar_mul(
                    out=o, in0=a, scalar1=127.0), r=["sc2"], w=["sc2"])
                for qtr in range(4):
                    p.op("scalar", lambda e, o=hq1,
                         i=hb[mt][:, qtr * 1024:(qtr + 1) * 1024], s=sc2,
                         b=cmagic: e.activation(out=o, in_=i,
                                                func=ACT.Identity,
                                                scale=s, bias=b),
                         r=hbkeys + ["sc2", "cmagic", "hq1"], w=["hq1"])
                    p.op("vector", lambda e, o=hq1.bitcast(I32),
                         a=hq1.bitcast(I32):
                         e.tensor_single_scalar(out=o, in_=a, scalar=-8192,
                                                op=ALU.bitwise_and),
                         r=["hq1"], w=["hq1"])
                    p.op("scalar", lambda e, o=hq1, i=hq1, s=corr2, b=nbias:
                         e.activation(out=o, in_=i, func=ACT.Identity,
                                      scale=s, bias=b),
                         r=["hq1", "corr2", "nbias"], w=["hq1"])
                    for hk in range(8):
                        if hk % 4 == 0:
                            bi, bank, bkey = pp.get()
                            pp.hold(bi)
                        p.op("tensor", lambda e,
                             o=bank[:, (hk % 4) * 128:(hk % 4 + 1) * 128],
                             i=hq1[:, hk * 128:(hk + 1) * 128], d=self.id_f32:
                             e.transpose(o, i, d),
                             r=["hq1", "idf32", bkey], w=[bkey])
                        if hk % 4 == 3:
                            hkg = qtr * 8 + hk - 3
                            for q in range(4):
                                p.op("scalar", lambda e,
                                     o=hqT[hkg + q][:, mt * 128:(mt + 1) * 128],
                                     i=bank[:, q * 128:(q + 1) * 128]:
                                     e.copy(out=o, in_=i),
                                     r=[bkey, ("hqT", hkg + q)],
                                     w=[("hqT", hkg + q)])
                            pp.free(bi)
            if g == 0:
                self.maybe_dump("hqT0", hqT,
                                keys=[[("hqT", i)] for i in range(32)],
                                cols=512)
            for dc in range(8):
                bi, bank, bkey = pp.get()
                pp.hold(bi)
                for hk in range(32):
                    wap, wkey = w2s.get(w2i)
                    w2i += 1
                    p.op("tensor", lambda e, o=bank[:, 0:512], w=wap,
                         rr=hqT[hk][:, :], s=(hk == 0), t=(hk == 31):
                         e.matmul(o, w, rr, start=s, stop=t),
                         r=[wkey, ("hqT", hk)] + ([bkey] if hk else []),
                         w=[bkey])
                f = dc
                cs = g * 512
                p.op("vector", lambda e, o=s1[f][:, cs:cs + 512],
                     a=s1[f][:, cs:cs + 512], b=bank[:, 0:512]:
                     e.tensor_add(o, a, b), r=[("s1", f), bkey],
                     w=[("s1", f)])
                pp.free(bi)
        self.maybe_dump("s1post", s1, keys=[[("s1", f)] for f in range(NFT)],
                        cols=T)
        s2st = [ar.alloc(512, F32) for _ in range(2)]
        s2bt = [ar.alloc(512, BF16) for _ in range(2)]

        def dst2(f, c, nrm, key):
            i2 = (f * 4 + c) % 2
            st = s2st[i2]
            bt = s2bt[i2]
            p.op("scalar", lambda e, o=st, i=nrm: e.copy(out=o, in_=i),
                 r=[key, ("s2st", i2)], w=[("s2st", i2)])
            p.op("vector", lambda e, o=bt, i=st: e.tensor_copy(out=o, in_=i),
                 r=[("s2st", i2), ("s2bt", i2)], w=[("s2bt", i2)])
            p.dma("gpsimd",
                  self.stream2.ap()[f * 128:(f + 1) * 128,
                                    c * 512:(c + 1) * 512],
                  st, ch=("s1", i2), r=[("s2st", i2)],
                  w=[("s2st", i2), ("str2", f, c)])
            p.dma("gpsimd",
                  self.xg_in.ap()[f * 128:(f + 1) * 128,
                                  c * 512:(c + 1) * 512],
                  bt, ch=("s2b", i2), r=[("s2bt", i2)],
                  w=[("s2bt", i2), ("xgi", f, c)])

        self.layernorm(lambda f: (s1[f], ("s1", f)), dst2)
        self.p.fence()
        ar.release()

    def build_attn(self):
        p, ar, pp = self.p, self.ar, self.pp
        wkT = self.ein("wkT", [DM, 512])
        wqT = self.ein("wqT", [DM, 512])
        wvT = self.ein("wvT", [DM, 512])
        slopes = self.ein("slopes", [128, 8])
        maskA = self.ein("maskA", [128, 1])
        maskB = self.ein("maskB", [128, 1])
        self.xg = self.idram("xg", [2, DM, T], BF16)
        kqv_sp = self.idram("kqv_sp", [1536, S_FULL], BF16)
        attn_in = self.idram("attn_in", [512, S_FULL], BF16)
        attn_out = self.idram("attn_out", [2, 512, S_FULL], BF16)
        self.stream3 = self.idram("stream3", [DM, T])
        self.x3g = self.idram("x3g", [DM, T], BF16)
        tkcol_np = np.concatenate(
            [np.arange(2048, dtype=np.float32),
             np.arange(4095, 2047, -1, dtype=np.float32)]
        ).reshape(32, 128).T.copy()
        tkcolt = self.nc.inline_tensor(tkcol_np, "truekcol")
        tkc = self.nc.inline_tensor(
            np.concatenate([np.arange(2048, dtype=np.float32),
                            np.arange(4095, 2047, -1, dtype=np.float32)]
                           ).reshape(1, S_FULL), "truek")
        tri = np.tril(np.full((128, 128), -1e30, dtype=np.float32), -1)
        trj = np.triu(np.full((128, 128), -1e30, dtype=np.float32), 1)
        mlo = self.nc.inline_tensor(tri, "masklo")
        mup = self.nc.inline_tensor(trj, "maskup")
        p.coll("AllGather", ALU.bypass, [self.xg_in.ap().opt()],
               [self.xg.ap().opt()],
               r=[("xgi", f, c) for f in range(NFT) for c in range(4)],
               w=["xg"])
        ar.mark()
        xt = [ar.alloc(T, BF16) for _ in range(NFT * 2)]
        for sl in range(2):
            for f in range(NFT):
                p.dma("gpsimd", xt[sl * NFT + f],
                      self.xg.ap()[sl, f * 128:(f + 1) * 128, :],
                      ch=("x", f % 4), r=["xg"], w=[("xt", sl, f)])
        kst = ar.alloc(512, BF16)
        for wi, (wt, base) in enumerate(((wkT, 0), (wqT, 512), (wvT, 1024))):
            aps = [wt.ap()[k * 128:(k + 1) * 128, m * 128:(m + 1) * 128]
                   for m in range(4) for k in range(8)]
            ws = self.get_stream("wkqv", aps)
            for m in range(4):
                banks = []
                for tc8 in range(8):
                    bi, bank, bkey = pp.get()
                    pp.hold(bi)
                    banks.append((bi, bank, bkey))
                for k in range(8):
                    wap, wkey = ws.get(m * 8 + k)
                    for tc8 in range(8):
                        _, bank, bkey = banks[tc8]
                        sl, tc = tc8 // 4, tc8 % 4
                        p.op("tensor", lambda e, o=bank[:, :], w=wap,
                             rr=xt[sl * NFT + k][:, tc * 512:(tc + 1) * 512],
                             s=(k == 0), t=(k == 7):
                             e.matmul(o, w, rr, start=s, stop=t),
                             r=[wkey, ("xt", sl, k)] + ([bkey] if k else []),
                             w=[bkey])
                for tc8 in range(8):
                    bi, bank, bkey = banks[tc8]
                    if wi == 1:
                        p.op("scalar", lambda e, o=kst, i=bank, s=self.c32:
                             e.activation(out=o, in_=i[:, :],
                                          func=ACT.Identity, scale=s),
                             r=[bkey, "kst", "c32"], w=["kst"])
                    else:
                        p.op("scalar", lambda e, o=kst, i=bank:
                             e.copy(out=o, in_=i[:, :]),
                             r=[bkey, "kst"], w=["kst"])
                    p.dma("gpsimd",
                          kqv_sp.ap()[base + m * 128:base + (m + 1) * 128,
                                      tc8 * 512:(tc8 + 1) * 512],
                          kst, ch="kqv", r=["kst"],
                          w=["kst", ("kqv", wi, m, tc8)])
                    pp.free(bi)
        self.drop_streams("wkqv")
        kqvkeys = [("kqv", wi, m, tc8) for wi in range(3) for m in range(4)
                   for tc8 in range(8)]
        p.fence()
        ar.release()
        ar.mark()
        slp = ar.alloc(8, F32)
        p.dma("sync", slp, slopes.ap(), ch="c1", w=["slp"])
        mlo_sb = ar.alloc(128, F32)
        mup_sb = ar.alloc(128, F32)
        tk_sb = ar.alloc(S_FULL, F32, parts=1)
        p.dma("sync", mlo_sb, mlo.ap(), ch="c0", w=["mlo"])
        p.dma("sync", mup_sb, mup.ap(), ch="c0", r=["mlo"], w=["mup"])
        p.dma("sync", tk_sb[0:1, :], tkc.ap(), ch="c0", r=["mup"], w=["tk"])
        tkcol = ar.alloc(32, F32)
        p.dma("sync", tkcol, tkcolt.ap(), ch="c0", r=["tk"], w=["tkcol"])
        kh = ar.alloc(S_FULL, BF16)
        qh = ar.alloc(S_FULL, BF16)
        vv = [ar.alloc(72, BF16) for _ in range(32)]
        vt = ar.alloc(S_FULL, BF16, parts=64)
        r0 = ar.alloc(S_FULL, F32, parts=1)
        r1 = ar.alloc(S_FULL, F32, parts=1)
        r2 = ar.alloc(S_FULL, F32, parts=1)
        rb1 = ar.alloc(S_FULL, BF16, parts=1)
        sq = ar.alloc(S_FULL, F32, parts=64)
        pt = ar.alloc(512, BF16)
        osr = ar.alloc(512, F32)
        ost = ar.alloc(512, BF16)

        onesrow = ar.alloc(S_FULL, BF16, parts=1)
        p.op("vector", lambda e, a=onesrow[0:1, :]: e.memset(a, 1.0),
             w=["onesrow"])

        def qlo_of(kj):
            if kj < 16:
                return kj
            return 16

        acol = ar.alloc(32, F32)
        for h in range(NH):
            p.dma("gpsimd", kh[0:64, :],
                  kqv_sp.ap()[h * 64:(h + 1) * 64, :], ch="kh",
                  r=kqvkeys + ["khb"], w=["khb"])
            p.dma("gpsimd", kh[64:65, :], onesrow[0:1, :], ch="khr1",
                  r=["onesrow", "khb"], w=["khb"])
            p.op("vector", lambda e, o=acol, a=tkcol, s=slp[:, h:h + 1]:
                 e.tensor_scalar_mul(out=o, in0=a, scalar1=s),
                 r=["tkcol", "slp", "acol"], w=["acol"])
            p.dma("gpsimd", qh[0:64, :],
                  kqv_sp.ap()[512 + h * 64:512 + (h + 1) * 64, :], ch="qh",
                  r=kqvkeys + ["qhb"], w=["qhb"])
            p.op("scalar", lambda e, o=sq[0:64, :], i=qh[0:64, :]:
                 e.square(out=o, in_=i), r=["qhb", "sq"], w=["sq"])
            for c in range(8):
                bi, bank, bkey = pp.get()
                pp.hold(bi)
                p.op("tensor", lambda e, o=bank[0:1, :],
                     w=self.ones_f32[0:64, 0:1],
                     rr=sq[0:64, c * 512:(c + 1) * 512]:
                     e.matmul(o, w, rr, start=True, stop=True),
                     r=["ones", "sq"], w=[bkey])
                p.op("scalar", lambda e, o=r0[0:1, c * 512:(c + 1) * 512],
                     i=bank[0:1, :]: e.activation(out=o, in_=i,
                                                  func=ACT.Sqrt),
                     r=[bkey, "r0"], w=["r0"])
                pp.free(bi)
            p.op("scalar", lambda e, o=sq[0:64, :], i=kh[0:64, :]:
                 e.square(out=o, in_=i), r=["khb", "sq"], w=["sq"])
            for c in range(8):
                bi, bank, bkey = pp.get()
                pp.hold(bi)
                p.op("tensor", lambda e, o=bank[0:1, :],
                     w=self.ones_f32[0:64, 0:1],
                     rr=sq[0:64, c * 512:(c + 1) * 512]:
                     e.matmul(o, w, rr, start=True, stop=True),
                     r=["ones", "sq"], w=[bkey])
                p.op("scalar", lambda e, o=r1[0:1, c * 512:(c + 1) * 512],
                     i=bank[0:1, :]: e.copy(out=o, in_=i),
                     r=[bkey, "r1"], w=["r1"])
                pp.free(bi)
            p.op("vector", lambda e, o=r2[0:1, 0:1], a=r1[0:1, :]:
                 e.tensor_reduce(out=o, in_=a, axis=mybir.AxisListType.X,
                                 op=ALU.max), r=["r1", "r2"], w=["r2"])
            p.op("scalar", lambda e, o=r2[0:1, 0:1], i=r2[0:1, 0:1]:
                 e.activation(out=o, in_=i, func=ACT.Sqrt),
                 r=["r2"], w=["r2"])
            p.op("vector", lambda e, o=r0[0:1, :], a=r0[0:1, :],
                 s=r2[0:1, 0:1]: e.tensor_scalar_mul(out=o, in0=a, scalar1=s),
                 r=["r0", "r2"], w=["r0"])
            p.op("vector", lambda e, o=r1[0:1, :], a=tk_sb[0:1, :],
                 s=slp[0:1, h:h + 1], b=r0[0:1, :]:
                 e.scalar_tensor_tensor(out=o, in0=a, scalar=s, in1=b,
                                        op0=ALU.mult, op1=ALU.add),
                 r=["tk", "slp", "r0", "r1"], w=["r1"])
            p.op("vector", lambda e, o=rb1[0:1, :], a=r1[0:1, :]:
                 e.tensor_scalar_mul(out=o, in0=a, scalar1=-1.0),
                 r=["r1", "rb1"], w=["rb1"])
            p.dma("gpsimd", qh[64:65, :], rb1[0:1, :], ch="qhr",
                  r=["rb1", "qhb"], w=["qhb"])
            p.dma("gpsimd", vt[0:64, :],
                  kqv_sp.ap()[1024 + h * 64:1024 + (h + 1) * 64, :], ch="vh",
                  r=kqvkeys + ["vtb"], w=["vtb"])
            for kj in range(32):
                bi, bank, bkey = pp.get()
                pp.hold(bi)
                p.op("vector", lambda e, a=vv[kj][0:128, 64:65]:
                     e.memset(a, 1.0), r=[("vv", kj)], w=[("vv", kj)])
                p.op("tensor", lambda e, o=bank[:, 0:32].bitcast(BF16),
                     i=vt[0:64, kj * 128:(kj + 1) * 128],
                     d=self.id_bf16[0:64, 0:64]:
                     e.transpose(o, i, d), r=["vtb", "id", bkey], w=[bkey])
                p.op("scalar", lambda e, o=vv[kj][0:128, 0:64],
                     i=bank[:, 0:32].bitcast(BF16):
                     e.copy(out=o, in_=i), r=[bkey, ("vv", kj)],
                     w=[("vv", kj)])
                pp.free(bi)
                if h == 1 and kj == 0 and "vvdump" in self.dump:
                    self.maybe_dump("vvdump", [vv[0]], keys=[[("vv", 0)]],
                                    cols=72)
            for qc in range(8):
                q0t = qc * 4
                bo, bank_o, bkey_o = pp.get()
                pp.hold(bo)
                kjs = []
                for kj in range(32):
                    if kj < 16:
                        if q0t + 3 >= kj:
                            kjs.append(kj)
                    else:
                        if q0t + 3 >= 16 and q0t <= kj:
                            kjs.append(kj)
                first = True
                for kj in kjs:
                    bs, bank_s, bkey_s = pp.get()
                    pp.hold(bs)
                    p.op("tensor", lambda e, o=bank_s[:, :],
                         w=kh[0:65, kj * 128:(kj + 1) * 128],
                         rr=qh[0:65, qc * 512:(qc + 1) * 512]:
                         e.matmul(o, w, rr, start=True, stop=True),
                         r=["khb", "qhb"], w=[bkey_s])
                    if h == 1 and qc == 0 and kj == 0 and                             "sdump" in self.dump:
                        sdbuf = self.ar.alloc(512, F32)
                        p.op("scalar", lambda e, o=sdbuf, i=bank_s:
                             e.copy(out=o, in_=i[:, :]),
                             r=[bkey_s, "sdb"], w=["sdb"])
                        self.maybe_dump("sdump", [sdbuf], keys=[["sdb"]],
                                        cols=512)
                        self.maybe_dump("mdump", [mlo_sb], keys=[["mlo"]],
                                        cols=128)
                    for st in range(4):
                        qt = q0t + st
                        sub = bank_s[:, st * 128:(st + 1) * 128]
                        if kj < 16:
                            if qt < 16:
                                if qt < kj:
                                    p.op("vector", lambda e, o=sub, a=sub:
                                         e.tensor_scalar_add(
                                             out=o, in0=a, scalar1=-1e30),
                                         r=[bkey_s], w=[bkey_s])
                                elif qt == kj:
                                    p.op("vector", lambda e, o=sub, a=sub,
                                         b=mlo_sb: e.tensor_add(o, a, b),
                                         r=[bkey_s, "mlo"], w=[bkey_s])
                        else:
                            if qt < 16 or qt > kj:
                                p.op("vector", lambda e, o=sub, a=sub:
                                     e.tensor_scalar_add(
                                         out=o, in0=a, scalar1=-1e30),
                                     r=[bkey_s], w=[bkey_s])
                            elif qt == kj:
                                p.op("vector", lambda e, o=sub, a=sub,
                                     b=mup_sb: e.tensor_add(o, a, b),
                                     r=[bkey_s, "mup"], w=[bkey_s])
                    p.op("scalar", lambda e, o=pt, i=bank_s,
                         b=acol[:, kj:kj + 1]: e.activation(
                             out=o, in_=i[:, :], func=ACT.Exp, bias=b),
                         r=[bkey_s, "pt", "acol"], w=["pt"])
                    if h == 1 and qc == 0 and kj == 0 and                             "ptdump" in self.dump:
                        self.maybe_dump("ptdump", [pt], keys=[["pt"]],
                                        cols=512)
                    pp.free(bs)
                    p.op("tensor", lambda e, o=bank_o[0:65, :],
                         w=vv[kj][0:128, 0:65], rr=pt[:, :],
                         s=first, t=(kj == kjs[-1]):
                         e.matmul(o, w, rr, start=s, stop=t),
                         r=[("vv", kj), "pt"] + ([] if first else [bkey_o]),
                         w=[bkey_o])
                    first = False
                p.op("scalar", lambda e, o=osr[64:65, :], i=bank_o[64:65, :]:
                     e.copy(out=o, in_=i), r=[bkey_o, "osr"], w=["osr"])
                p.dma("gpsimd", r2[0:1, 0:512], osr[64:65, :], ch="snrm",
                      r=["osr", "r2"], w=["r2"])
                p.op("vector", lambda e, o=r2[0:1, 0:512], i=r2[0:1, 0:512]:
                     e.reciprocal(out=o, in_=i), r=["r2"], w=["r2"])
                bb, bank_b, bkey_b = pp.get()
                pp.hold(bb)
                p.op("tensor", lambda e, o=bank_b[0:64, :],
                     w=self.ones_f32[0:1, 0:64], rr=r2[0:1, 0:512]:
                     e.matmul(o, w, rr, start=True, stop=True),
                     r=["ones", "r2"], w=[bkey_b])
                p.op("scalar", lambda e, o=osr[0:64, :], i=bank_b[0:64, :]:
                     e.copy(out=o, in_=i), r=[bkey_b, "osr"],
                     w=["osr"])
                pp.free(bb)
                p.op("vector", lambda e, o=ost[0:64, :], a=bank_o[0:64, :],
                     b=osr[0:64, :]: e.tensor_mul(o, a, b),
                     r=[bkey_o, "osr", "ostA"], w=["ostA"])
                pp.free(bo)
                p.dma("gpsimd",
                      attn_in.ap()[h * 64:(h + 1) * 64,
                                   qc * 512:(qc + 1) * 512],
                      ost[0:64, :], ch="aout", r=["ostA"],
                      w=["ostA", ("ain", h, qc)])
        p.coll("AllGather", ALU.bypass, [attn_in.ap().opt()],
               [attn_out.ap().opt()],
               r=[("ain", h, qc) for h in range(NH) for qc in range(8)],
               w=["aog"])
        p.fence()
        ar.release()
        ar.mark()
        mA = ar.alloc(1, F32)
        mB = ar.alloc(1, F32)
        p.dma("sync", mA, maskA.ap(), ch="c1", w=["mA"])
        p.dma("sync", mB, maskB.ap(), ch="c1", r=["mA"], w=["mB"])
        res = [ar.alloc(T, F32) for _ in range(NFT)]
        ao = ar.alloc(T, BF16)
        at = ar.alloc(T, BF16)
        for f in range(NFT):
            p.dma("gpsimd", res[f],
                  self.stream2.ap()[f * 128:(f + 1) * 128, :],
                  ch=("x", f % 4), r=[("str2", f, c) for c in range(4)],
                  w=[("res", f)])
        for f in range(NFT):
            sl = f // 4
            fr = f % 4
            p.dma("gpsimd", ao,
                  attn_out.ap()[sl, fr * 128:(fr + 1) * 128, 0:T],
                  ch="ao0", r=["aog", "aobuf"], w=["aobuf"])
            p.dma("gpsimd", at,
                  attn_out.ap()[sl, fr * 128:(fr + 1) * 128, T:2 * T],
                  ch="ao1", r=["aog", "atbuf"], w=["atbuf"])
            p.op("vector", lambda e, o=ao, a=ao, s=mA:
                 e.tensor_scalar_mul(out=o, in0=a, scalar1=s),
                 r=["aobuf", "mA"], w=["aobuf"])
            p.op("vector", lambda e, o=ao, a=at, s=mB, b=ao:
                 e.scalar_tensor_tensor(out=o, in0=a, scalar=s, in1=b,
                                        op0=ALU.mult, op1=ALU.add),
                 r=["atbuf", "mB", "aobuf"], w=["aobuf"])
            p.op("vector", lambda e, o=res[f], a=res[f], b=ao:
                 e.tensor_add(o, a, b), r=[("res", f), "aobuf"],
                 w=[("res", f)])
        s3st = [ar.alloc(512, F32) for _ in range(2)]
        s3bt = [ar.alloc(512, BF16) for _ in range(2)]

        def dst3(f, c, nrm, key):
            i2 = (f * 4 + c) % 2
            st = s3st[i2]
            bt = s3bt[i2]
            p.op("scalar", lambda e, o=st, i=nrm: e.copy(out=o, in_=i),
                 r=[key, ("s3st", i2)], w=[("s3st", i2)])
            p.op("vector", lambda e, o=bt, i=st: e.tensor_copy(out=o, in_=i),
                 r=[("s3st", i2), ("s3bt", i2)], w=[("s3bt", i2)])
            p.dma("gpsimd",
                  self.stream3.ap()[f * 128:(f + 1) * 128,
                                    c * 512:(c + 1) * 512],
                  st, ch=("s1", i2), r=[("s3st", i2)],
                  w=[("s3st", i2), ("str3", f, c)])
            p.dma("gpsimd",
                  self.x3g.ap()[f * 128:(f + 1) * 128,
                                c * 512:(c + 1) * 512],
                  bt, ch=("s2b", i2), r=[("s3bt", i2)],
                  w=[("s3bt", i2), ("x3g", f, c)])

        self.layernorm(lambda f: (res[f], ("res", f)), dst3)
        p.fence()
        ar.release()

    def build_moe(self):
        p, ar, pp = self.p, self.ar, self.pp
        wrT = self.ein("wrT", [DM, 8])
        rbp = self.ein("rb", [8, 1])
        sgT = self.ein("sgT", [DM, MOE_I])
        suT = self.ein("suT", [DM, MOE_I])
        sdT = self.ein("sdT", [MOE_I, DM])
        rgT = self.ein("rgT", [NEXP, DM, MOE_I])
        ruT = self.ein("ruT", [NEXP, DM, MOE_I])
        rdT = self.ein("rdT", [NEXP, MOE_I, DM])
        outT = self.eout("outT", [DM, T])
        NI = MOE_I // 128
        sel7c = np.zeros((7, 7 * 128), dtype=np.float32)
        for e2 in range(7):
            sel7c[e2, e2 * 128:(e2 + 1) * 128] = 1.0
        sel7t = self.nc.inline_tensor(sel7c, "sel7")
        ar.mark()
        x3 = [ar.alloc(T, BF16) for _ in range(NFT)]
        for f in range(NFT):
            p.dma("gpsimd", x3[f],
                  self.x3g.ap()[f * 128:(f + 1) * 128, :],
                  ch=("x", f % 4), r=[("x3g", f, c) for c in range(4)],
                  w=[("x3", f)])
        wT = ar.alloc(T, F32, parts=7)
        sel7 = ar.alloc(7 * 128, F32, parts=7)
        p.dma("sync", sel7, sel7t.ap(), ch="c0", w=["sel7"])
        rb_sb = ar.alloc(1, F32, parts=8)
        p.dma("sync", rb_sb, rbp.ap(), ch="c1", w=["rb"])
        ar.mark()
        wrs = ar.alloc(8, F32)
        wrb = ar.alloc(64, BF16)
        for k in range(NFT):
            p.dma("sync", wrs, wrT.ap()[k * 128:(k + 1) * 128, :], ch="c1",
                  r=["wrs"], w=["wrs"])
            p.op("scalar", lambda e, a=wrb[:, k * 8:(k + 1) * 8], b=wrs:
                 e.copy(out=a, in_=b), r=["wrs", "wrb"], w=["wrb"])
        lg = ar.alloc(T, F32, parts=7)
        for tc in range(4):
            bi, bank, bkey = pp.get()
            pp.hold(bi)
            for k in range(NFT):
                p.op("tensor", lambda e, o=bank[0:7, :],
                     w=wrb[:, k * 8:k * 8 + 7],
                     rr=x3[k][:, tc * 512:(tc + 1) * 512],
                     s=(k == 0), t=(k == 7):
                     e.matmul(o, w, rr, start=s, stop=t),
                     r=["wrb", ("x3", k)] + ([bkey] if k else []), w=[bkey])
            p.op("scalar", lambda e, o=lg[0:7, tc * 512:(tc + 1) * 512],
                 i=bank[0:7, :], b=rb_sb[0:7, 0:1]:
                 e.activation(out=o, in_=i, func=ACT.Identity, bias=b),
                 r=[bkey, "rb", "lg"], w=["lg"])
            pp.free(bi)
        gt = ar.alloc(NST * 8, F32)
        wtok = ar.alloc(NST * 8, F32)
        tv = [ar.alloc(8, F32) for _ in range(6)]
        for j in range(NST):
            bi, bank, bkey = pp.get()
            pp.hold(bi)
            p.op("tensor", lambda e, o=bank[:, 0:7],
                 i=lg[0:7, j * 128:(j + 1) * 128], d=self.id_f32[0:7, 0:7]:
                 e.transpose(o, i, d), r=["lg", "idf32", bkey], w=[bkey])
            g = gt[:, j * 8:j * 8 + 7]
            p.op("scalar", lambda e, o=g, i=bank[:, 0:7]: e.copy(out=o, in_=i),
                 r=[bkey, "gt"], w=["gt"])
            pp.free(bi)
            mx, nm, sm, m1, m2, den = tv
            p.op("vector", lambda e, o=mx[:, 0:1], a=g: e.tensor_reduce(
                out=o, in_=a, axis=mybir.AxisListType.X, op=ALU.max),
                r=["gt", "tv"], w=["tv"])
            p.op("vector", lambda e, o=nm[:, 0:1], a=mx[:, 0:1]:
                 e.tensor_scalar_mul(out=o, in0=a, scalar1=-1.0),
                 r=["tv"], w=["tv"])
            p.op("scalar", lambda e, o=g, i=g, b=nm[:, 0:1]: e.activation(
                out=o, in_=i, func=ACT.Exp, bias=b),
                r=["gt", "tv"], w=["gt"])
            p.op("vector", lambda e, o=sm[:, 0:1], a=g: e.tensor_reduce(
                out=o, in_=a, axis=mybir.AxisListType.X, op=ALU.add),
                r=["gt", "tv"], w=["tv"])
            p.op("vector", lambda e, o=sm[:, 0:1], i=sm[:, 0:1]:
                 e.reciprocal(out=o, in_=i), r=["tv"], w=["tv"])
            p.op("vector", lambda e, o=g, a=g, s=sm[:, 0:1]:
                 e.tensor_scalar_mul(out=o, in0=a, scalar1=s),
                 r=["gt", "tv"], w=["gt"])
            p.op("vector", lambda e, o=m1[:, 0:1], a=g: e.tensor_reduce(
                out=o, in_=a, axis=mybir.AxisListType.X, op=ALU.max),
                r=["gt", "tv"], w=["tv"])
            wv = wtok[:, j * 8:j * 8 + 7]
            p.op("vector", lambda e, o=wv, a=g, s=m1[:, 0:1]:
                 e.tensor_single_scalar(out=o, in_=a, scalar=s, op=ALU.is_ge),
                 r=["gt", "tv", "wt"], w=["wt"])
            p2 = gt[:, j * 8 + 7:j * 8 + 8]
            pp2 = wtok
            p.op("vector", lambda e, o=g, a=g, b=wv, m1v=m1:
                 e.scalar_tensor_tensor(out=o, in0=b, scalar=m1v[:, 0:1],
                                        in1=a, op0=ALU.mult, op1=ALU.subtract),
                 r=["gt", "wt", "tv"], w=["gt"])
            p.op("vector", lambda e, o=g, a=g: e.tensor_scalar_mul(
                out=o, in0=a, scalar1=-1.0), r=["gt"], w=["gt"])
            p.op("vector", lambda e, o=m2[:, 0:1], a=g: e.tensor_reduce(
                out=o, in_=a, axis=mybir.AxisListType.X, op=ALU.max),
                r=["gt", "tv"], w=["tv"])
            p.op("vector", lambda e, o=den[:, 0:1], a=m1[:, 0:1],
                 b=m2[:, 0:1]: e.tensor_add(o, a, b), r=["tv"], w=["tv"])
            p.op("vector", lambda e, o=den[:, 0:1], i=den[:, 0:1]:
                 e.reciprocal(out=o, in_=i), r=["tv"], w=["tv"])
            p.op("vector", lambda e, o=g, a=g, s=m2[:, 0:1]:
                 e.tensor_single_scalar(out=o, in_=a, scalar=s, op=ALU.is_ge),
                 r=["gt", "tv"], w=["gt"])
            p.op("vector", lambda e, o=wv, a=wv, s=m1[:, 0:1]:
                 e.tensor_scalar_mul(out=o, in0=a, scalar1=s),
                 r=["wt", "tv"], w=["wt"])
            p.op("vector", lambda e, o=wv, a=g, s=m2[:, 0:1], b=wv:
                 e.scalar_tensor_tensor(out=o, in0=a, scalar=s, in1=b,
                                        op0=ALU.mult, op1=ALU.add),
                 r=["gt", "tv", "wt"], w=["wt"])
            p.op("vector", lambda e, o=wv, a=wv, s=den[:, 0:1]:
                 e.tensor_scalar_mul(out=o, in0=a, scalar1=s),
                 r=["wt", "tv"], w=["wt"])
            bi, bank, bkey = pp.get()
            pp.hold(bi)
            p.op("tensor", lambda e, o=bank[0:7, 0:128], i=wv, d=self.id_f32:
                 e.transpose(o, i, d), r=["wt", "idf32", bkey], w=[bkey])
            p.op("scalar", lambda e, o=wT[0:7, j * 128:(j + 1) * 128],
                 i=bank[0:7, 0:128]: e.copy(out=o, in_=i),
                 r=[bkey, "wTf"], w=["wTf"])
            pp.free(bi)
        p.fence()
        ar.release()
        acc = [ar.alloc(T, BF16) for _ in range(NFT)]
        ar.mark()
        h1 = [ar.alloc(T, BF16) for _ in range(NI)]
        gs = ar.alloc(T, BF16)
        usb = ar.alloc(T, BF16)
        wrep = ar.alloc(T, BF16)
        for f in range(NFT):
            p.op("vector", lambda e, a=acc[f]: e.memset(a, 0.0),
                 w=[("acc", f)])
        for ei in range(8):
            routed = ei > 0
            e_ = ei - 1
            if routed:
                gT = rgT.ap()[e_]
                uT = ruT.ap()[e_]
                dT = rdT.ap()[e_]
            else:
                gT, uT, dT = sgT.ap(), suT.ap(), sdT.ap()
            if routed:
                for tc in range(4):
                    bi, bank, bkey = pp.get()
                    pp.hold(bi)
                    p.op("tensor", lambda e, o=bank[:, :],
                         w=sel7[0:7, e_ * 128:(e_ + 1) * 128],
                         rr=wT[0:7, tc * 512:(tc + 1) * 512]:
                         e.matmul(o, w, rr, start=True, stop=True),
                         r=["sel7", "wTf"], w=[bkey])
                    p.op("scalar", lambda e,
                         o=wrep[:, tc * 512:(tc + 1) * 512], i=bank[:, :]:
                         e.copy(out=o, in_=i), r=[bkey, "wrep"], w=["wrep"])
                    pp.free(bi)
            gaps = [gT[k * 128:(k + 1) * 128, m * 128:(m + 1) * 128]
                    for m in range(NI) for k in range(NFT)]
            uaps = [uT[k * 128:(k + 1) * 128, m * 128:(m + 1) * 128]
                    for m in range(NI) for k in range(NFT)]
            gws = self.get_stream("moeg", gaps)
            uws = self.get_stream("moeu", uaps)
            for m in range(NI):
                banks = []
                for tc in range(4):
                    bi, bank, bkey = pp.get()
                    pp.hold(bi)
                    banks.append((bi, bank, bkey))
                for k in range(NFT):
                    wap, wkey = gws.get(m * NFT + k)
                    for tc in range(4):
                        _, bank, bkey = banks[tc]
                        p.op("tensor", lambda e, o=bank[:, :], w=wap,
                             rr=x3[k][:, tc * 512:(tc + 1) * 512],
                             s=(k == 0), t=(k == 7):
                             e.matmul(o, w, rr, start=s, stop=t),
                             r=[wkey, ("x3", k)] + ([bkey] if k else []),
                             w=[bkey])
                for tc in range(4):
                    _, bank, bkey = banks[tc]
                    p.op("scalar", lambda e, o=gs[:, tc * 512:(tc + 1) * 512],
                         i=bank[:, :]: e.activation(out=o, in_=i,
                                                    func=ACT.Silu),
                         r=[bkey, "gs"], w=["gs"])
                for k in range(NFT):
                    wap, wkey = uws.get(m * NFT + k)
                    for tc in range(4):
                        _, bank, bkey = banks[tc]
                        p.op("tensor", lambda e, o=bank[:, :], w=wap,
                             rr=x3[k][:, tc * 512:(tc + 1) * 512],
                             s=(k == 0), t=(k == 7):
                             e.matmul(o, w, rr, start=s, stop=t),
                             r=[wkey, ("x3", k), "gs"] + ([bkey] if k else []),
                             w=[bkey])
                for tc in range(4):
                    _, bank, bkey = banks[tc]
                    p.op("scalar", lambda e, o=usb[:, tc * 512:(tc + 1) * 512],
                         i=bank[:, :]: e.copy(out=o, in_=i),
                         r=[bkey, "usb"], w=["usb"])
                p.op("vector", lambda e, o=h1[m], a=gs, b=usb:
                     e.tensor_mul(o, a, b), r=["gs", "usb", ("h1", m)],
                     w=[("h1", m)])
                if routed:
                    p.op("vector", lambda e, o=h1[m], a=h1[m], b=wrep:
                         e.tensor_mul(o, a, b),
                         r=[("h1", m), "wrep"], w=[("h1", m)])
                for tc in range(4):
                    pp.free(banks[tc][0])
            daps = [dT[hk * 128:(hk + 1) * 128, m2 * 128:(m2 + 1) * 128]
                    for m2 in range(NFT) for hk in range(NI)]
            dws = self.get_stream("moed", daps)
            for m2 in range(NFT):
                banks = []
                for tc in range(4):
                    bi, bank, bkey = pp.get()
                    pp.hold(bi)
                    banks.append((bi, bank, bkey))
                for hk in range(NI):
                    wap, wkey = dws.get(m2 * NI + hk)
                    for tc in range(4):
                        _, bank, bkey = banks[tc]
                        p.op("tensor", lambda e, o=bank[:, :], w=wap,
                             rr=h1[hk][:, tc * 512:(tc + 1) * 512],
                             s=(hk == 0), t=(hk == NI - 1):
                             e.matmul(o, w, rr, start=s, stop=t),
                             r=[wkey, ("h1", hk)] + ([bkey] if hk else []),
                             w=[bkey])
                for tc in range(4):
                    bi, bank, bkey = banks[tc]
                    p.op("vector", lambda e,
                         o=acc[m2][:, tc * 512:(tc + 1) * 512],
                         a=acc[m2][:, tc * 512:(tc + 1) * 512], b=bank[:, :]:
                         e.tensor_add(o, a, b),
                         r=[("acc", m2), bkey], w=[("acc", m2)])
                    pp.free(bi)
        p.fence()
        ar.release()
        self.drop_streams("moeg", "moeu", "moed")
        res = [ar.alloc(T, F32) for _ in range(NFT)]
        for f in range(NFT):
            p.dma("gpsimd", res[f],
                  self.stream3.ap()[f * 128:(f + 1) * 128, :],
                  ch=("x", f % 4), r=[("str3", f, c) for c in range(4)],
                  w=[("res", f)])
            p.op("vector", lambda e, o=res[f], a=res[f], b=acc[f]:
                 e.tensor_add(o, a, b), r=[("res", f), ("acc", f)],
                 w=[("res", f)])
        o4st = [ar.alloc(512, F32) for _ in range(2)]
        self.final_regions = []

        def dst4(f, c, nrm, key):
            i2 = (f * 4 + c) % 2
            st = o4st[i2]
            p.op("scalar", lambda e, o=st, i=nrm: e.copy(out=o, in_=i),
                 r=[key, ("o4", i2)], w=[("o4", i2)])
            p.dma("gpsimd",
                  outT.ap()[f * 128:(f + 1) * 128, c * 512:(c + 1) * 512],
                  st, ch=("s1", i2), r=[("o4", i2)],
                  w=[("o4", i2), ("out", f, c)])
            self.final_regions.append(("out", f, c))

        self.layernorm(lambda f: (res[f], ("res", f)), dst4)
        p.fence()
        ar.release()

    def build_all(self):
        self.setup_consts()
        self.build_l0()
        self.build_ffn()
        for nm, rows, cols, dt in (("stream2", DM, T, F32),
                                   ("w1q_sp", DM, HID, BF16),
                                   ("w2q_sp", HID, DM, BF16)):
            if nm in self.dump:
                self.ar.mark()
                self.dump_dram(nm, self.intern[nm], rows, cols, dt)
                self.ar.release()
        self.build_attn()
        for nm, rows, cols, dt in (("stream3", DM, T, F32),
                                   ("kqv_sp", 1536, S_FULL, BF16),
                                   ("attn_in", 512, S_FULL, BF16)):
            if nm in self.dump:
                self.ar.mark()
                self.dump_dram(nm, self.intern[nm], rows, cols, dt)
                self.ar.release()
        self.build_moe()
        self.p.op("gpsimd", lambda e: e.engine_nop(),
                  r=list(self.final_regions))


def _t(a):
    return np.ascontiguousarray(np.asarray(a).T).astype(np.float32)


def stage_l0_inputs(inputs):
    x = np.asarray(inputs["L0_attn_x"])
    per_core = []
    for c in range(8):
        b, h = c // 2, c % 2
        xs = x[b, :2048] if h == 0 else x[b, 2048:][::-1]
        m = {"xT": _t(xs)}
        dmap = {"s": "f" if h == 0 else "b", "r": "b" if h == 0 else "f"}
        for d, t in dmap.items():
            m["winT_" + d] = _t(inputs[f"L0_attn_w_in_{t}"])
            m["convw_" + d] = np.ascontiguousarray(
                np.asarray(inputs[f"L0_attn_conv_w_{t}"]).reshape(16, 128, 4)
                .transpose(1, 0, 2).reshape(128, 64)).astype(np.float32)
            m["convb_" + d] = np.ascontiguousarray(
                np.asarray(inputs[f"L0_attn_conv_b_{t}"]).reshape(16, 128).T
            ).astype(np.float32)
            m["xprojT_" + d] = _t(inputs[f"L0_attn_x_proj_{t}"])
            m["dtwT_" + d] = _t(inputs[f"L0_attn_dt_w_{t}"])
            m["dtb_" + d] = np.ascontiguousarray(
                np.asarray(inputs[f"L0_attn_dt_b_{t}"]).reshape(16, 128).T
            ).astype(np.float32)
            m["Alog_" + d] = np.ascontiguousarray(
                np.asarray(inputs[f"L0_attn_A_log_{t}"]).reshape(16, 128, 16)
                .transpose(1, 0, 2).reshape(128, 256)).astype(np.float32)
            m["Dsk_" + d] = np.ascontiguousarray(
                np.asarray(inputs[f"L0_attn_D_{t}"]).reshape(16, 128).T
            ).astype(np.float32)
            m["woutT_" + d] = _t(inputs[f"L0_attn_w_out_{t}"])
        pT = _t(inputs["L0_attn_proj_w"])
        m["projwT_s"] = np.ascontiguousarray(
            pT[:1024] if h == 0 else pT[1024:])
        m["projwT_r"] = np.ascontiguousarray(
            pT[1024:] if h == 0 else pT[:1024])
        m["projb"] = np.ascontiguousarray(
            np.asarray(inputs["L0_attn_proj_b"]).reshape(8, 128).T
        ).astype(np.float32)
        m["ln_g"] = np.ascontiguousarray(
            np.asarray(inputs["L0_attn_ln_g"]).reshape(8, 128).T
        ).astype(np.float32)
        m["ln_b"] = np.ascontiguousarray(
            np.asarray(inputs["L0_attn_ln_b"]).reshape(8, 128).T
        ).astype(np.float32)
        m["nbr0"] = np.full((128, 1), 1.0 if h == 1 else 0.0, np.float32)
        m["nbr1"] = np.full((128, 1), 0.0 if h == 1 else 1.0, np.float32)
        per_core.append(m)
    return per_core

def stage_all_inputs(inputs):
    per_core = stage_l0_inputs(inputs)
    wk = _t(inputs["L1_attn_W_kqv"])
    sgT = _t(inputs["L1_ffn_sg"])
    suT = _t(inputs["L1_ffn_su"])
    sdT = _t(inputs["L1_ffn_sd"])
    rgT = np.ascontiguousarray(
        np.asarray(inputs["L1_ffn_rg"]).transpose(0, 2, 1)).astype(np.float32)
    ruT = np.ascontiguousarray(
        np.asarray(inputs["L1_ffn_ru"]).transpose(0, 2, 1)).astype(np.float32)
    rdT = np.ascontiguousarray(
        np.asarray(inputs["L1_ffn_rd"]).transpose(0, 2, 1)).astype(np.float32)
    wrT = np.zeros((1024, 8), np.float32)
    wrT[:, 0:7] = _t(inputs["L1_ffn_wr"])
    rb = np.zeros((8, 1), np.float32)
    rb[0:7, 0] = np.asarray(inputs["L1_ffn_rb"]).astype(np.float32)
    slopes_all = (1.0 / ((2 ** 8) ** (1.0 / 16)) **
                  (np.arange(1, 17))).astype(np.float32)
    for c in range(8):
        b, h = c // 2, c % 2
        m = per_core[c]
        m["w1T"] = _t(inputs["L0_ffn_w1"])
        m["w2T"] = _t(inputs["L0_ffn_w2"])
        hs = slice(h * 512, (h + 1) * 512)
        m["wkT"] = np.ascontiguousarray(wk[:, 0:1024][:, hs])
        m["wqT"] = np.ascontiguousarray(wk[:, 1024:2048][:, hs])
        m["wvT"] = np.ascontiguousarray(wk[:, 2048:3072][:, hs])
        m["slopes"] = np.tile(slopes_all[h * 8:(h + 1) * 8].reshape(1, 8),
                              (128, 1)).copy()
        m["maskA"] = np.full((128, 1), 1.0 if h == 0 else 0.0, np.float32)
        m["maskB"] = np.full((128, 1), 0.0 if h == 0 else 1.0, np.float32)
        m["wrT"] = wrT
        m["rb"] = rb
        m["sgT"] = sgT
        m["suT"] = suT
        m["sdT"] = sdT
        m["rgT"] = rgT
        m["ruT"] = ruT
        m["rdT"] = rdT
    return per_core


def kernel(**inputs):
    nc = bass.Bass("TRN2", target_bir_lowering=False)
    m = MK(nc)
    m.build_all()
    m.p.emit()
    in_maps = stage_all_inputs(inputs)
    res = run_bass_kernel_spmd(nc, in_maps, core_ids=list(range(8)))
    out = np.empty((4, 4096, 1024), np.float32)
    for c in range(8):
        b, h = c // 2, c % 2
        sl = np.asarray(res.results[c]["outT"]).T
        if h == 1:
            sl = sl[::-1]
        out[b, h * 2048:(h + 1) * 2048] = sl
    return out
